# version 07
# speedup vs baseline: 2.6624x; 1.0978x over previous
import contextlib
import numpy as np
import ml_dtypes
import concourse.bacc as bacc
import concourse.bass as bass
import concourse.mybir as mybir
from concourse.bass_utils import run_bass_kernel_spmd
from concourse.library_config import attn as LIB_ATTN, mlp as LIB_MLP, sparse_gather as LIB_SG

F32 = mybir.dt.float32
BF16 = mybir.dt.bfloat16
I16 = mybir.dt.int16
I32 = mybir.dt.int32
U32 = mybir.dt.uint32
AF = mybir.ActivationFunctionType
ALU = mybir.AluOpType
AX = mybir.AxisListType
DRM = mybir.MatmulPerfMode.DoubleRow

D, T, NH, HD, E = 2048, 1024, 16, 128, 7
CAPP = 1280
NSLOT = 8 * CAPP
FF, F2 = 1024, 2048
EPS = 1e-5
CEXP = 10.0



CHAIN_SKIP = {"wait_ge", "then_inc", "dma_start", "dma_start_transpose", "dma_gather",
              "dma_scatter_add", "collective_compute", "indirect_dma_start",
              "load_library", "sem_inc", "nop", "engine_nop", "alloc_register",
              "reg_load", "free_register", "drain", "memset", "memzero"}


class _Chained:

    def __init__(self, P, eng, name):
        self._P, self._eng, self._name = P, eng, name

    def __getattr__(self, k):
        v = getattr(self._eng, k)
        if k in CHAIN_SKIP or not callable(v):
            return v
        P, eng, name = self._P, self._eng, self._name

        def wrap(*a, **kw):
            r = v(*a, **kw)
            n = P.inc(name, r)
            eng.wait_ge(P.sem(name), n)
            return r
        return wrap


class Prog:
    def __init__(self, nc, st):
        self.nc, self.st, self.sems, self.cnt = nc, st, {}, {}
        self.marks = {}

    def sem(self, name):
        if name not in self.sems:
            self.sems[name] = self.st.enter_context(self.nc.semaphore(name))
            self.cnt[name] = 0
        return self.sems[name]

    def inc(self, name, ins, k=1):
        ins.then_inc(self.sem(name), k)
        self.cnt[name] += k
        return self.cnt[name]

    def tot(self, name):
        self.sem(name)
        return self.cnt[name]

    def wait(self, eng, name, val=None):
        eng.wait_ge(self.sem(name), self.tot(name) if val is None else val)

    def chain(self, eng, name):
        return _Chained(self, eng, name)


def build(stage=5):
    nc = bacc.Bacc("TRN2")
    st = contextlib.ExitStack()
    P = Prog(nc, st)
    sem, inc, tot, wait = P.sem, P.inc, P.tot, P.wait

    def din(name, shape, dt=F32):
        return nc.dram_tensor(name, shape, dt, kind="ExternalInput")

    x_d = din("x", [T, D])
    wqkvT_d = din("wqkvT", [D, 3 * D])
    woutT_d = din("woutT", [D, D])
    gatew_d = din("gatew", [D, E])
    ebias_d = din("ebias", [E, 1])
    w1s_d = din("w1s", [D, F2])
    w2s_d = din("w2s", [FF, D])
    w1r_d = din("w1r", [D, F2])
    w2r_d = din("w2r", [FF, D])
    c_d = din("cc", [2, D])
    wmod_d = din("wmod", [D, 1536])
    identb_d = din("identb", [128, 128], BF16)
    identf_d = din("identf", [128, 128], F32)
    rotT_d = din("rotT", [128, 128], BF16)
    cosq_d = din("cosq", [128, T], BF16)
    sinq_d = din("sinq", [128, T], BF16)
    cosk_d = din("cosk", [128, T], BF16)
    sink_d = din("sink", [128, T], BF16)
    u16s_d = din("u16s", [16, 16], BF16)
    gmapp1_d = din("gmapp1", [16, 512], F32)
    irev_d = din("irev", [128, E], F32)
    bsel_d = din("bsel", [6, 2], F32)
    rowm_d = din("rowm", [6, 1], F32)
    eoh_d = din("eoh", [16, E], F32)
    onesr_b_d = din("onesr_b", [1, 128], BF16)
    onesr_f_d = din("onesr_f", [1, 128], F32)
    onescol_b_d = din("onescol_b", [128, 1], BF16)
    ones2_b_d = din("ones2_b", [128, 2], BF16)
    bisinit_d = din("bisinit", [16, 2], I32)
    ones16_d = din("ones16", [16, 16], F32)

    out_d = nc.dram_tensor("out", [T, D], F32, kind="ExternalOutput")

    modin = nc.dram_tensor("modin", [2, 1536], F32)
    modall = nc.dram_tensor("modall", [16, 1536], F32)
    kin = nc.dram_tensor("kin", [D, T], BF16)
    vin = nc.dram_tensor("vin", [T, D], BF16)
    knin = nc.dram_tensor("knin", [1, NH], F32)
    kall = nc.dram_tensor("kall", [4 * D, T], BF16)
    vall = nc.dram_tensor("vall", [4 * T, D], BF16)
    knall = nc.dram_tensor("knall", [4, NH], F32)
    h2in = nc.dram_tensor("h2in", [T, D], BF16)
    affin = nc.dram_tensor("affin", [112, 64], F32)
    h2all = nc.dram_tensor("h2all", [8 * T, D], BF16)
    affall = nc.dram_tensor("affall", [8 * 112, 64], F32)
    routin = nc.dram_tensor("routin", [CAPP, D], BF16)
    routall = nc.dram_tensor("routall", [NSLOT, D], BF16)
    x2spill = nc.dram_tensor("x2spill", [T, D], F32)
    scr128 = nc.dram_tensor("scr128", [128, 64], F32)
    olscr = nc.dram_tensor("olscr", [16, 128], I16)
    thin = nc.dram_tensor("thin", [1, 1], F32)
    thall = nc.dram_tensor("thall", [8, 1], F32)

    dbg = {}

    def dbgadd(name, shape, dt):
        dbg[name] = nc.dram_tensor("dbg_" + name, shape, dt, kind="ExternalOutput")

    if stage == 1:
        dbgadd("h1t", [128, 16, T], BF16)
        dbgadd("vbo", [6, D], F32)
        dbgadd("mparm", [128, 64], F32)
    if stage == 2:
        dbgadd("qf", [128, NH, T], BF16)
        dbgadd("kf", [128, NH, T], BF16)
        dbgadd("vt", [128, 8, D], BF16)
    if stage == 3:
        dbgadd("x2", [T, D], F32)
    if stage == 4:
        dbgadd("aff", [E, T], F32)
        dbgadd("theta", [1, 16], F32)
        dbgadd("sidx", [16, 128], I16)
        dbgadd("wown", [128, 16], F32)
        dbgadd("h2t", [128, 16, T], BF16)

    def sbt(name, shape, dt, stk=None):
        return (stk or st).enter_context(nc.sbuf_tensor(name, shape, dt))

    identb = sbt("identb_s", [128, 128], BF16)
    identf = sbt("identf_s", [128, 128], F32)
    rotT = sbt("rotT_s", [128, 128], BF16)
    u16s = sbt("u16s_s", [16, 16], BF16)
    gmapp1 = sbt("gmapp1_s", [16, 512], F32)
    irev = sbt("irev_s", [128, E], F32)
    bsel = sbt("bsel_s", [6, 2], F32)
    rowm = sbt("rowm_s", [6, 1], F32)
    eoh = sbt("eoh_s", [16, E], F32)
    onesr_b = sbt("onesr_b_s", [1, 128], BF16)
    onesr_f = sbt("onesr_f_s", [1, 128], F32)
    onescol_b = sbt("onescol_b_s", [128, 1], BF16)
    ones2_b = sbt("ones2_b_s", [128, 2], BF16)
    ebias_sb = sbt("ebias_sb", [E, 1], F32)
    mparm = sbt("mparm", [128, 64], F32)
    gag = sbt("gag", [128, 2 * D], BF16)
    cbb = sbt("cbb", [128, D], BF16)
    sbb = sbt("sbb", [128, D], BF16)
    mbias = sbt("mbias", [128, NH], F32)
    thtm = sbt("thtm", [128, E], F32)
    wownb = sbt("wownb", [128, 16], F32)
    stats = sbt("stats", [128, 16], F32)
    nf_sb = sbt("nf_sb", [1, 1], U32)
    sq_rt1 = sbt("sq_rt1", [128, E], F32)
    sq_rt2 = sbt("sq_rt2", [128, E], F32)
    sq_rt3 = sbt("sq_rt3", [128, E], F32)
    manyi = sbt("manyi", [128, 1], I16)
    kn = sbt("kn", [1, 64], F32)

    pbank = [st.enter_context(nc.psum_tensor(f"pb{i}", [128, 512], F32)) for i in range(8)]

    def pbank_bf(i):
        return pbank[i].ap().bitcast(BF16)

    with nc.Block() as blk:
        @blk.sync
        def _(s):
            for t_, d_ in ((identb, identb_d), (identf, identf_d), (rotT, rotT_d),
                           (u16s, u16s_d), (gmapp1, gmapp1_d), (irev, irev_d),
                           (bsel, bsel_d), (rowm, rowm_d), (eoh, eoh_d), (onesr_b, onesr_b_d),
                           (onesr_f, onesr_f_d), (onescol_b, onescol_b_d),
                           (ones2_b, ones2_b_d), (ebias_sb, ebias_d)):
                inc("dma_s", s.dma_start(t_[:], d_[:]), 16)

    with contextlib.ExitStack() as pst:
        vb = sbt("vb", [6, D], F32, pst)
        csb = sbt("csb", [2, D], F32, pst)
        csil = sbt("csil", [2, D], F32, pst)
        ct = sbt("ct", [128, 32], F32, pst)
        wmv = sbt("wmv", [128, 16, 1536], F32, pst)
        modv = sbt("modv", [2, 1536], F32, pst)
        vrow = sbt("vrow", [1, D], F32, pst)
        garow = sbt("garow", [1, D], BF16, pst)
        gbrow = sbt("gbrow", [1, D], BF16, pst)
        cbrow = sbt("cbrow", [1, D], BF16, pst)
        sbrow = sbt("sbrow", [1, D], BF16, pst)
        mv = sbt("mv", [6, 2 * D], F32, pst)

        with nc.Block() as blk:
            @blk.sync
            def _(s):
                inc("dma_c", s.dma_start(csb[:], c_d[:]), 16)

            @blk.gpsimd
            def _(g):
                inc("dma_w", g.dma_start(
                    wmv.ap(),
                    wmod_d.ap().rearrange("(dc p) j -> p dc j", p=128)), 16)

            @blk.scalar
            def _(sc):
                sc = P.chain(sc, "chs")
                wait(sc, "dma_c")
                sc.activation(csil[:], csb[:], AF.Silu)
            m_sil = tot("chs")

            @blk.tensor
            def _(tn):
                wait(tn, "dma_s")
                tn.wait_ge(sem("chs"), m_sil)
                for dc in range(16):
                    ins = tn.transpose(pbank[5][:, 2 * dc:2 * dc + 2],
                                       csil[:, dc * 128:(dc + 1) * 128], identf[0:2, 0:2])
                inc("mm_s", ins)

            @blk.vector
            def _(v):
                v = P.chain(v, "chv")
                wait(v, "mm_s")
                v.tensor_copy(ct[:], pbank[5][:, 0:32])
            m_ct = tot("chv")
            mm0 = tot("mm_s")

            @blk.tensor
            def _(tn):
                tn.wait_ge(sem("chv"), m_ct)
                wait(tn, "dma_w")
                for jg in range(3):
                    if jg >= 2:
                        tn.wait_ge(sem("chv"), m_ct + jg - 1)
                    for dc in range(16):
                        ins = tn.matmul(pbank[jg % 2][0:2, :], ct[:, 2 * dc:2 * dc + 2],
                                        wmv[:, dc, jg * 512:(jg + 1) * 512],
                                        start=(dc == 0), stop=(dc == 15))
                    inc("mm_s", ins)

            @blk.vector
            def _(v):
                v = P.chain(v, "chv")
                for jg in range(3):
                    wait(v, "mm_s", mm0 + jg + 1)
                    v.tensor_copy(modv[:, jg * 512:(jg + 1) * 512], pbank[jg % 2][0:2, :])
            m_modv = tot("chv")

            @blk.gpsimd
            def _(g):
                g.wait_ge(sem("chv"), m_modv)
                inc("dma_g", g.dma_start(modin[:], modv[:]), 16)
                wait(g, "dma_g")
                inc("cc_s", g.collective_compute(
                    "AllGather", ALU.bypass, replica_groups=[list(range(8))],
                    ins=[modin.ap().opt()], outs=[modall.ap().opt()]))

        with nc.Block() as blk:
            @blk.gpsimd
            def _(g):
                wait(g, "cc_s")
                for v_ in range(6):
                    for b_ in range(2):
                        J0, done = 2048 * v_, 0
                        while done < 2048:
                            rk, c0 = J0 // 1536, J0 % 1536
                            n = min(1536 - c0, 2048 - done)
                            inc("dma_g", g.dma_start(
                                mv[v_:v_ + 1, b_ * D + done:b_ * D + done + n],
                                modall[2 * rk + b_:2 * rk + b_ + 1, c0:c0 + n]), 16)
                            J0 += n
                            done += n

            @blk.vector
            def _(v):
                v = P.chain(v, "chv")
                wait(v, "dma_g")
                wait(v, "dma_s")
                v.tensor_scalar(vb[:], mv[:, 0:D], bsel[:, 0:1], None, op0=ALU.mult)
                v.scalar_tensor_tensor(vb[:], mv[:, D:2 * D], bsel[:, 1:2], vb[:],
                                       op0=ALU.mult, op1=ALU.add)
                sgn = mv[:, 0:D]
                ab = mv[:, D:2 * D]
                v.tensor_single_scalar(sgn, vb[:], 0.0, op=ALU.is_ge)
                v.tensor_scalar(sgn, sgn, 2.0, -1.0, op0=ALU.mult, op1=ALU.add)
                v.tensor_scalar(ab, vb[:], -1.0, None, op0=ALU.mult)
                v.tensor_max(ab, ab, vb[:])
                v.tensor_scalar_max(ab, ab, 0.25)
                v.tensor_mul(sgn, sgn, ab)
                v.tensor_sub(sgn, sgn, vb[:])
                v.scalar_tensor_tensor(vb[:], sgn, rowm[:, 0:1], vb[:],
                                       op0=ALU.mult, op1=ALU.add)
            m_vb = tot("chv")

            @blk.gpsimd
            def _(g):
                g.wait_ge(sem("chv"), m_vb)
                inc("dma_gr", g.dma_start(garow[:], vb[2:3, :]), 16)
                inc("dma_gr", g.dma_start(gbrow[:], vb[5:6, :]), 16)
                inc("dma_gr", g.dma_start(cbrow[:], vb[4:5, :]), 16)
                inc("dma_gr", g.dma_start(sbrow[:], vb[3:4, :]), 16)
                for i, row in enumerate((1, 0, 4, 3)):
                    if i > 0:
                        wait(g, "mm_vr", i)
                    inc("dma_vr", g.dma_start(vrow[:], vb[row:row + 1, :]), 16)

            @blk.tensor
            def _(tn):
                for i in range(4):
                    tn.wait_ge(sem("dma_vr"), 16 * (i + 1))
                    for dc in range(16):
                        ins = tn.transpose(pbank[2][:, i * 16 + dc:i * 16 + dc + 1],
                                           vrow[:, dc * 128:(dc + 1) * 128],
                                           identf[0:1, 0:1])
                    inc("mm_vr", ins)
                wait(tn, "dma_gr")
                for kgrp in range(16):
                    i, jg = divmod(kgrp, 4)
                    if kgrp >= 2:
                        tn.wait_ge(sem("chv"), m_vb + 3 + kgrp - 1)
                    row = (garow, gbrow, cbrow, sbrow)[i]
                    inc("mm_gag", tn.matmul(pbank[3 + kgrp % 2][:],
                                            onesr_b[:],
                                            row[:, jg * 512:(jg + 1) * 512],
                                            start=True, stop=True))

            @blk.vector
            def _(v):
                v = P.chain(v, "chv")
                wait(v, "mm_vr")
                v.tensor_copy(mparm[:], pbank[2][:, 0:64])
                v.tensor_scalar_add(mparm[:, 0:16], mparm[:, 0:16], 1.0)
                v.tensor_scalar_add(mparm[:, 32:48], mparm[:, 32:48], 1.0)
                assert tot("chv") == m_vb + 3
                for kgrp in range(16):
                    i, jg = divmod(kgrp, 4)
                    wait(v, "mm_gag", kgrp + 1)
                    dst = (gag[:, jg * 512:(jg + 1) * 512],
                           gag[:, D + jg * 512:D + (jg + 1) * 512],
                           cbb[:, jg * 512:(jg + 1) * 512],
                           sbb[:, jg * 512:(jg + 1) * 512])[i]
                    v.tensor_copy(dst, pbank[3 + kgrp % 2][:])
                v.tensor_scalar_add(cbb[:], cbb[:], 1.0)

        if stage == 1:
            with nc.Block() as blk:
                @blk.sync
                def _(s):
                    s.wait_ge(sem("chv"), tot("chv"))
                    inc("dma_o", s.dma_start(dbg["vbo"].ap(), vb.ap()), 16)
                    wait(s, "dma_o")

    bigA = sbt("bigA", [128, NH, T], BF16)
    bigB = sbt("bigB", [128, NH, T], BF16)
    bigC = sbt("bigC", [128, NH, T], BF16)
    bigD = sbt("bigD", [128, NH, T], BF16)

    h1t, q_fm, k_fm = bigA, bigB, bigC
    v_tm = bigD.ap().rearrange("p a t -> p (a t)").rearrange("p (tc o) -> p tc o", tc=8)

    with contextlib.ExitStack() as pst:
        xt = [sbt(f"xt{i}", [128, D], F32, pst) for i in range(2)]
        xnt = [sbt(f"xnt{i}", [128, D], BF16, pst) for i in range(2)]
        sqt = sbt("sqt", [128, D], F32, pst)
        veL0 = tot("chv")
        scL0 = tot("chs")

        def ln_xn(tc):
            return veL0 + 26 * tc + 10

        def ln_mod(tc):
            return veL0 + 26 * (tc + 1)

        with nc.Block() as blk:
            @blk.sync
            def _(s):
                for tc in range(8):
                    if tc >= 2:
                        s.wait_ge(sem("chv"), ln_xn(tc - 2))
                    inc(f"dma_x{tc % 2}", s.dma_start(
                        xt[tc % 2][:], x_d[tc * 128:(tc + 1) * 128, :]), 16)

            @blk.vector
            def _(v):
                v = P.chain(v, "chv")
                for tc in range(8):
                    v.wait_ge(sem(f"dma_x{tc % 2}"), 16 * (tc // 2 + 1))
                    x_ = xt[tc % 2]
                    mean, msq, var, rin = (stats[:, 4:5], stats[:, 5:6],
                                           stats[:, 6:7], stats[:, 7:8])
                    v.reduce_sum(mean, x_[:], axis=AX.X)
                    v.tensor_scalar_mul(mean, mean, 1.0 / D)
                    v.tensor_mul(sqt[:], x_[:], x_[:])
                    v.reduce_sum(msq, sqt[:], axis=AX.X)
                    v.tensor_scalar_mul(msq, msq, 1.0 / D)
                    v.tensor_mul(var, mean, mean)
                    v.tensor_sub(var, msq, var)
                    v.tensor_scalar_add(var, var, EPS)
                    v.reciprocal(rin, var)
                    v.wait_ge(sem("chs"), scL0 + tc + 1)
                    v.scalar_tensor_tensor(xnt[tc % 2][:], x_[:], mean,
                                           stats[:, 8:9].to_broadcast([128, D]),
                                           op0=ALU.subtract, op1=ALU.mult)
                    assert tot("chv") == veL0 + 26 * tc + 10
                    v.wait_ge(sem("mm_ln"), tc + 1)
                    for dc in range(16):
                        bk = (tc % 2) * 2 + dc // 8
                        v.scalar_tensor_tensor(
                            h1t[:, dc, tc * 128:(tc + 1) * 128],
                            pbank_bf(bk)[:, (dc % 8) * 128:(dc % 8 + 1) * 128],
                            mparm[:, dc:dc + 1],
                            mparm[:, 16 + dc:17 + dc].to_broadcast([128, 128]),
                            op0=ALU.mult, op1=ALU.add)
                    assert tot("chv") == ln_mod(tc)

            @blk.scalar
            def _(sc):
                sc = P.chain(sc, "chs")
                for tc in range(8):
                    sc.wait_ge(sem("chv"), veL0 + 26 * tc + 9)
                    sc.activation(stats[:, 8:9], stats[:, 7:8], AF.Sqrt)

            @blk.tensor
            def _(tn):
                for tc in range(8):
                    tn.wait_ge(sem("chv"), ln_xn(tc))
                    if tc >= 2:
                        tn.wait_ge(sem("chv"), ln_mod(tc - 2))
                    for dc in range(16):
                        bk = (tc % 2) * 2 + dc // 8
                        ins = tn.transpose(
                            pbank_bf(bk)[:, (dc % 8) * 128:(dc % 8 + 1) * 128],
                            xnt[tc % 2][:, dc * 128:(dc + 1) * 128], identb[:])
                    inc("mm_ln", ins)

    m_h1 = tot("chv")

    if stage == 1:
        with nc.Block() as blk:
            @blk.sync
            def _(s):
                s.wait_ge(sem("chv"), m_h1)
                inc("dma_o", s.dma_start(dbg["h1t"].ap(), h1t.ap()), 16)
                inc("dma_o", s.dma_start(dbg["mparm"].ap(), mparm.ap()), 16)
                wait(s, "dma_o")
        st.close()
        return nc, dbg

    with contextlib.ExitStack() as pst:
        wstr = sbt("wstr", [128, 2, 16, 512], BF16, pst)
        cosq = sbt("cosq_s", [128, T], BF16, pst)
        sinq = sbt("sinq_s", [128, T], BF16, pst)
        cosk = sbt("cosk_s", [128, T], BF16, pst)
        sink = sbt("sink_s", [128, T], BF16, pst)
        rot_sb = sbt("rot_sb", [128, T], BF16, pst)
        tmp1 = sbt("tmp1", [128, T], BF16, pst)
        tmp2 = sbt("tmp2", [128, T], BF16, pst)

        scB = tot("chs")
        h1_ready = m_h1

        with nc.Block() as blk:
            @blk.sync
            def _(s):
                for t_, d_ in ((cosq, cosq_d), (sinq, sinq_d), (cosk, cosk_d), (sink, sink_d)):
                    inc("dma_rt", s.dma_start(t_[:], d_[:]), 16)

            @blk.gpsimd
            def _(g):
                for og in range(12):
                    if og >= 2:
                        wait(g, "mm_qkv", 8 * (og - 1))
                    inc(f"dma_q{og % 2}", g.dma_start(
                        wstr[:, og % 2, :, :],
                        wqkvT_d.ap().rearrange("(dc p) o -> p dc o", p=128)
                        [:, :, og * 512:(og + 1) * 512]), 16)

            @blk.tensor
            def _(tn):
                tn.wait_ge(sem("chv"), h1_ready)
                for og in range(12):
                    tn.wait_ge(sem(f"dma_q{og % 2}"), 16 * (og // 2 + 1))
                    for k in range(8):
                        if og >= 1:
                            tn.wait_ge(sem("chs"), scB + 8 * (og - 1) + k + 1)
                        if og < 8:
                            oc, th = divmod(k, 2)
                            for dc in range(16):
                                ins = tn.matmul(
                                    pbank[k][:],
                                    wstr[:, og % 2, dc, oc * 128:(oc + 1) * 128],
                                    h1t[:, dc, th * 512:(th + 1) * 512],
                                    start=(dc == 0), stop=(dc == 15))
                        else:
                            tc = k
                            for dc in range(16):
                                ins = tn.matmul(
                                    pbank[k][:],
                                    h1t[:, dc, tc * 128:(tc + 1) * 128],
                                    wstr[:, og % 2, dc, :],
                                    start=(dc == 0), stop=(dc == 15))
                        inc("mm_qkv", ins)

            @blk.scalar
            def _(sc):
                sc = P.chain(sc, "chs")
                for og in range(12):
                    for k in range(8):
                        sc.wait_ge(sem("mm_qkv"), 8 * og + k + 1)
                        if og < 8:
                            oc, th = divmod(k, 2)
                            head = (og % 4) * 4 + oc
                            dest = (q_fm if og < 4 else k_fm)[:, head, th * 512:(th + 1) * 512]
                        else:
                            dest = v_tm[:, k, (og - 8) * 512:(og - 7) * 512]
                        sc.copy(dest, pbank[k][:])
            assert tot("chs") == scB + 96

        with nc.Block() as blk:
            veR = tot("chv")
            scR = tot("chs")

            @blk.tensor
            def _(tn):
                wait(tn, "dma_rt")
                for hh in range(32):
                    tgt = q_fm if hh < 16 else k_fm
                    h = hh % 16
                    for th in range(2):
                        bk = (hh % 2) * 2 + th
                        if hh >= 2:
                            tn.wait_ge(sem("chs"), scR + 2 * (hh - 1))
                        ins = tn.matmul(pbank[bk][:], rotT[:],
                                        tgt[:, h, th * 512:(th + 1) * 512],
                                        start=True, stop=True)
                    inc("mm_rope", ins)

            @blk.scalar
            def _(sc):
                sc = P.chain(sc, "chs")
                for hh in range(32):
                    sc.wait_ge(sem("mm_rope"), hh + 1)
                    if hh >= 1:
                        sc.wait_ge(sem("chv"), veR + 3 * hh)
                    for th in range(2):
                        bk = (hh % 2) * 2 + th
                        sc.copy(rot_sb[:, th * 512:(th + 1) * 512], pbank[bk][:])

            @blk.vector
            def _(v):
                v = P.chain(v, "chv")
                for hh in range(32):
                    tgt = q_fm if hh < 16 else k_fm
                    co = cosq if hh < 16 else cosk
                    si = sinq if hh < 16 else sink
                    h = hh % 16
                    v.wait_ge(sem("chs"), scR + 2 * (hh + 1))
                    v.tensor_mul(tmp1[:], tgt[:, h, :], co[:])
                    v.tensor_mul(tmp2[:], rot_sb[:], si[:])
                    v.tensor_add(tgt[:, h, :], tmp1[:], tmp2[:])
                    assert tot("chv") == veR + 3 * (hh + 1)

        m_rope_v = tot("chv")
        m_qkv_sc = tot("chs")

    if stage == 2:
        with nc.Block() as blk:
            @blk.sync
            def _(s):
                s.wait_ge(sem("chv"), m_rope_v)
                s.wait_ge(sem("chs"), m_qkv_sc)
                inc("dma_o", s.dma_start(dbg["qf"].ap(), q_fm.ap()), 16)
                inc("dma_o", s.dma_start(dbg["kf"].ap(), k_fm.ap()), 16)
                inc("dma_o", s.dma_start(dbg["vt"].ap(), bigD.ap()), 16)
                wait(s, "dma_o")
        st.close()
        return nc, dbg

    with contextlib.ExitStack() as pst:
        sq_at = sbt("sq_at", [128, T], BF16, pst)
        P_sb2 = sbt("P_sb2", [128, 4, 512], BF16, pst)
        lin_sb = sbt("lin_sb", [1, 512], F32, pst)
        linb_sb = sbt("linb_sb", [128, 512], F32, pst)
        kn4 = sbt("kn4", [1, 64], F32, pst)

        attn_fm = bigA

        with nc.Block() as blk:
            scN = tot("chs")
            veN = tot("chv")

            @blk.scalar
            def _(sc):
                sc = P.chain(sc, "chs")
                sc.wait_ge(sem("chv"), m_rope_v)
                for hh in range(32):
                    if hh >= 1:
                        sc.wait_ge(sem("mm_nrm"), 2 * hh)
                    src = (k_fm if hh < 16 else q_fm)[:, hh % 16, :]
                    sc.activation(sq_at[:], src, AF.Square)

            @blk.tensor
            def _(tn):
                for hh in range(32):
                    tn.wait_ge(sem("chs"), scN + hh + 1)
                    if hh >= 1:
                        tn.wait_ge(sem("chv"), veN + hh)
                    for th in range(2):
                        ins = tn.matmul(pbank[6 + hh % 2][0:1, :], onescol_b[:],
                                        sq_at[:, th * 512:(th + 1) * 512],
                                        start=(th == 0), stop=(th == 1))
                    inc("mm_nrm", ins, 2)

            @blk.vector
            def _(v):
                v = P.chain(v, "chv")
                for hh in range(32):
                    v.wait_ge(sem("mm_nrm"), 2 * (hh + 1))
                    col = hh if hh < 16 else 16 + (hh - 16)
                    v.reduce_max(kn[0:1, col:col + 1], pbank[6 + hh % 2][0:1, :], axis=AX.X)
            m_nrm = tot("chv")

            @blk.gpsimd
            def _(g):
                g.wait_ge(sem("chv"), m_nrm)
                inc("dma_kv", g.dma_start(knin.ap(), kn[0:1, 0:16]), 16)
                inc("dma_kv", g.dma_start(
                    kin.ap().rearrange("(dc p) t -> p dc t", p=128), k_fm.ap()), 16)
                inc("dma_kv", g.dma_start(
                    vin.ap().rearrange("(tc p) o -> p tc o", p=128), v_tm), 16)
                wait(g, "dma_kv")
                for t_in, t_out in ((kin, kall), (vin, vall), (knin, knall)):
                    inc("cc_kv", g.collective_compute(
                        "AllGather", ALU.bypass,
                        replica_groups=[[0, 1, 2, 3], [4, 5, 6, 7]],
                        ins=[t_in.ap().opt()], outs=[t_out.ap().opt()]))
                wait(g, "cc_kv")
                inc("dma_kn", g.dma_start(kn4[:], knall.ap().rearrange("r h -> (r h)").unsqueeze(0)), 16)

            @blk.vector
            def _(v):
                v = P.chain(v, "chv")
                wait(v, "dma_kn")
                v.reduce_max(kn[0:1, 32:48],
                             kn4.ap().rearrange("a (r h) -> a h r", r=4), axis=AX.X)
                v.tensor_mul(kn[0:1, 48:64], kn[0:1, 32:48], kn[0:1, 16:32])
            m_prod = tot("chv")

            @blk.scalar
            def _(sc):
                sc = P.chain(sc, "chs")
                sc.wait_ge(sem("chv"), m_prod)
                sc.activation(kn[0:1, 48:64], kn[0:1, 48:64], AF.Sqrt)
            m_sqrtM = tot("chs")

            @blk.tensor
            def _(tn):
                tn.wait_ge(sem("chs"), m_sqrtM)
                inc("mm_mb", tn.matmul(pbank[6][:, 0:16], onesr_f[:], kn[0:1, 48:64],
                                       start=True, stop=True))

            @blk.vector
            def _(v):
                v = P.chain(v, "chv")
                wait(v, "mm_mb")
                v.tensor_scalar_mul(mbias[:], pbank[6][:, 0:16], -1.0 / CEXP)
            m_mbias = tot("chv")

        kvb = bigC.ap().rearrange("p a t -> p (a t)")
        lnv = pst.enter_context(nc.sbuf_tensor("lnv", [1, 512], F32))
        negmh = pst.enter_context(nc.sbuf_tensor("negmh", [1, 512], BF16))
        negmhf = pst.enter_context(nc.sbuf_tensor("negmhf", [1, 512], F32))
        negmb = pst.enter_context(nc.sbuf_tensor("negmb", [128, 512], F32))
        with nc.Block() as blk:
            attB = tot("mm_att") if "mm_att" in P.cnt else 0
            scA = tot("chs")
            veA = tot("chv")

            @blk.sync
            def _(s_):
                s_.wait_ge(sem("chv"), m_mbias)
                for h in range(16):
                    if h >= 2:
                        s_.wait_ge(sem("mm_att"), attB + 258 * (h - 1))
                    par = h % 2
                    kdst = kvb[:, par * 8192: par * 8192 + 4096].rearrange("p (r t) -> p r t", r=4)
                    ksrc = kall.ap().rearrange("(r row) t -> row r t", r=4)[h * 128:(h + 1) * 128, :, :]
                    inc(f"dma_K{par}", s_.dma_start(kdst, ksrc), 16)
                    vdst = kvb[:, par * 8192 + 4096: par * 8192 + 8192].rearrange(
                        "p (r j o) -> p r j o", r=4, j=8)
                    for r_ in range(4):
                        vsrc = vall.ap().rearrange("(r j p) o -> p r j o", r=4, j=8)[
                            :, r_:r_ + 1, :, h * 128:(h + 1) * 128]
                        inc(f"dma_V{par}", s_.dma_start(vdst[:, r_:r_ + 1, :, :], vsrc), 16)

            @blk.tensor
            def _(tn):
                for h in range(16):
                    par = h % 2
                    tn.wait_ge(sem(f"dma_K{par}"), 16 * (h // 2 + 1))
                    tn.wait_ge(sem(f"dma_V{par}"), 64 * (h // 2 + 1))
                    kslc = kvb[:, par * 8192: par * 8192 + 4096]
                    vslc = kvb[:, par * 8192 + 4096: par * 8192 + 8192]
                    for qh in range(2):
                        hq = h * 2 + qh
                        base = attB + 129 * hq
                        csc = scA + 65 * hq
                        def emit_S(kt):
                            if kt >= 2:
                                tn.wait_ge(sem("chs"), csc + kt - 1)
                            elif hq >= 1:
                                tn.wait_ge(sem("chs"), scA + 65 * (hq - 1) + 64 + kt)
                            ins = tn.matmul(pbank[kt % 2][:],
                                            kslc[:, kt * 128:(kt + 1) * 128],
                                            q_fm[:, h, qh * 512:(qh + 1) * 512],
                                            start=True, stop=True)
                            inc("mm_att", ins)

                        def emit_l16(kt):
                            tn.wait_ge(sem("chs"), csc + kt + 1)
                            if kt == 0 and hq >= 2:
                                tn.wait_ge(sem("chv"), veA + 39 * (hq - 2) + 37)
                            ins = tn.matmul(pbank[4 + qh][0:1, :], onescol_b[:],
                                            P_sb2[:, kt % 4, :],
                                            start=(kt == 0), stop=(kt == 31))
                            inc("mm_att", ins)

                        emit_S(0)
                        for kt in range(31):
                            emit_S(kt + 1)
                            emit_l16(kt)
                        emit_l16(31)

                        tn.wait_ge(sem("chv"), veA + 39 * hq + 3)
                        if hq >= 2:
                            tn.wait_ge(sem("chv"), veA + 39 * (hq - 2) + 38)
                        inc("mm_nb", tn.matmul(pbank[6 + qh][:], onesr_f[:], negmhf[:],
                                               start=True, stop=True))

                        def emit_SR(kt):
                            if kt == 0 and hq >= 2:
                                tn.wait_ge(sem("chv"), veA + 39 * (hq - 2) + 39)
                            if kt >= 2:
                                tn.wait_ge(sem("chs"), csc + 33 + kt - 1)
                            else:
                                tn.wait_ge(sem("chs"), csc + 31 + kt)
                            ins = tn.matmul(pbank[kt % 2][:],
                                            kslc[:, kt * 128:(kt + 1) * 128],
                                            q_fm[:, h, qh * 512:(qh + 1) * 512],
                                            start=True, stop=True)
                            inc("mm_att", ins)

                        def emit_lO(kt):
                            tn.wait_ge(sem("chs"), csc + 33 + kt + 1)
                            tn.matmul(pbank[4 + qh][0:1, :], onescol_b[:],
                                      P_sb2[:, kt % 4, :],
                                      start=(kt == 0), stop=(kt == 31))
                            ins = tn.matmul(pbank[2 + qh][:],
                                            vslc[:, kt * 128:(kt + 1) * 128],
                                            P_sb2[:, kt % 4, :],
                                            start=(kt == 0), stop=(kt == 31))
                            inc("mm_att", ins)

                        emit_SR(0)
                        for kt in range(31):
                            emit_SR(kt + 1)
                            emit_lO(kt)
                        emit_lO(31)
                        tn.wait_ge(sem("chv"), veA + 39 * hq + 37)
                        inc("mm_att", tn.matmul(pbank[6 + qh][:], onesr_f[:], lin_sb[:],
                                                start=True, stop=True))

            @blk.scalar
            def _(sc):
                sc = P.chain(sc, "chs")
                sc.wait_ge(sem("chv"), m_mbias)
                for h in range(16):
                    for qh in range(2):
                        hq = h * 2 + qh
                        base = attB + 129 * hq
                        for kt in range(32):
                            sc.wait_ge(sem("mm_att"), base + (1 if kt == 0 else 2 * kt))
                            if kt >= 4:
                                sc.wait_ge(sem("mm_att"), base + 2 * (kt - 4) + 3)
                            elif hq >= 1:
                                sc.wait_ge(sem("mm_att"),
                                           attB + 129 * (hq - 1) + 64 +
                                           (2 * (28 + kt) + 3 if kt < 3 else 64))
                            sc.activation(P_sb2[:, kt % 4, :], pbank[kt % 2][:], AF.Exp,
                                          bias=mbias[:, h:h + 1], scale=1.0 / CEXP)
                        sc.wait_ge(sem("mm_att"), base + 64)
                        sc.activation(lnv[:], pbank[4 + qh][0:1, :], AF.Ln)
                        for kt in range(32):
                            sc.wait_ge(sem("chv"), veA + 39 * hq + 5 + kt)
                            if kt >= 4:
                                sc.wait_ge(sem("mm_att"), base + 64 + 2 * (kt - 4) + 3)
                            else:
                                sc.wait_ge(sem("mm_att"),
                                           base + (2 * (28 + kt) + 3 if kt < 3 else 64))
                            sc.activation(P_sb2[:, kt % 4, :], pbank[kt % 2][:], AF.Exp,
                                          bias=0.0, scale=1.0)

            @blk.vector
            def _(v):
                v = P.chain(v, "chv")
                for h in range(16):
                    for qh in range(2):
                        hq = h * 2 + qh
                        base = attB + 129 * hq
                        csc = scA + 65 * hq
                        v.wait_ge(sem("chs"), csc + 33)
                        if hq >= 1:
                            v.wait_ge(sem("mm_nb"), hq)
                        v.tensor_scalar_mul(lnv[:], lnv[:], float(CEXP))
                        v.tensor_scalar(lnv[:], lnv[:], kn[0:1, 48 + h:49 + h], None,
                                        op0=ALU.add)
                        v.tensor_scalar_mul(negmhf[:], lnv[:], -1.0)
                        assert tot("chv") == veA + 39 * hq + 3
                        v.wait_ge(sem("mm_nb"), hq + 1)
                        v.tensor_copy(negmb[:], pbank[6 + qh][:])
                        for kt in range(32):
                            v.wait_ge(sem("mm_att"),
                                      base + 64 + (1 if kt == 0 else 2 * kt))
                            v.tensor_add(pbank[kt % 2][:], pbank[kt % 2][:],
                                         negmb[:])
                        assert tot("chv") == veA + 39 * hq + 36
                        v.wait_ge(sem("mm_att"), base + 128)
                        v.reciprocal(lin_sb[:], pbank[4 + qh][0:1, :])
                        v.wait_ge(sem("mm_att"), base + 129)
                        v.tensor_copy(linb_sb[:], pbank[6 + qh][:])
                        v.tensor_mul(attn_fm[:, h, qh * 512:(qh + 1) * 512],
                                     pbank[2 + qh][:], linb_sb[:])
                        assert tot("chv") == veA + 39 * (hq + 1)
            m_attn = tot("chv")

        wo0 = bigD.ap().rearrange("p a t -> p (a t)").rearrange("p (oc c) -> p oc c", oc=16)
        wo1 = bigC.ap().rearrange("p a t -> p (a t)").rearrange("p (oc c) -> p oc c", oc=16)
        nspill = 2 if stage == 3 else 1
        with nc.Block() as blk:
            veO = tot("chv")
            xl = [sbt(f"xl{i}", [128, D], F32, pst) for i in range(2)]
            x2t = [sbt(f"x2t{i}", [128, D], F32, pst) for i in range(2)]
            tmpf = sbt("tmpf", [128, 512], F32, pst)

            @blk.gpsimd
            def _(g):
                wait(g, "dma_kv")
                inc("dma_wo", g.dma_start(
                    wo0[:],
                    woutT_d.ap().rearrange("(oc p) d -> p oc d", p=128)[:, :, 0:1024]), 16)
                g.wait_ge(sem("chv"), m_attn)
                inc("dma_wo", g.dma_start(
                    wo1[:],
                    woutT_d.ap().rearrange("(oc p) d -> p oc d", p=128)[:, :, 1024:2048]), 16)

            @blk.sync
            def _(s):
                for tc in range(8):
                    if tc >= 2:
                        s.wait_ge(sem("chv"), veO + 8 * (tc - 1))
                    inc(f"dma_xr{tc % 2}", s.dma_start(xl[tc % 2][:],
                                                       x_d[tc * 128:(tc + 1) * 128, :]), 16)

            @blk.tensor
            def _(tn):
                tn.wait_ge(sem("chv"), m_attn)
                tn.wait_ge(sem("dma_wo"), 32)
                for tc in range(8):
                    for dq in range(4):
                        bk = (tc * 4 + dq) % 4
                        g_ = tc * 4 + dq
                        if g_ >= 4:
                            gp = g_ - 4
                            tn.wait_ge(sem("chv"), veO + 8 * (gp // 4) + 2 * (gp % 4) + 1)
                        wos = wo0 if dq < 2 else wo1
                        for oc in range(16):
                            ins = tn.matmul(pbank[bk][:],
                                            attn_fm[:, oc, tc * 128:(tc + 1) * 128],
                                            wos[:, oc, (dq % 2) * 512:(dq % 2 + 1) * 512],
                                            start=(oc == 0), stop=(oc == 15))
                        inc("mm_op", ins)

            @blk.vector
            def _(v):
                v = P.chain(v, "chv")
                for tc in range(8):
                    v.wait_ge(sem(f"dma_xr{tc % 2}"), 16 * (tc // 2 + 1))
                    if tc >= 2:
                        v.wait_ge(sem("dma_x2"), 16 * nspill * tc)
                    for dq in range(4):
                        bk = (tc * 4 + dq) % 4
                        v.wait_ge(sem("mm_op"), tc * 4 + dq + 1)
                        v.tensor_mul(tmpf[:], pbank[bk][:], gag[:, dq * 512:(dq + 1) * 512])
                        v.tensor_add(x2t[tc % 2][:, dq * 512:(dq + 1) * 512],
                                     xl[tc % 2][:, dq * 512:(dq + 1) * 512], tmpf[:])
                    assert tot("chv") == veO + 8 * tc + 8
            m_x2 = tot("chv")

            @blk.gpsimd
            def _(g2):
                for tc in range(8):
                    g2.wait_ge(sem("chv"), veO + 8 * (tc + 1))
                    inc("dma_x2", g2.dma_start(x2spill.ap()[tc * 128:(tc + 1) * 128, :],
                                               x2t[tc % 2][:]), 16)
                    if stage == 3:
                        inc("dma_x2", g2.dma_start(dbg["x2"].ap()[tc * 128:(tc + 1) * 128, :],
                                                   x2t[tc % 2][:]), 16)

    if stage == 3:
        with nc.Block() as blk:
            @blk.sync
            def _(s):
                wait(s, "dma_x2")
        st.close()
        return nc, dbg

    h2fm = bigA
    with contextlib.ExitStack() as pst:
        zl = [pst.enter_context(nc.sbuf_tensor(f"zl{i}", [128, D], F32)) for i in range(2)]
        znt = [pst.enter_context(nc.sbuf_tensor(f"znt{i}", [128, D], BF16)) for i in range(2)]
        h2t_t = [pst.enter_context(nc.sbuf_tensor(f"h2t{i}", [128, D], BF16)) for i in range(2)]
        sq2 = pst.enter_context(nc.sbuf_tensor("sq2", [128, D], F32))
        veL2 = tot("chv")
        scL2 = tot("chs")

        def l2_xn(tc):
            return veL2 + 28 * tc + 10

        def l2_tm(tc):
            return veL2 + 28 * tc + 12

        def l2_mod(tc):
            return veL2 + 28 * (tc + 1)

        with nc.Block() as blk:
            d_x2_all = tot("dma_x2")

            @blk.sync
            def _(s_):
                s_.wait_ge(sem("dma_x2"), d_x2_all)
                for tc in range(8):
                    if tc >= 2:
                        s_.wait_ge(sem("chv"), l2_xn(tc - 2))
                    inc(f"dma_z{tc % 2}", s_.dma_start(
                        zl[tc % 2][:], x2spill.ap()[tc * 128:(tc + 1) * 128, :]), 16)

            @blk.vector
            def _(v):
                v = P.chain(v, "chv")
                v.wait_ge(sem("dma_x2"), d_x2_all)
                for tc in range(8):
                    v.wait_ge(sem(f"dma_z{tc % 2}"), 16 * (tc // 2 + 1))
                    x_ = zl[tc % 2]
                    mean, msq, var, rin = (stats[:, 4:5], stats[:, 5:6],
                                           stats[:, 6:7], stats[:, 7:8])
                    v.reduce_sum(mean, x_[:], axis=AX.X)
                    v.tensor_scalar_mul(mean, mean, 1.0 / D)
                    v.tensor_mul(sq2[:], x_[:], x_[:])
                    v.reduce_sum(msq, sq2[:], axis=AX.X)
                    v.tensor_scalar_mul(msq, msq, 1.0 / D)
                    v.tensor_mul(var, mean, mean)
                    v.tensor_sub(var, msq, var)
                    v.tensor_scalar_add(var, var, EPS)
                    v.reciprocal(rin, var)
                    v.wait_ge(sem("chs"), scL2 + tc + 1)
                    v.scalar_tensor_tensor(znt[tc % 2][:], x_[:], mean,
                                           stats[:, 8:9].to_broadcast([128, D]),
                                           op0=ALU.subtract, op1=ALU.mult)
                    assert tot("chv") == l2_xn(tc)
                    if tc >= 2:
                        v.wait_ge(sem(f"dma_h2{tc % 2}"), 16 * (tc // 2))
                    v.tensor_mul(h2t_t[tc % 2][:], znt[tc % 2][:], cbb[:])
                    v.tensor_add(h2t_t[tc % 2][:], h2t_t[tc % 2][:], sbb[:])
                    assert tot("chv") == l2_tm(tc)
                    v.wait_ge(sem("mm_l2"), tc + 1)
                    for dc in range(16):
                        bk = (tc % 2) * 2 + dc // 8
                        v.scalar_tensor_tensor(
                            h2fm[:, dc, tc * 128:(tc + 1) * 128],
                            pbank_bf(bk)[:, (dc % 8) * 128:(dc % 8 + 1) * 128],
                            mparm[:, 32 + dc:33 + dc],
                            mparm[:, 48 + dc:49 + dc].to_broadcast([128, 128]),
                            op0=ALU.mult, op1=ALU.add)
                    assert tot("chv") == l2_mod(tc)

            @blk.scalar
            def _(sc):
                sc = P.chain(sc, "chs")
                sc.wait_ge(sem("chv"), m_x2)
                for tc in range(8):
                    sc.wait_ge(sem("chv"), veL2 + 28 * tc + 9)
                    sc.activation(stats[:, 8:9], stats[:, 7:8], AF.Sqrt)

            @blk.tensor
            def _(tn):
                for tc in range(8):
                    tn.wait_ge(sem("chv"), l2_xn(tc))
                    if tc >= 2:
                        tn.wait_ge(sem("chv"), l2_mod(tc - 2))
                    for dc in range(16):
                        bk = (tc % 2) * 2 + dc // 8
                        ins = tn.transpose(
                            pbank_bf(bk)[:, (dc % 8) * 128:(dc % 8 + 1) * 128],
                            znt[tc % 2][:, dc * 128:(dc + 1) * 128], identb[:])
                    inc("mm_l2", ins)

            @blk.gpsimd
            def _(g):
                for tc in range(8):
                    g.wait_ge(sem("chv"), l2_tm(tc))
                    inc(f"dma_h2{tc % 2}", g.dma_start(h2in.ap()[tc * 128:(tc + 1) * 128, :],
                                                       h2t_t[tc % 2][:]), 16)
                wait(g, "dma_h20")
                wait(g, "dma_h21")
                inc("cc_h2", g.collective_compute(
                    "AllGather", ALU.bypass, replica_groups=[list(range(8))],
                    ins=[h2in.ap().opt()], outs=[h2all.ap().opt()]))
        m_h2 = tot("chv")

    aff7 = sbt("aff7", [E, T], F32)
    afftm = sbt("afftm", [128, 8, E], F32)
    with contextlib.ExitStack() as pst:
        gw_sb = pst.enter_context(nc.sbuf_tensor("gw_sb", [128, 16, E], BF16))
        with nc.Block() as blk:
            scG = tot("chs")
            veGT = tot("chv")

            @blk.gpsimd
            def _(g):
                inc("dma_gw", g.dma_start(
                    gw_sb.ap(), gatew_d.ap().rearrange("(dc p) e -> p dc e", p=128)), 16)

            @blk.tensor
            def _(tn):
                tn.wait_ge(sem("chv"), m_h2)
                wait(tn, "dma_gw")
                for th in range(2):
                    for dc in range(16):
                        ins = tn.matmul(pbank[th][0:E, :], gw_sb[:, dc, :],
                                        h2fm[:, dc, th * 512:(th + 1) * 512],
                                        start=(dc == 0), stop=(dc == 15))
                    inc("mm_g", ins)
                for tc in range(8):
                    tn.wait_ge(sem("chs"), scG + 2)
                    if tc >= 2:
                        tn.wait_ge(sem("chv"), veGT + tc - 1)
                    ins = tn.transpose(pbank[2 + tc % 2][:, 0:E],
                                       aff7[:, tc * 128:(tc + 1) * 128], identf[0:E, 0:E])
                    inc("mm_gt", ins)

            @blk.scalar
            def _(sc):
                sc = P.chain(sc, "chs")
                for th in range(2):
                    sc.wait_ge(sem("mm_g"), th + 1)
                    sc.activation(aff7[:, th * 512:(th + 1) * 512], pbank[th][0:E, :],
                                  AF.Sigmoid, bias=ebias_sb[:])

            @blk.vector
            def _(v):
                v = P.chain(v, "chv")
                for tc in range(8):
                    v.wait_ge(sem("mm_gt"), tc + 1)
                    v.tensor_copy(afftm[:, tc, :], pbank[2 + tc % 2][:, 0:E])

            @blk.gpsimd
            def _(g):
                g.wait_ge(sem("chs"), scG + 2)
                inc("dma_af", g.dma_start(
                    affin.ap().rearrange("(e a) c -> e a c", a=16),
                    aff7.ap().rearrange("e (a c) -> e a c", a=16)), 16)
                wait(g, "dma_af")
                inc("cc_af", g.collective_compute(
                    "AllGather", ALU.bypass, replica_groups=[list(range(8))],
                    ins=[affin.ap().opt()], outs=[affall.ap().opt()]))
        m_afftm = tot("chv")

    CAPK = 1169
    olist = sbt("olist", [16, 128], I16)
    th_sb = sbt("th_sb", [1, 16], F32)
    idx_e = sbt("idx_e", [16, 7 * 80], I16)
    with contextlib.ExitStack() as pst:
        def rt(name):
            return pst.enter_context(nc.sbuf_tensor(name, [16, 512], F32))
        affs = [rt(f"affs{e}") for e in range(E)]
        m1r = rt("m1r"); m2r = rt("m2r"); sumeq = rt("sumeq")
        e1r = rt("e1r"); e2r = rt("e2r")
        acc1 = rt("acc1"); acc2 = rt("acc2"); k1acc = rt("k1acc"); k2acc = rt("k2acc")
        tA = rt("tA"); tB = rt("tB"); tC = rt("tC"); tD = rt("tD")
        keepb = pst.enter_context(nc.sbuf_tensor("keepb", [16, 512], BF16))
        vali = pst.enter_context(nc.sbuf_tensor("vali", [16, 512], I16))
        csrow = pst.enter_context(nc.sbuf_tensor("csrow", [1, 512 * 3], F32))
        sc128 = pst.enter_context(nc.sbuf_tensor("sc128", [128, 64], F32))
        thb16 = pst.enter_context(nc.sbuf_tensor("thb16", [16, E], F32))
        thr3 = pst.enter_context(nc.sbuf_tensor("thr3", [1, 8], F32))
        thr8 = pst.enter_context(nc.sbuf_tensor("thr8", [1, 8], F32))
        lohi = pst.enter_context(nc.sbuf_tensor("lohi", [16, 2], I32))
        midt = pst.enter_context(nc.sbuf_tensor("midt", [16, 1], I32))
        bisf = pst.enter_context(nc.sbuf_tensor("bisf", [16, 2], F32))
        bisp = pst.enter_context(nc.sbuf_tensor("bisp", [16, 2], I16))
        ones16 = pst.enter_context(nc.sbuf_tensor("ones16_sb", [16, 16], F32))
        sck = pst.enter_context(nc.sbuf_tensor("sck", [128, 64], F32))
        tE128 = pst.enter_context(nc.sbuf_tensor("tE128", [128, 64], F32))
        own16 = pst.enter_context(nc.sbuf_tensor("own16", [16, 2 * 64], I16))
        tBi = pst.enter_context(nc.sbuf_tensor("tBi", [16, 512], I16))

        with nc.Block() as blk:
            @blk.gpsimd
            def _(g):
                wait(g, "cc_af")
                for e in range(E):
                    inc("dma_as", g.dma_start(
                        affs[e].ap().rearrange("a (r c) -> a r c", r=8),
                        affall.ap().rearrange("(r e a) c -> e a r c", r=8, e=E)[e]), 16)

            @blk.vector
            def _(v):
                v = P.chain(v, "chv")
                wait(v, "dma_as")
                wait(v, "dma_z0")
                wait(v, "dma_z1")
                wait(v, "dma_gw")
                mins = v.memset(th_sb[:], 0.0)
                P.inc("chv", mins)
                v.wait_ge(sem("chv"), tot("chv"))
                v.tensor_max(m1r[:], affs[0][:], affs[1][:])
                for e in range(2, E):
                    v.tensor_max(m1r[:], m1r[:], affs[e][:])
                for e in range(E):
                    v.tensor_tensor(tA[:], affs[e][:], m1r[:], op=ALU.is_equal)
                    if e == 0:
                        v.tensor_copy(sumeq[:], tA[:])
                        v.tensor_scalar_mul(e1r[:], tA[:], float(E - e))
                    else:
                        v.tensor_add(sumeq[:], sumeq[:], tA[:])
                        v.tensor_scalar_mul(tB[:], tA[:], float(E - e))
                        v.tensor_max(e1r[:], e1r[:], tB[:])
                for e in range(E):
                    v.tensor_tensor(tA[:], affs[e][:], m1r[:], op=ALU.is_equal)
                    v.tensor_scalar_mul(tA[:], tA[:], -1e30)
                    v.tensor_add(tA[:], tA[:], affs[e][:])
                    if e == 0:
                        v.tensor_copy(m2r[:], tA[:])
                    else:
                        v.tensor_max(m2r[:], m2r[:], tA[:])
                v.tensor_single_scalar(tB[:], sumeq[:], 2.0, op=ALU.is_ge)
                v.tensor_copy(tBi[:], tB[:])
                v.copy_predicated(m2r[:], tBi[:], m1r[:])
                for e in range(E):
                    v.tensor_tensor(tA[:], affs[e][:], m1r[:], op=ALU.is_equal)
                    v.tensor_scalar_mul(tA[:], tA[:], float(E - e))
                    v.tensor_tensor(tC[:], tA[:], e1r[:], op=ALU.is_equal)
                    v.tensor_scalar(tC[:], tC[:], -1.0, 1.0, op0=ALU.mult, op1=ALU.add)
                    v.tensor_mul(tA[:], tA[:], tC[:])
                    if e == 0:
                        v.tensor_copy(tD[:], tA[:])
                    else:
                        v.tensor_max(tD[:], tD[:], tA[:])
                for e in range(E):
                    v.tensor_tensor(tA[:], affs[e][:], m2r[:], op=ALU.is_equal)
                    v.tensor_scalar_mul(tA[:], tA[:], float(E - e))
                    if e == 0:
                        v.tensor_copy(tC[:], tA[:])
                    else:
                        v.tensor_max(tC[:], tC[:], tA[:])
                v.copy_predicated(tC[:], tBi[:], tD[:])
                v.tensor_scalar(e2r[:], tC[:], -1.0, float(E), op0=ALU.mult, op1=ALU.add)
                v.tensor_scalar(e1r[:], e1r[:], -1.0, float(E), op0=ALU.mult, op1=ALU.add)
            m_top2 = tot("chv")

            @blk.vector
            def _(v):
                v = P.chain(v, "chv")
                for e in range(E):
                    v.tensor_tensor(tA[:], affs[e][:], m2r[:], op=ALU.is_ge)
                    v.tensor_scalar_add(tB[:], affs[e][:], 1.0)
                    v.tensor_mul(tB[:], tA[:], tB[:])
                    v.tensor_scalar(tB[:], tB[:], eoh[:, e:e + 1], None, op0=ALU.mult)
                    if e == 0:
                        v.tensor_copy(tD[:], tB[:])
                    else:
                        v.tensor_add(tD[:], tD[:], tB[:])
                v.tensor_scalar_add(tD[:], tD[:], -1.0)
            m_sco = tot("chv")

            @blk.gpsimd
            def _(g):
                inc("dma_th", g.dma_start(lohi[:], bisinit_d.ap()), 16)
                inc("dma_th", g.dma_start(ones16[:], ones16_d.ap()), 16)
                g.wait_ge(sem("chv"), m_sco + 10 * 30)
                inc("dma_th", g.dma_start(thin.ap(), lohi[0:1, 0:1].bitcast(F32)), 16)
                g.wait_ge(sem("dma_th"), 48)
                inc("cc_th", g.collective_compute(
                    "AllGather", ALU.bypass, replica_groups=[list(range(8))],
                    ins=[thin.ap().opt()], outs=[thall.ap().opt()]))
                wait(g, "cc_th")
                inc("dma_th", g.dma_start(thr8[:], bass.AP(thall, 0, [[8, 1], [1, 8]])), 16)

            @blk.tensor
            def _(tn):
                tn.wait_ge(sem("dma_th"), 32)
                for it in range(30):
                    tn.wait_ge(sem("chv"), m_sco + 10 * it + 4)
                    ins = tn.matmul(pbank[5][0:16, 0:1], ones16[:],
                                    bisf[:, 0:1], start=True, stop=True)
                    inc("mm_bis", ins)

            @blk.vector
            def _(v):
                v = P.chain(v, "chv")
                v.wait_ge(sem("dma_th"), 32)
                for it in range(30):
                    v.tensor_tensor(midt[:], lohi[:, 0:1], lohi[:, 1:2], op=ALU.add)
                    v.tensor_single_scalar(midt[:], midt[:], 1,
                                           op=ALU.logical_shift_right)
                    v.tensor_scalar(tA[:], tD[:], midt[:, 0:1].bitcast(F32), None,
                                    op0=ALU.is_ge)
                    v.reduce_sum(bisf[:, 0:1], tA[:], axis=AX.X)
                    v.wait_ge(sem("mm_bis"), it + 1)
                    v.tensor_single_scalar(bisf[:, 0:1], pbank[5][0:16, 0:1], 1170.0,
                                           op=ALU.is_ge)
                    v.tensor_copy(bisp[:, 0:1], bisf[:, 0:1])
                    v.tensor_single_scalar(bisf[:, 1:2], bisf[:, 0:1], 0.0,
                                           op=ALU.is_equal)
                    v.tensor_copy(bisp[:, 1:2], bisf[:, 1:2])
                    v.copy_predicated(lohi[:, 0:1], bisp[:, 0:1], midt[:])
                    v.copy_predicated(lohi[:, 1:2], bisp[:, 1:2], midt[:])
                    assert tot("chv") == m_sco + 10 * (it + 1)
                v.wait_ge(sem("dma_th"), 64)
                v.tensor_copy(bass.AP(th_sb, 0, [[16, 1], [2, E]]), thr8[0:1, 0:E])
            m_kth = tot("chv")

            @blk.tensor
            def _(tn):
                tn.wait_ge(sem("chv"), m_kth)
                for e in range(E):
                    ins = tn.matmul(pbank[0][0:16, e:e + 1], onesr_f[:, 0:16],
                                    th_sb[0:1, 2 * e:2 * e + 1], start=True, stop=True)
                inc("mm_thb", ins)
                ins = tn.matmul(pbank[1][:, 0:E], onesr_f[:],
                                th_sb[0:1, 0:2 * E:2], start=True, stop=True)
                inc("mm_thb", ins)

            @blk.vector
            def _(v):
                v = P.chain(v, "chv")
                wait(v, "mm_thb")
                v.tensor_copy(thb16[:], pbank[0][0:16, 0:E])
                v.tensor_copy(thtm[:], pbank[1][:, 0:E])
            m_thb = tot("chv")

            @blk.vector
            def _(v):
                v = P.chain(v, "chv")
                for e in range(E):
                    v.tensor_tensor(tA[:], affs[e][:], m2r[:], op=ALU.is_ge)
                    v.tensor_scalar(tB[:], affs[e][:], thb16[:, e:e + 1], None, op0=ALU.is_ge)
                    v.tensor_mul(tA[:], tA[:], tB[:])
                    if e >= 1:
                        wait(v, "gp_sp", e)
                    v.tensor_mul(tB[:], tA[:], gmapp1[:])
                    v.tensor_scalar_add(tB[:], tB[:], -1.0)
                    v.tensor_copy(vali[:], tB[:])
                    m_val = tot("chv")
                    v.tensor_copy(keepb[:], tA[:])
                    m_keepb = tot("chv")
                    v.wait_ge(sem("mm_pos"), 2 * e + 1)
                    v.tensor_copy(csrow[0:1, 0:512], pbank[3][0:1, :])
                    v.tensor_tensor_scan(csrow[0:1, 512:1024], csrow[0:1, 0:512],
                                         csrow[0:1, 0:512], 0.0,
                                         op0=ALU.add, op1=ALU.bypass)
                    v.tensor_sub(csrow[0:1, 1024:1536], csrow[0:1, 512:1024],
                                 csrow[0:1, 0:512])
                    m_excl = tot("chv")
                    v.wait_ge(sem("mm_pos"), 2 * e + 2)
                    v.tensor_copy(tB[:], pbank[2][0:16, :])
                    v.wait_ge(sem("mm_pos2"), e + 1)
                    v.tensor_copy(tC[:], pbank[4][0:16, :])
                    v.tensor_add(tB[:], tB[:], tC[:])
                    v.tensor_single_scalar(tC[:], e1r[:], float(e), op=ALU.is_equal)
                    v.tensor_single_scalar(tD[:], e2r[:], float(e), op=ALU.is_equal)
                    if e == 0:
                        v.tensor_mul(acc1[:], tC[:], tB[:])
                        v.tensor_mul(acc2[:], tD[:], tB[:])
                        v.tensor_mul(k1acc[:], tC[:], tA[:])
                        v.tensor_mul(k2acc[:], tD[:], tA[:])
                    else:
                        v.tensor_mul(m1r[:], tC[:], tB[:])
                        v.tensor_add(acc1[:], acc1[:], m1r[:])
                        v.tensor_mul(m1r[:], tD[:], tB[:])
                        v.tensor_add(acc2[:], acc2[:], m1r[:])
                        v.tensor_mul(m1r[:], tC[:], tA[:])
                        v.tensor_add(k1acc[:], k1acc[:], m1r[:])
                        v.tensor_mul(m1r[:], tD[:], tA[:])
                        v.tensor_add(k2acc[:], k2acc[:], m1r[:])
                    P.marks[f"val{e}"] = m_val
                    P.marks[f"keepb{e}"] = m_keepb
                    P.marks[f"excl{e}"] = m_excl
                v.wait_ge(sem("gp_sp"), E)
                v.tensor_scalar_mul(tA[:], e1r[:], float(CAPP))
                v.tensor_add(tA[:], tA[:], acc1[:])
                v.tensor_mul(tA[:], tA[:], k1acc[:])
                v.tensor_scalar_min(tA[:], tA[:], float(NSLOT - 1))
                v.tensor_scalar_mul(tB[:], e2r[:], float(CAPP))
                v.tensor_add(tB[:], tB[:], acc2[:])
                v.tensor_mul(tB[:], tB[:], k2acc[:])
                v.tensor_scalar_min(tB[:], tB[:], float(NSLOT - 1))
                v.tensor_copy(vali[:], tA[:])
                v.tensor_copy(keepb.ap().bitcast(I16), tB[:])
            m_sidx = tot("chv")

            @blk.tensor
            def _(tn):
                for e in range(E):
                    tn.wait_ge(sem("chv"), P.marks[f"keepb{e}"])
                    if e >= 1:
                        tn.wait_ge(sem("chv"), P.marks[f"excl{e - 1}"] + 2)
                    ins = tn.matmul(pbank[3][0:1, :], onescol_b[0:16, 0:1], keepb[:],
                                    start=True, stop=True)
                    inc("mm_pos", ins)
                    ins = tn.matmul(pbank[2][0:16, :], u16s[:], keepb[:],
                                    start=True, stop=True)
                    inc("mm_pos", ins)
                    tn.wait_ge(sem("chv"), P.marks[f"excl{e}"])
                    ins = tn.matmul(pbank[4][0:16, :], onesr_f[:, 0:16],
                                    csrow[0:1, 1024:1536], start=True, stop=True)
                    inc("mm_pos2", ins)

            @blk.gpsimd
            def _(g):
                g.load_library(LIB_SG)
                for e in range(E):
                    g.wait_ge(sem("chv"), P.marks[f"val{e}"])
                    if e >= 1:
                        wait(g, "gp_sp", e)
                    inc("gp_sp", g.sparse_gather(idx_e[:, e * 80:(e + 1) * 80], vali[:],
                                                 num_found=nf_sb[:]))
                g.wait_ge(sem("chv"), m_sidx)
                pid = nc.partition_id()
                src1 = bass.AP(vali, pid * 64, [[512, 16], [1, 64]])
                src2 = bass.AP(keepb, pid * 64, [[512, 16], [1, 64]]).bitcast(I16)
                inc("dma_ow", g.dma_start(own16[:, 0:64], src1), 16)
                inc("dma_ow", g.dma_start(own16[:, 64:128], src2), 16)
                wait(g, "dma_ow")
                inc("dma_ol", g.dma_start(olscr.ap(), own16.ap()), 16)
                wait(g, "dma_ol")
                for k_ in range(2):
                    for a_ in range(16):
                        inc("dma_ol", g.dma_start(
                            bass.AP(olist, 64 * k_ + 4 * a_, [[128, 16], [1, 4]]),
                            bass.AP(olscr, 64 * k_ + 128 * a_, [[1, 16], [16, 4]]),
                            allow_slow_non_contiguous=True), 16)
        m_rt = tot("chv")

    with nc.Block() as blk:
        @blk.vector
        def _(v):
            v = P.chain(v, "chv")
            v.wait_ge(sem("chv"), m_thb)
            for tc in range(8):
                a_ = afftm[:, tc, :]
                m1t = stats[:, 0:1]
                m2t = stats[:, 3:4]
                v.reduce_max(m1t, a_, axis=AX.X)
                v.tensor_tensor(sq_rt1[:], a_, m1t.to_broadcast([128, E]), op=ALU.is_equal)
                v.reduce_sum(stats[:, 11:12], sq_rt1[:], axis=AX.X)
                v.tensor_single_scalar(stats[:, 12:13], stats[:, 11:12], 2.0, op=ALU.is_ge)
                v.tensor_scalar_mul(sq_rt2[:], sq_rt1[:], -1e30)
                v.tensor_add(sq_rt2[:], sq_rt2[:], a_)
                v.reduce_max(stats[:, 4:5], sq_rt2[:], axis=AX.X)
                v.tensor_copy(m2t, stats[:, 4:5])
                v.tensor_copy(manyi[:], stats[:, 12:13])
                v.copy_predicated(m2t, manyi[:], m1t)
                v.tensor_mul(sq_rt2[:], sq_rt1[:], irev[:])
                v.reduce_max(stats[:, 10:11], sq_rt2[:], axis=AX.X)
                v.tensor_tensor(sq_rt1[:], sq_rt2[:], stats[:, 10:11].to_broadcast([128, E]),
                                op=ALU.is_equal)
                v.tensor_scalar(sq_rt3[:], sq_rt1[:], -1.0, 1.0, op0=ALU.mult, op1=ALU.add)
                v.tensor_mul(sq_rt2[:], sq_rt2[:], sq_rt3[:])
                v.reduce_max(stats[:, 13:14], sq_rt2[:], axis=AX.X)
                v.tensor_tensor(sq_rt3[:], a_, m2t.to_broadcast([128, E]), op=ALU.is_equal)
                v.tensor_mul(sq_rt3[:], sq_rt3[:], irev[:])
                v.reduce_max(stats[:, 14:15], sq_rt3[:], axis=AX.X)
                v.tensor_tensor(sq_rt2[:], sq_rt2[:], stats[:, 13:14].to_broadcast([128, E]),
                                op=ALU.is_equal)
                v.tensor_tensor(sq_rt3[:], sq_rt3[:], stats[:, 14:15].to_broadcast([128, E]),
                                op=ALU.is_equal)
                v.copy_predicated(sq_rt3[:], manyi[:].to_broadcast([128, E]), sq_rt2[:])
                v.tensor_mul(sq_rt1[:], sq_rt1[:], thtm[:])
                v.reduce_sum(stats[:, 5:6], sq_rt1[:], axis=AX.X)
                v.tensor_mul(sq_rt3[:], sq_rt3[:], thtm[:])
                v.reduce_sum(stats[:, 6:7], sq_rt3[:], axis=AX.X)
                v.tensor_tensor(stats[:, 7:8], m1t, stats[:, 5:6], op=ALU.is_ge)
                v.tensor_tensor(stats[:, 8:9], m2t, stats[:, 6:7], op=ALU.is_ge)
                v.tensor_mul(wownb[:, tc:tc + 1], stats[:, 7:8], m1t)
                v.tensor_mul(wownb[:, 8 + tc:9 + tc], stats[:, 8:9], m2t)
    m_wown = tot("chv")

    if stage == 4:
        with nc.Block() as blk:
            @blk.sync
            def _(s_):
                s_.wait_ge(sem("chv"), m_wown)
                wait(s_, "dma_ol")
                wait(s_, "dma_z0")
                wait(s_, "dma_z1")
                inc("dma_o", s_.dma_start(dbg["aff"].ap(), aff7.ap()), 16)
                inc("dma_o", s_.dma_start(dbg["theta"].ap(), th_sb.ap()), 16)
                inc("dma_o", s_.dma_start(dbg["sidx"].ap(), olist.ap()), 16)
                inc("dma_o", s_.dma_start(dbg["wown"].ap(), wownb.ap()), 16)
                inc("dma_o", s_.dma_start(dbg["h2t"].ap(), bigA.ap()), 16)
                wait(s_, "dma_o")
        st.close()
        return nc, dbg

    pst5 = contextlib.ExitStack()
    silb = sbt("silb", [128, 512], BF16, pst5)
    routsb = sbt("routsb", [128, 4, D], BF16, pst5)
    wstr5 = sbt("wstr5", [128, 2, 16, 512], BF16, pst5)
    x1g = bigB.ap().rearrange("p a t -> p (a t)").rearrange("p (fc t) -> p fc t", fc=16)
    shared_tm = bigC.ap().rearrange("p a t -> p (a t)").rearrange("p (tc d) -> p tc d", tc=8)
    with nc.Block() as blk:
        scS = tot("chs")
        veS = tot("chv")
        mmS = tot("mm_ff") if "mm_ff" in P.cnt else 0

        @blk.gpsimd
        def _(g):
            g.wait_ge(sem("chv"), veS)
            g.wait_ge(sem("chs"), scS)
            for nm in ("dma_ol", "dma_z0", "dma_z1", "gp_sp", "dma_o"):
                if nm in P.cnt:
                    wait(g, nm)
            for fq in range(4):
                if fq >= 2:
                    wait(g, "mm_ff", 8 * (fq - 1))
                inc(f"dma_w1{fq % 2}", g.dma_start(
                    wstr5[:, fq % 2, :, :],
                    w1s_d.ap().rearrange("(dc p) f -> p dc f", p=128)
                    [:, :, fq * 512:(fq + 1) * 512]), 16)

        @blk.tensor
        def _(tn):
            tn.wait_ge(sem("chv"), m_h2)
            for fq in range(4):
                tn.wait_ge(sem(f"dma_w1{fq % 2}"), 16 * (fq // 2 + 1))
                for k in range(8):
                    fc, th = divmod(k, 2)
                    if fq * 8 + k >= 4:
                        tn.wait_ge(sem("chs"), scS + (fq * 8 + k) - 3)
                    for dc in range(16):
                        ins = tn.matmul(pbank[k % 4][:],
                                        wstr5[:, fq % 2, dc, fc * 128:(fc + 1) * 128],
                                        h2fm[:, dc, th * 512:(th + 1) * 512],
                                        start=(dc == 0), stop=(dc == 15))
                    inc("mm_ff", ins)

        @blk.scalar
        def _(sc):
            sc = P.chain(sc, "chs")
            for fq in range(4):
                for k in range(8):
                    fc, th = divmod(k, 2)
                    sc.wait_ge(sem("mm_ff"), 8 * fq + k + 1)
                    f_g = fq * 512 + fc * 128
                    if fq < 2:
                        sc.copy(x1g[:, f_g // 128, th * 512:(th + 1) * 512],
                                pbank[k % 4][:])
                    else:
                        idx = (fq - 2) * 8 + k
                        if idx >= 1:
                            sc.wait_ge(sem("chv"), veS + idx)
                        sc.activation(silb[:], pbank[k % 4][:], AF.Silu)

        @blk.vector
        def _(v):
            v = P.chain(v, "chv")
            for fq in range(2, 4):
                for k in range(8):
                    fc, th = divmod(k, 2)
                    idx = (fq - 2) * 8 + k
                    v.wait_ge(sem("chs"), scS + 16 + idx + 1)
                    fg = fq * 512 + fc * 128 - 1024
                    v.tensor_mul(x1g[:, 8 + fg // 128, th * 512:(th + 1) * 512],
                                 x1g[:, fg // 128, th * 512:(th + 1) * 512], silb[:])
    m_w1s = tot("chv")
    scW1 = tot("chs")

    with nc.Block() as blk:
        scS2 = tot("chs")

        @blk.gpsimd
        def _(g):
            wait(g, "mm_ff", 32)
            for dq in range(4):
                if dq >= 2:
                    wait(g, "mm_f2", 8 * (dq - 1))
                inc(f"dma_w2{dq % 2}", g.dma_start(
                    wstr5[:, dq % 2, 0:8, :],
                    w2s_d.ap().rearrange("(fc p) d -> p fc d", p=128)
                    [:, :, dq * 512:(dq + 1) * 512]), 16)

        @blk.tensor
        def _(tn):
            tn.wait_ge(sem("chv"), m_w1s)
            for dq in range(4):
                tn.wait_ge(sem(f"dma_w2{dq % 2}"), 16 * (dq // 2 + 1))
                for tc in range(8):
                    g_ = dq * 8 + tc
                    if g_ >= 4:
                        tn.wait_ge(sem("chs"), scS2 + g_ - 3)
                    for fc in range(8):
                        ins = tn.matmul(pbank[g_ % 4][:],
                                        x1g[:, 8 + fc, tc * 128:(tc + 1) * 128],
                                        wstr5[:, dq % 2, fc, :],
                                        start=(fc == 0), stop=(fc == 7))
                    inc("mm_f2", ins)

        @blk.scalar
        def _(sc):
            sc = P.chain(sc, "chs")
            for dq in range(4):
                for tc in range(8):
                    sc.wait_ge(sem("mm_f2"), 8 * dq + tc + 1)
                    sc.copy(shared_tm[:, tc, dq * 512:(dq + 1) * 512],
                            pbank[(dq * 8 + tc) % 4][:])
    m_sh = tot("chs")

    with contextlib.ExitStack() as pst:
        elrep = pst.enter_context(nc.sbuf_tensor("elrep", [128, 80], I16))
        elist = pst.enter_context(nc.sbuf_tensor("elist", [16, 80], I16))
        elf = pst.enter_context(nc.sbuf_tensor("elf", [16, 80], F32))
        elacc = pst.enter_context(nc.sbuf_tensor("elacc", [16, 80], F32))

        with nc.Block() as blk:
            @blk.vector
            def _(v):
                v = P.chain(v, "chv")
                v.wait_ge(sem("gp_sp"), E)
                for e in range(E):
                    v.tensor_copy(elf[:], idx_e[:, e * 80:(e + 1) * 80])
                    if e == 0:
                        v.tensor_scalar(elacc[:], elf[:], eoh[:, 0:1], None, op0=ALU.mult)
                    else:
                        v.tensor_scalar(elf[:], elf[:], eoh[:, e:e + 1], None, op0=ALU.mult)
                        v.tensor_add(elacc[:], elacc[:], elf[:])
                v.tensor_scalar_max(elacc[:], elacc[:], 0.0)
                v.tensor_scalar_min(elacc[:], elacc[:], 8191.0)
                v.tensor_copy(elist[:], elacc[:])
            m_el = tot("chv")

            @blk.gpsimd
            def _(g):
                g.load_library(LIB_MLP)
                g.wait_ge(sem("chv"), m_el)
                for rep in range(8):
                    inc("dma_el", g.dma_start(elrep[rep * 16:(rep + 1) * 16, :], elist[:]), 16)
                wait(g, "dma_el")
                wait(g, "cc_h2")

        CH = (512, 512, 256)
        for c in range(3):
            nsl = CH[c]
            gexp = (bigD.ap().rearrange("p a t -> p (a t)")[:, 0:16 * nsl]
                    .rearrange("p (dc x) -> p dc x", dc=16))
            with nc.Block() as blk:
                scR = tot("chs")
                veR2 = tot("chv")
                mmR = tot("mm_r1") if "mm_r1" in P.cnt else 0
                mmR2 = tot("mm_r2") if "mm_r2" in P.cnt else 0

                @blk.gpsimd
                def _(g):
                    if c > 0:
                        wait(g, "mm_r1", mmR)
                        wait(g, "mm_r2", mmR2)
                    inc("gd_x", g.dma_gather(
                        gexp[:, :, :], h2all.ap(), elrep[:, 32 * c:32 * c + nsl // 16],
                        nsl, nsl, D, transpose=True), 16)
                    for fq in range(4):
                        if fq >= 2:
                            wait(g, "mm_r1", mmR + 4 * (fq - 1))
                        inc(f"dma_r1{fq % 2}", g.dma_start(
                            wstr5[:, fq % 2, :, :],
                            w1r_d.ap().rearrange("(dc p) f -> p dc f", p=128)
                            [:, :, fq * 512:(fq + 1) * 512]), 16)

                @blk.tensor
                def _(tn):
                    wait(tn, "gd_x")
                    if c == 0:
                        tn.wait_ge(sem("chs"), m_sh)
                    for fq in range(4):
                        tn.wait_ge(sem(f"dma_r1{fq % 2}"), 16 * (2 * c + fq // 2 + 1))
                        for fc in range(4):
                            g_ = fq * 4 + fc
                            if g_ >= 4:
                                tn.wait_ge(sem("chs"), scR + g_ - 3)
                            for dc in range(16):
                                ins = tn.matmul(pbank[g_ % 4][0:128, 0:nsl],
                                                wstr5[:, fq % 2, dc, fc * 128:(fc + 1) * 128],
                                                gexp[:, dc, 0:nsl],
                                                start=(dc == 0), stop=(dc == 15))
                            inc("mm_r1", ins)

                @blk.scalar
                def _(sc):
                    sc = P.chain(sc, "chs")
                    for fq in range(4):
                        for fc in range(4):
                            g_ = fq * 4 + fc
                            sc.wait_ge(sem("mm_r1"), mmR + g_ + 1)
                            f_g = fq * 512 + fc * 128
                            if fq < 2:
                                sc.copy(x1g[:, f_g // 128, 0:nsl], pbank[g_ % 4][0:128, 0:nsl])
                            else:
                                idx = (fq - 2) * 4 + fc
                                if idx >= 1:
                                    sc.wait_ge(sem("chv"), veR2 + idx)
                                elif c > 0:
                                    sc.wait_ge(sem("chv"), veR2)
                                sc.activation(silb[:, 0:nsl], pbank[g_ % 4][0:128, 0:nsl], AF.Silu)

                @blk.vector
                def _(v):
                    v = P.chain(v, "chv")
                    for fq in range(2, 4):
                        for fc in range(4):
                            idx = (fq - 2) * 4 + fc
                            v.wait_ge(sem("chs"), scR + 8 + idx + 1)
                            fg = fq * 512 + fc * 128 - 1024
                            v.tensor_mul(x1g[:, 8 + fg // 128, 0:nsl],
                                         x1g[:, fg // 128, 0:nsl], silb[:, 0:nsl])
                m_gr = tot("chv")

                @blk.gpsimd
                def _(g):
                    wait(g, "mm_r1", mmR + 16)
                    for dq in range(4):
                        if dq >= 2:
                            wait(g, "mm_r2", mmR2 + (nsl // 128) * (dq - 1))
                        inc(f"dma_r2{dq % 2}", g.dma_start(
                            wstr5[:, dq % 2, 0:8, :],
                            w2r_d.ap().rearrange("(fc p) d -> p fc d", p=128)
                            [:, :, dq * 512:(dq + 1) * 512]), 16)

                @blk.tensor
                def _(tn):
                    tn.wait_ge(sem("chv"), m_gr)
                    scR2 = scR + 16
                    for dq in range(4):
                        tn.wait_ge(sem(f"dma_r2{dq % 2}"), 16 * (2 * c + dq // 2 + 1))
                        for sc_ in range(nsl // 128):
                            g_ = dq * (nsl // 128) + sc_
                            if g_ >= 2:
                                tn.wait_ge(sem("chs"), scR2 + g_ - 1)
                            for fc in range(8):
                                ins = tn.matmul(pbank[4 + g_ % 2][:],
                                                x1g[:, 8 + fc, sc_ * 128:(sc_ + 1) * 128],
                                                wstr5[:, dq % 2, fc, :],
                                                start=(fc == 0), stop=(fc == 7))
                            inc("mm_r2", ins)

                @blk.scalar
                def _(sc):
                    sc = P.chain(sc, "chs")
                    for dq in range(4):
                        for sc_ in range(nsl // 128):
                            g_ = dq * (nsl // 128) + sc_
                            if g_ == 0 and c > 0:
                                sc.wait_ge(sem("dma_ro"), 16 * c)
                            sc.wait_ge(sem("mm_r2"), mmR2 + g_ + 1)
                            sc.copy(routsb[:, sc_, dq * 512:(dq + 1) * 512],
                                    pbank[4 + g_ % 2][:])
                m_rout = tot("chs")

                @blk.gpsimd
                def _(g):
                    g.wait_ge(sem("chs"), m_rout)
                    inc("dma_ro", g.dma_start(
                        routin.ap()[c * 512:c * 512 + nsl, :]
                        .rearrange("(sc p) d -> p sc d", p=128),
                        routsb[:, 0:nsl // 128, :]), 16)

        with nc.Block() as blk:
            @blk.gpsimd
            def _(g):
                wait(g, "dma_ro")
                inc("cc_ro", g.collective_compute(
                    "AllGather", ALU.bypass, replica_groups=[list(range(8))],
                    ins=[routin.ap().opt()], outs=[routall.ap().opt()]))

    pst5.close()
    with contextlib.ExitStack() as pst:
        olrep = pst.enter_context(nc.sbuf_tensor("olrep", [128, 128], I16))
        xf = [pst.enter_context(nc.sbuf_tensor(f"xf{i}", [128, D], F32)) for i in range(2)]
        of = [pst.enter_context(nc.sbuf_tensor(f"of{i}", [128, D], F32)) for i in range(2)]
        t1c = pst.enter_context(nc.sbuf_tensor("t1c", [128, D], F32))
        gown = bigD.ap().rearrange("p a t -> p (a t)").rearrange("p (b d) -> p b d", b=8)

        with nc.Block() as blk:
            veC = tot("chv")
            d_ro = tot("dma_ro")

            @blk.gpsimd
            def _(g):
                wait(g, "cc_ro")
                for rep in range(8):
                    inc("dma_or", g.dma_start(olrep[rep * 16:(rep + 1) * 16, :], olist[:]), 16)
                wait(g, "dma_or")
                for j2 in range(2):
                    inc("gd_o", g.dma_gather(
                        gown[:, 4 * j2:4 * j2 + 4, :], routall.ap(),
                        olrep[:, 32 * (2 * j2):32 * (2 * j2) + 32],
                        512, 512, D), 16)
                for tc in range(4):
                    if tc >= 2:
                        g.wait_ge(sem("chv"), veC + 5 * (tc - 1))
                    inc(f"dma_xf{tc % 2}", g.dma_start(
                        xf[tc % 2][:], x2spill.ap()[tc * 128:(tc + 1) * 128, :]), 16)
                g.wait_ge(sem("chv"), veC + 5 * 4)
                for j2 in range(2):
                    inc("gd_o", g.dma_gather(
                        gown[:, 4 * j2:4 * j2 + 4, :], routall.ap(),
                        olrep[:, 32 * (2 * j2 + 1):32 * (2 * j2 + 1) + 32],
                        512, 512, D), 16)
                for tc in range(4, 8):
                    g.wait_ge(sem("chv"), veC + 5 * (tc - 1))
                    inc(f"dma_xf{tc % 2}", g.dma_start(
                        xf[tc % 2][:], x2spill.ap()[tc * 128:(tc + 1) * 128, :]), 16)

            @blk.vector
            def _(v):
                v = P.chain(v, "chv")
                for tc in range(8):
                    half, off = divmod(tc, 4)
                    v.wait_ge(sem("gd_o"), 32 * (half + 1))
                    v.wait_ge(sem(f"dma_xf{tc % 2}"), 16 * (tc // 2 + 1))
                    if tc >= 2:
                        v.wait_ge(sem(f"dma_out{tc % 2}"), 16 * (tc // 2))
                    g1 = gown[:, off, :]
                    g2 = gown[:, 4 + off, :]
                    v.tensor_scalar(t1c[:], g1, wownb[:, tc:tc + 1], None, op0=ALU.mult)
                    v.scalar_tensor_tensor(t1c[:], g2, wownb[:, 8 + tc:9 + tc], t1c[:],
                                           op0=ALU.mult, op1=ALU.add)
                    v.tensor_add(t1c[:], t1c[:], shared_tm[:, tc, :])
                    v.tensor_mul(t1c[:], t1c[:], gag[:, D:2 * D])
                    v.tensor_add(of[tc % 2][:], xf[tc % 2][:], t1c[:])
                    assert tot("chv") == veC + 5 * (tc + 1)

            @blk.sync
            def _(s2):
                for tc in range(8):
                    s2.wait_ge(sem("chv"), veC + 5 * (tc + 1))
                    inc(f"dma_out{tc % 2}", s2.dma_start(out_d.ap()[tc * 128:(tc + 1) * 128, :],
                                                         of[tc % 2][:]), 16)
                wait(s2, "dma_out0")
                wait(s2, "dma_out1")

    st.close()
    return nc, dbg


def rope_tables(qi):
    pos = np.arange(qi * 1024, (qi + 1) * 1024, dtype=np.float64)
    invf = (10000.0 ** (-np.arange(0, 64, dtype=np.float64) / 64.0))
    ang = pos[None, :] * np.concatenate([invf, invf])[:, None]
    return np.cos(ang), np.sin(ang)


def make_inputs(ins, r):
    b, qi = r // 4, r % 4
    bf = lambda a: np.ascontiguousarray(a).astype(ml_dtypes.bfloat16)
    f32 = lambda a: np.ascontiguousarray(a, dtype=np.float32)
    x_full = np.asarray(ins["L0_attn_x"], np.float32).reshape(8192, 2048)
    cos, sin = rope_tables(qi)
    sc = 1.0 / np.sqrt(128.0)
    R = np.zeros((128, 128), np.float32)
    for i in range(64):
        R[i, i + 64] = -1.0
        R[i + 64, i] = 1.0
    gmap = np.zeros((16, 512), np.float32)
    for a in range(16):
        for col in range(512):
            gmap[a, col] = 1024 * (col // 64) + 64 * a + (col % 64) + 1.0
    m = {
        "x": f32(x_full[r * 1024:(r + 1) * 1024]),
        "wqkvT": f32(np.asarray(ins["L0_attn_w_qkv"]).T),
        "woutT": f32(np.asarray(ins["L0_attn_w_out"]).T),
        "gatew": f32(np.asarray(ins["L0_ffn_gate_w"])),
        "ebias": f32(np.asarray(ins["L0_ffn_expert_bias"]).reshape(E, 1)),
        "w1s": f32(np.asarray(ins["L0_ffn_shared_w1"])[0]),
        "w2s": f32(np.asarray(ins["L0_ffn_shared_w2"])[0]),
        "w1r": f32(np.asarray(ins["L0_ffn_routed_w1"])[min(r, E - 1)]),
        "w2r": f32(np.asarray(ins["L0_ffn_routed_w2"])[min(r, E - 1)]),
        "cc": f32(np.asarray(ins["L0_c"])),
        "wmod": f32(np.asarray(ins["L0_wmod"])[:, r * 1536:(r + 1) * 1536]),
        "identb": bf(np.eye(128)),
        "identf": f32(np.eye(128)),
        "rotT": bf(R.T),
        "cosq": bf(cos * sc), "sinq": bf(sin * sc),
        "cosk": bf(cos), "sink": bf(sin),
        "u16s": bf(np.triu(np.ones((16, 16)), 1)),
        "gmapp1": gmap,
        "irev": f32(np.tile(np.arange(E - 1, -1, -1, dtype=np.float32)[None, :] + 1.0, (128, 1))),
        "bsel": f32(np.tile([[1.0, 0.0]] if b == 0 else [[0.0, 1.0]], (6, 1))),
        "rowm": f32(np.array([0, 0, 1, 0, 0, 1], np.float32).reshape(6, 1)),
        "eoh": f32(np.tile(np.eye(E)[min(r, E - 1)][None, :], (16, 1))),
        "onesr_b": bf(np.ones((1, 128))),
        "onesr_f": f32(np.ones((1, 128))),
        "onescol_b": bf(np.ones((128, 1))),
        "ones2_b": bf(np.ones((128, 2))),
        "bisinit": np.tile(np.array([[0, 0x40000000]], np.int32), (16, 1)),
        "ones16": f32(np.ones((16, 16))),
    }
    return m


def kernel(**inputs):
    nc, _ = build(5)
    nc.compile()
    in_maps = [make_inputs(inputs, r) for r in range(8)]
    res = run_bass_kernel_spmd(nc, in_maps, core_ids=list(range(8)))
    out = np.concatenate([np.asarray(res.results[r]["out"], np.float32)
                          for r in range(8)], axis=0)
    return out.reshape(2, 4096, 2048)


# version 08 (final)
# speedup vs baseline: 2.6978x; 1.0133x over previous
import contextlib
import numpy as np
import ml_dtypes
import concourse.bacc as bacc
import concourse.bass as bass
import concourse.mybir as mybir
from concourse.bass_utils import run_bass_kernel_spmd
from concourse.library_config import attn as LIB_ATTN, mlp as LIB_MLP, sparse_gather as LIB_SG

F32 = mybir.dt.float32
BF16 = mybir.dt.bfloat16
I16 = mybir.dt.int16
I32 = mybir.dt.int32
U32 = mybir.dt.uint32
AF = mybir.ActivationFunctionType
ALU = mybir.AluOpType
AX = mybir.AxisListType
DRM = mybir.MatmulPerfMode.DoubleRow

D, T, NH, HD, E = 2048, 1024, 16, 128, 7
CAPP = 1280
NSLOT = 8 * CAPP
FF, F2 = 1024, 2048
EPS = 1e-5
CEXP = 10.0



CHAIN_SKIP = {"wait_ge", "then_inc", "dma_start", "dma_start_transpose", "dma_gather",
              "dma_scatter_add", "collective_compute", "indirect_dma_start",
              "load_library", "sem_inc", "nop", "engine_nop", "alloc_register",
              "reg_load", "free_register", "drain", "memset", "memzero"}


class _Chained:

    def __init__(self, P, eng, name):
        self._P, self._eng, self._name = P, eng, name

    def __getattr__(self, k):
        v = getattr(self._eng, k)
        if k in CHAIN_SKIP or not callable(v):
            return v
        P, eng, name = self._P, self._eng, self._name

        def wrap(*a, **kw):
            r = v(*a, **kw)
            n = P.inc(name, r)
            eng.wait_ge(P.sem(name), n)
            return r
        return wrap


class Prog:
    def __init__(self, nc, st):
        self.nc, self.st, self.sems, self.cnt = nc, st, {}, {}
        self.marks = {}

    def sem(self, name):
        if name not in self.sems:
            self.sems[name] = self.st.enter_context(self.nc.semaphore(name))
            self.cnt[name] = 0
        return self.sems[name]

    def inc(self, name, ins, k=1):
        ins.then_inc(self.sem(name), k)
        self.cnt[name] += k
        return self.cnt[name]

    def tot(self, name):
        self.sem(name)
        return self.cnt[name]

    def wait(self, eng, name, val=None):
        eng.wait_ge(self.sem(name), self.tot(name) if val is None else val)

    def chain(self, eng, name):
        return _Chained(self, eng, name)


def build(stage=5):
    nc = bacc.Bacc("TRN2")
    st = contextlib.ExitStack()
    P = Prog(nc, st)
    sem, inc, tot, wait = P.sem, P.inc, P.tot, P.wait

    def din(name, shape, dt=F32):
        return nc.dram_tensor(name, shape, dt, kind="ExternalInput")

    x_d = din("x", [T, D])
    wqkvT_d = din("wqkvT", [D, 3 * D])
    woutT_d = din("woutT", [D, D])
    gatew_d = din("gatew", [D, E])
    ebias_d = din("ebias", [E, 1])
    w1s_d = din("w1s", [D, F2])
    w2s_d = din("w2s", [FF, D])
    w1r_d = din("w1r", [D, F2])
    w2r_d = din("w2r", [FF, D])
    c_d = din("cc", [2, D])
    wmod_d = din("wmod", [D, 1536])
    identb_d = din("identb", [128, 128], BF16)
    identf_d = din("identf", [128, 128], F32)
    rotT_d = din("rotT", [128, 128], BF16)
    cosq_d = din("cosq", [128, T], BF16)
    sinq_d = din("sinq", [128, T], BF16)
    cosk_d = din("cosk", [128, T], BF16)
    sink_d = din("sink", [128, T], BF16)
    u16s_d = din("u16s", [16, 16], BF16)
    gmapp1_d = din("gmapp1", [16, 512], F32)
    irev_d = din("irev", [128, E], F32)
    bsel_d = din("bsel", [6, 2], F32)
    rowm_d = din("rowm", [6, 1], F32)
    eoh_d = din("eoh", [16, E], F32)
    onesr_b_d = din("onesr_b", [1, 128], BF16)
    onesr_f_d = din("onesr_f", [1, 128], F32)
    onescol_b_d = din("onescol_b", [128, 1], BF16)
    ones2_b_d = din("ones2_b", [128, 2], BF16)
    bisinit_d = din("bisinit", [16, 2], I32)
    ones16_d = din("ones16", [16, 16], F32)

    out_d = nc.dram_tensor("out", [T, D], F32, kind="ExternalOutput")

    modin = nc.dram_tensor("modin", [2, 1536], F32)
    modall = nc.dram_tensor("modall", [16, 1536], F32)
    kin0 = nc.dram_tensor("kin0", [D // 2, T], BF16)
    kin1 = nc.dram_tensor("kin1", [D // 2, T], BF16)
    vin = nc.dram_tensor("vin", [T, D], BF16)
    knin = nc.dram_tensor("knin", [1, NH], F32)
    kall0 = nc.dram_tensor("kall0", [2 * D, T], BF16)
    kall1 = nc.dram_tensor("kall1", [2 * D, T], BF16)
    vall = nc.dram_tensor("vall", [4 * T, D], BF16)
    knall = nc.dram_tensor("knall", [4, NH], F32)
    h2in = nc.dram_tensor("h2in", [T, D], BF16)
    affin = nc.dram_tensor("affin", [112, 64], F32)
    h2all = nc.dram_tensor("h2all", [8 * T, D], BF16)
    affall = nc.dram_tensor("affall", [8 * 112, 64], F32)
    routin = nc.dram_tensor("routin", [CAPP, D], BF16)
    routall = nc.dram_tensor("routall", [NSLOT, D], BF16)
    x2spill = nc.dram_tensor("x2spill", [T, D], F32)
    scr128 = nc.dram_tensor("scr128", [128, 64], F32)
    olscr = nc.dram_tensor("olscr", [16, 128], I16)
    thin = nc.dram_tensor("thin", [1, 1], F32)
    thall = nc.dram_tensor("thall", [8, 1], F32)

    dbg = {}

    def dbgadd(name, shape, dt):
        dbg[name] = nc.dram_tensor("dbg_" + name, shape, dt, kind="ExternalOutput")

    if stage == 1:
        dbgadd("h1t", [128, 16, T], BF16)
        dbgadd("vbo", [6, D], F32)
        dbgadd("mparm", [128, 64], F32)
    if stage == 2:
        dbgadd("qf", [128, NH, T], BF16)
        dbgadd("kf", [128, NH, T], BF16)
        dbgadd("vt", [128, 8, D], BF16)
    if stage == 3:
        dbgadd("x2", [T, D], F32)
    if stage == 4:
        dbgadd("aff", [E, T], F32)
        dbgadd("theta", [1, 16], F32)
        dbgadd("sidx", [16, 128], I16)
        dbgadd("wown", [128, 16], F32)
        dbgadd("h2t", [128, 16, T], BF16)

    def sbt(name, shape, dt, stk=None):
        return (stk or st).enter_context(nc.sbuf_tensor(name, shape, dt))

    identb = sbt("identb_s", [128, 128], BF16)
    identf = sbt("identf_s", [128, 128], F32)
    rotT = sbt("rotT_s", [128, 128], BF16)
    u16s = sbt("u16s_s", [16, 16], BF16)
    gmapp1 = sbt("gmapp1_s", [16, 512], F32)
    irev = sbt("irev_s", [128, E], F32)
    bsel = sbt("bsel_s", [6, 2], F32)
    rowm = sbt("rowm_s", [6, 1], F32)
    eoh = sbt("eoh_s", [16, E], F32)
    onesr_b = sbt("onesr_b_s", [1, 128], BF16)
    onesr_f = sbt("onesr_f_s", [1, 128], F32)
    onescol_b = sbt("onescol_b_s", [128, 1], BF16)
    ones2_b = sbt("ones2_b_s", [128, 2], BF16)
    ebias_sb = sbt("ebias_sb", [E, 1], F32)
    mparm = sbt("mparm", [128, 64], F32)
    gag = sbt("gag", [128, 2 * D], BF16)
    cbb = sbt("cbb", [128, D], BF16)
    sbb = sbt("sbb", [128, D], BF16)
    mbias = sbt("mbias", [128, NH], F32)
    thtm = sbt("thtm", [128, E], F32)
    wownb = sbt("wownb", [128, 16], F32)
    stats = sbt("stats", [128, 16], F32)
    nf_sb = sbt("nf_sb", [1, 1], U32)
    sq_rt1 = sbt("sq_rt1", [128, E], F32)
    sq_rt2 = sbt("sq_rt2", [128, E], F32)
    sq_rt3 = sbt("sq_rt3", [128, E], F32)
    manyi = sbt("manyi", [128, 1], I16)
    kn = sbt("kn", [1, 64], F32)

    pbank = [st.enter_context(nc.psum_tensor(f"pb{i}", [128, 512], F32)) for i in range(8)]

    def pbank_bf(i):
        return pbank[i].ap().bitcast(BF16)

    with nc.Block() as blk:
        @blk.sync
        def _(s):
            for t_, d_ in ((identb, identb_d), (identf, identf_d), (rotT, rotT_d),
                           (u16s, u16s_d), (gmapp1, gmapp1_d), (irev, irev_d),
                           (bsel, bsel_d), (rowm, rowm_d), (eoh, eoh_d), (onesr_b, onesr_b_d),
                           (onesr_f, onesr_f_d), (onescol_b, onescol_b_d),
                           (ones2_b, ones2_b_d), (ebias_sb, ebias_d)):
                inc("dma_s", s.dma_start(t_[:], d_[:]), 16)

    with contextlib.ExitStack() as pst:
        vb = sbt("vb", [6, D], F32, pst)
        csb = sbt("csb", [2, D], F32, pst)
        csil = sbt("csil", [2, D], F32, pst)
        ct = sbt("ct", [128, 32], F32, pst)
        wmv = sbt("wmv", [128, 16, 1536], F32, pst)
        modv = sbt("modv", [2, 1536], F32, pst)
        vrow = sbt("vrow", [1, D], F32, pst)
        garow = sbt("garow", [1, D], BF16, pst)
        gbrow = sbt("gbrow", [1, D], BF16, pst)
        cbrow = sbt("cbrow", [1, D], BF16, pst)
        sbrow = sbt("sbrow", [1, D], BF16, pst)
        mv = sbt("mv", [6, 2 * D], F32, pst)

        with nc.Block() as blk:
            @blk.sync
            def _(s):
                inc("dma_c", s.dma_start(csb[:], c_d[:]), 16)

            @blk.gpsimd
            def _(g):
                inc("dma_w", g.dma_start(
                    wmv.ap(),
                    wmod_d.ap().rearrange("(dc p) j -> p dc j", p=128)), 16)

            @blk.scalar
            def _(sc):
                sc = P.chain(sc, "chs")
                wait(sc, "dma_c")
                sc.activation(csil[:], csb[:], AF.Silu)
            m_sil = tot("chs")

            @blk.tensor
            def _(tn):
                wait(tn, "dma_s")
                tn.wait_ge(sem("chs"), m_sil)
                for dc in range(16):
                    ins = tn.transpose(pbank[5][:, 2 * dc:2 * dc + 2],
                                       csil[:, dc * 128:(dc + 1) * 128], identf[0:2, 0:2])
                inc("mm_s", ins)

            @blk.vector
            def _(v):
                v = P.chain(v, "chv")
                wait(v, "mm_s")
                v.tensor_copy(ct[:], pbank[5][:, 0:32])
            m_ct = tot("chv")
            mm0 = tot("mm_s")

            @blk.tensor
            def _(tn):
                tn.wait_ge(sem("chv"), m_ct)
                wait(tn, "dma_w")
                for jg in range(3):
                    if jg >= 2:
                        tn.wait_ge(sem("chv"), m_ct + jg - 1)
                    for dc in range(16):
                        ins = tn.matmul(pbank[jg % 2][0:2, :], ct[:, 2 * dc:2 * dc + 2],
                                        wmv[:, dc, jg * 512:(jg + 1) * 512],
                                        start=(dc == 0), stop=(dc == 15))
                    inc("mm_s", ins)

            @blk.vector
            def _(v):
                v = P.chain(v, "chv")
                for jg in range(3):
                    wait(v, "mm_s", mm0 + jg + 1)
                    v.tensor_copy(modv[:, jg * 512:(jg + 1) * 512], pbank[jg % 2][0:2, :])
            m_modv = tot("chv")

            @blk.gpsimd
            def _(g):
                g.wait_ge(sem("chv"), m_modv)
                inc("dma_g", g.dma_start(modin[:], modv[:]), 16)
                wait(g, "dma_g")
                inc("cc_s", g.collective_compute(
                    "AllGather", ALU.bypass, replica_groups=[list(range(8))],
                    ins=[modin.ap().opt()], outs=[modall.ap().opt()]))

        with nc.Block() as blk:
            @blk.gpsimd
            def _(g):
                wait(g, "cc_s")
                for v_ in range(6):
                    for b_ in range(2):
                        J0, done = 2048 * v_, 0
                        while done < 2048:
                            rk, c0 = J0 // 1536, J0 % 1536
                            n = min(1536 - c0, 2048 - done)
                            inc("dma_g", g.dma_start(
                                mv[v_:v_ + 1, b_ * D + done:b_ * D + done + n],
                                modall[2 * rk + b_:2 * rk + b_ + 1, c0:c0 + n]), 16)
                            J0 += n
                            done += n

            @blk.vector
            def _(v):
                v = P.chain(v, "chv")
                wait(v, "dma_g")
                wait(v, "dma_s")
                v.tensor_scalar(vb[:], mv[:, 0:D], bsel[:, 0:1], None, op0=ALU.mult)
                v.scalar_tensor_tensor(vb[:], mv[:, D:2 * D], bsel[:, 1:2], vb[:],
                                       op0=ALU.mult, op1=ALU.add)
                sgn = mv[:, 0:D]
                ab = mv[:, D:2 * D]
                v.tensor_single_scalar(sgn, vb[:], 0.0, op=ALU.is_ge)
                v.tensor_scalar(sgn, sgn, 2.0, -1.0, op0=ALU.mult, op1=ALU.add)
                v.tensor_scalar(ab, vb[:], -1.0, None, op0=ALU.mult)
                v.tensor_max(ab, ab, vb[:])
                v.tensor_scalar_max(ab, ab, 0.25)
                v.tensor_mul(sgn, sgn, ab)
                v.tensor_sub(sgn, sgn, vb[:])
                v.scalar_tensor_tensor(vb[:], sgn, rowm[:, 0:1], vb[:],
                                       op0=ALU.mult, op1=ALU.add)
            m_vb = tot("chv")

            @blk.gpsimd
            def _(g):
                g.wait_ge(sem("chv"), m_vb)
                inc("dma_gr", g.dma_start(garow[:], vb[2:3, :]), 16)
                inc("dma_gr", g.dma_start(gbrow[:], vb[5:6, :]), 16)
                inc("dma_gr", g.dma_start(cbrow[:], vb[4:5, :]), 16)
                inc("dma_gr", g.dma_start(sbrow[:], vb[3:4, :]), 16)
                for i, row in enumerate((1, 0, 4, 3)):
                    if i > 0:
                        wait(g, "mm_vr", i)
                    inc("dma_vr", g.dma_start(vrow[:], vb[row:row + 1, :]), 16)

            @blk.tensor
            def _(tn):
                for i in range(4):
                    tn.wait_ge(sem("dma_vr"), 16 * (i + 1))
                    for dc in range(16):
                        ins = tn.transpose(pbank[2][:, i * 16 + dc:i * 16 + dc + 1],
                                           vrow[:, dc * 128:(dc + 1) * 128],
                                           identf[0:1, 0:1])
                    inc("mm_vr", ins)
                wait(tn, "dma_gr")
                for kgrp in range(16):
                    i, jg = divmod(kgrp, 4)
                    if kgrp >= 2:
                        tn.wait_ge(sem("chv"), m_vb + 3 + kgrp - 1)
                    row = (garow, gbrow, cbrow, sbrow)[i]
                    inc("mm_gag", tn.matmul(pbank[3 + kgrp % 2][:],
                                            onesr_b[:],
                                            row[:, jg * 512:(jg + 1) * 512],
                                            start=True, stop=True))

            @blk.vector
            def _(v):
                v = P.chain(v, "chv")
                wait(v, "mm_vr")
                v.tensor_copy(mparm[:], pbank[2][:, 0:64])
                v.tensor_scalar_add(mparm[:, 0:16], mparm[:, 0:16], 1.0)
                v.tensor_scalar_add(mparm[:, 32:48], mparm[:, 32:48], 1.0)
                assert tot("chv") == m_vb + 3
                for kgrp in range(16):
                    i, jg = divmod(kgrp, 4)
                    wait(v, "mm_gag", kgrp + 1)
                    dst = (gag[:, jg * 512:(jg + 1) * 512],
                           gag[:, D + jg * 512:D + (jg + 1) * 512],
                           cbb[:, jg * 512:(jg + 1) * 512],
                           sbb[:, jg * 512:(jg + 1) * 512])[i]
                    v.tensor_copy(dst, pbank[3 + kgrp % 2][:])
                v.tensor_scalar_add(cbb[:], cbb[:], 1.0)

        if stage == 1:
            with nc.Block() as blk:
                @blk.sync
                def _(s):
                    s.wait_ge(sem("chv"), tot("chv"))
                    inc("dma_o", s.dma_start(dbg["vbo"].ap(), vb.ap()), 16)
                    wait(s, "dma_o")

    bigA = sbt("bigA", [128, NH, T], BF16)
    bigB = sbt("bigB", [128, NH, T], BF16)
    bigC = sbt("bigC", [128, NH, T], BF16)
    bigD = sbt("bigD", [128, NH, T], BF16)

    h1t, q_fm, k_fm = bigA, bigB, bigC
    v_tm = bigD.ap().rearrange("p a t -> p (a t)").rearrange("p (tc o) -> p tc o", tc=8)

    with contextlib.ExitStack() as pst:
        xt = [sbt(f"xt{i}", [128, D], F32, pst) for i in range(2)]
        xnt = [sbt(f"xnt{i}", [128, D], BF16, pst) for i in range(2)]
        sqt = sbt("sqt", [128, D], F32, pst)
        veL0 = tot("chv")
        scL0 = tot("chs")

        def ln_xn(tc):
            return veL0 + 26 * tc + 10

        def ln_mod(tc):
            return veL0 + 26 * (tc + 1)

        with nc.Block() as blk:
            @blk.sync
            def _(s):
                for tc in range(8):
                    if tc >= 2:
                        s.wait_ge(sem("chv"), ln_xn(tc - 2))
                    inc(f"dma_x{tc % 2}", s.dma_start(
                        xt[tc % 2][:], x_d[tc * 128:(tc + 1) * 128, :]), 16)

            @blk.vector
            def _(v):
                v = P.chain(v, "chv")
                for tc in range(8):
                    v.wait_ge(sem(f"dma_x{tc % 2}"), 16 * (tc // 2 + 1))
                    x_ = xt[tc % 2]
                    mean, msq, var, rin = (stats[:, 4:5], stats[:, 5:6],
                                           stats[:, 6:7], stats[:, 7:8])
                    v.reduce_sum(mean, x_[:], axis=AX.X)
                    v.tensor_scalar_mul(mean, mean, 1.0 / D)
                    v.tensor_mul(sqt[:], x_[:], x_[:])
                    v.reduce_sum(msq, sqt[:], axis=AX.X)
                    v.tensor_scalar_mul(msq, msq, 1.0 / D)
                    v.tensor_mul(var, mean, mean)
                    v.tensor_sub(var, msq, var)
                    v.tensor_scalar_add(var, var, EPS)
                    v.reciprocal(rin, var)
                    v.wait_ge(sem("chs"), scL0 + tc + 1)
                    v.scalar_tensor_tensor(xnt[tc % 2][:], x_[:], mean,
                                           stats[:, 8:9].to_broadcast([128, D]),
                                           op0=ALU.subtract, op1=ALU.mult)
                    assert tot("chv") == veL0 + 26 * tc + 10
                    v.wait_ge(sem("mm_ln"), tc + 1)
                    for dc in range(16):
                        bk = (tc % 2) * 2 + dc // 8
                        v.scalar_tensor_tensor(
                            h1t[:, dc, tc * 128:(tc + 1) * 128],
                            pbank_bf(bk)[:, (dc % 8) * 128:(dc % 8 + 1) * 128],
                            mparm[:, dc:dc + 1],
                            mparm[:, 16 + dc:17 + dc].to_broadcast([128, 128]),
                            op0=ALU.mult, op1=ALU.add)
                    assert tot("chv") == ln_mod(tc)

            @blk.scalar
            def _(sc):
                sc = P.chain(sc, "chs")
                for tc in range(8):
                    sc.wait_ge(sem("chv"), veL0 + 26 * tc + 9)
                    sc.activation(stats[:, 8:9], stats[:, 7:8], AF.Sqrt)

            @blk.tensor
            def _(tn):
                for tc in range(8):
                    tn.wait_ge(sem("chv"), ln_xn(tc))
                    if tc >= 2:
                        tn.wait_ge(sem("chv"), ln_mod(tc - 2))
                    for dc in range(16):
                        bk = (tc % 2) * 2 + dc // 8
                        ins = tn.transpose(
                            pbank_bf(bk)[:, (dc % 8) * 128:(dc % 8 + 1) * 128],
                            xnt[tc % 2][:, dc * 128:(dc + 1) * 128], identb[:])
                    inc("mm_ln", ins)

    m_h1 = tot("chv")

    if stage == 1:
        with nc.Block() as blk:
            @blk.sync
            def _(s):
                s.wait_ge(sem("chv"), m_h1)
                inc("dma_o", s.dma_start(dbg["h1t"].ap(), h1t.ap()), 16)
                inc("dma_o", s.dma_start(dbg["mparm"].ap(), mparm.ap()), 16)
                wait(s, "dma_o")
        st.close()
        return nc, dbg

    with contextlib.ExitStack() as pst:
        wstr = sbt("wstr", [128, 2, 16, 512], BF16, pst)
        cosq = sbt("cosq_s", [128, T], BF16, pst)
        sinq = sbt("sinq_s", [128, T], BF16, pst)
        cosk = sbt("cosk_s", [128, T], BF16, pst)
        sink = sbt("sink_s", [128, T], BF16, pst)
        rot_sb = sbt("rot_sb", [128, T], BF16, pst)
        tmp1 = sbt("tmp1", [128, T], BF16, pst)
        tmp2 = sbt("tmp2", [128, T], BF16, pst)

        scB = tot("chs")
        h1_ready = m_h1

        with nc.Block() as blk:
            @blk.sync
            def _(s):
                for t_, d_ in ((cosq, cosq_d), (sinq, sinq_d), (cosk, cosk_d), (sink, sink_d)):
                    inc("dma_rt", s.dma_start(t_[:], d_[:]), 16)

            @blk.gpsimd
            def _(g):
                for og in range(12):
                    if og >= 2:
                        wait(g, "mm_qkv", 8 * (og - 1))
                    inc(f"dma_q{og % 2}", g.dma_start(
                        wstr[:, og % 2, :, :],
                        wqkvT_d.ap().rearrange("(dc p) o -> p dc o", p=128)
                        [:, :, og * 512:(og + 1) * 512]), 16)

            @blk.tensor
            def _(tn):
                tn.wait_ge(sem("chv"), h1_ready)
                for og in range(12):
                    tn.wait_ge(sem(f"dma_q{og % 2}"), 16 * (og // 2 + 1))
                    for k in range(8):
                        if og >= 1:
                            tn.wait_ge(sem("chs"), scB + 8 * (og - 1) + k + 1)
                        if og < 8:
                            oc, th = divmod(k, 2)
                            for dc in range(16):
                                ins = tn.matmul(
                                    pbank[k][:],
                                    wstr[:, og % 2, dc, oc * 128:(oc + 1) * 128],
                                    h1t[:, dc, th * 512:(th + 1) * 512],
                                    start=(dc == 0), stop=(dc == 15))
                        else:
                            tc = k
                            for dc in range(16):
                                ins = tn.matmul(
                                    pbank[k][:],
                                    h1t[:, dc, tc * 128:(tc + 1) * 128],
                                    wstr[:, og % 2, dc, :],
                                    start=(dc == 0), stop=(dc == 15))
                        inc("mm_qkv", ins)

            @blk.scalar
            def _(sc):
                sc = P.chain(sc, "chs")
                for og in range(12):
                    for k in range(8):
                        sc.wait_ge(sem("mm_qkv"), 8 * og + k + 1)
                        if og < 8:
                            oc, th = divmod(k, 2)
                            head = (og % 4) * 4 + oc
                            dest = (q_fm if og < 4 else k_fm)[:, head, th * 512:(th + 1) * 512]
                        else:
                            dest = v_tm[:, k, (og - 8) * 512:(og - 7) * 512]
                        sc.copy(dest, pbank[k][:])
            assert tot("chs") == scB + 96

        with nc.Block() as blk:
            veR = tot("chv")
            scR = tot("chs")

            @blk.tensor
            def _(tn):
                wait(tn, "dma_rt")
                for hh in range(32):
                    tgt = q_fm if hh < 16 else k_fm
                    h = hh % 16
                    for th in range(2):
                        bk = (hh % 2) * 2 + th
                        if hh >= 2:
                            tn.wait_ge(sem("chs"), scR + 2 * (hh - 1))
                        ins = tn.matmul(pbank[bk][:], rotT[:],
                                        tgt[:, h, th * 512:(th + 1) * 512],
                                        start=True, stop=True)
                    inc("mm_rope", ins)

            @blk.scalar
            def _(sc):
                sc = P.chain(sc, "chs")
                for hh in range(32):
                    sc.wait_ge(sem("mm_rope"), hh + 1)
                    if hh >= 1:
                        sc.wait_ge(sem("chv"), veR + 3 * hh)
                    for th in range(2):
                        bk = (hh % 2) * 2 + th
                        sc.copy(rot_sb[:, th * 512:(th + 1) * 512], pbank[bk][:])

            @blk.vector
            def _(v):
                v = P.chain(v, "chv")
                for hh in range(32):
                    tgt = q_fm if hh < 16 else k_fm
                    co = cosq if hh < 16 else cosk
                    si = sinq if hh < 16 else sink
                    h = hh % 16
                    v.wait_ge(sem("chs"), scR + 2 * (hh + 1))
                    v.tensor_mul(tmp1[:], tgt[:, h, :], co[:])
                    v.tensor_mul(tmp2[:], rot_sb[:], si[:])
                    v.tensor_add(tgt[:, h, :], tmp1[:], tmp2[:])
                    assert tot("chv") == veR + 3 * (hh + 1)

        m_rope_v = tot("chv")
        m_qkv_sc = tot("chs")
        veR3 = m_rope_v - 96

    if stage == 2:
        with nc.Block() as blk:
            @blk.sync
            def _(s):
                s.wait_ge(sem("chv"), m_rope_v)
                s.wait_ge(sem("chs"), m_qkv_sc)
                inc("dma_o", s.dma_start(dbg["qf"].ap(), q_fm.ap()), 16)
                inc("dma_o", s.dma_start(dbg["kf"].ap(), k_fm.ap()), 16)
                inc("dma_o", s.dma_start(dbg["vt"].ap(), bigD.ap()), 16)
                wait(s, "dma_o")
        st.close()
        return nc, dbg

    with contextlib.ExitStack() as pst:
        sq_at = sbt("sq_at", [128, T], BF16, pst)
        P_sb2 = sbt("P_sb2", [128, 4, 512], BF16, pst)
        lin_sb = sbt("lin_sb", [1, 512], F32, pst)
        linb_sb = sbt("linb_sb", [128, 512], F32, pst)
        kn4 = sbt("kn4", [1, 64], F32, pst)

        attn_fm = bigA

        with nc.Block() as blk:
            scN = tot("chs")
            veN = tot("chv")

            @blk.scalar
            def _(sc):
                sc = P.chain(sc, "chs")
                sc.wait_ge(sem("chv"), m_rope_v)
                for hh in range(32):
                    if hh >= 1:
                        sc.wait_ge(sem("mm_nrm"), 2 * hh)
                    src = (k_fm if hh < 16 else q_fm)[:, hh % 16, :]
                    sc.activation(sq_at[:], src, AF.Square)

            @blk.tensor
            def _(tn):
                for hh in range(32):
                    tn.wait_ge(sem("chs"), scN + hh + 1)
                    if hh >= 1:
                        tn.wait_ge(sem("chv"), veN + hh)
                    for th in range(2):
                        ins = tn.matmul(pbank[6 + hh % 2][0:1, :], onescol_b[:],
                                        sq_at[:, th * 512:(th + 1) * 512],
                                        start=(th == 0), stop=(th == 1))
                    inc("mm_nrm", ins, 2)

            @blk.vector
            def _(v):
                v = P.chain(v, "chv")
                for hh in range(32):
                    v.wait_ge(sem("mm_nrm"), 2 * (hh + 1))
                    col = hh if hh < 16 else 16 + (hh - 16)
                    v.reduce_max(kn[0:1, col:col + 1], pbank[6 + hh % 2][0:1, :], axis=AX.X)
            m_nrm = tot("chv")

            @blk.gpsimd
            def _(g):
                grp = [[0, 1, 2, 3], [4, 5, 6, 7]]
                g.wait_ge(sem("chs"), m_qkv_sc)
                inc("dma_kv", g.dma_start(
                    vin.ap().rearrange("(tc p) o -> p tc o", p=128), v_tm), 16)
                g.wait_ge(sem("dma_kv"), 16)
                inc("cc_kv", g.collective_compute(
                    "AllGather", ALU.bypass, replica_groups=grp,
                    ins=[vin.ap().opt()], outs=[vall.ap().opt()]))
                g.wait_ge(sem("chv"), veR3 + 3 * 24)
                inc("dma_kv", g.dma_start(
                    kin0.ap().rearrange("(dc p) t -> p dc t", p=128),
                    k_fm.ap()[:, 0:8, :]), 16)
                g.wait_ge(sem("dma_kv"), 32)
                inc("cc_kv", g.collective_compute(
                    "AllGather", ALU.bypass, replica_groups=grp,
                    ins=[kin0.ap().opt()], outs=[kall0.ap().opt()]))
                g.wait_ge(sem("chv"), veR3 + 3 * 32)
                inc("dma_kv", g.dma_start(
                    kin1.ap().rearrange("(dc p) t -> p dc t", p=128),
                    k_fm.ap()[:, 8:16, :]), 16)
                g.wait_ge(sem("dma_kv"), 48)
                inc("cc_kv", g.collective_compute(
                    "AllGather", ALU.bypass, replica_groups=grp,
                    ins=[kin1.ap().opt()], outs=[kall1.ap().opt()]))
                g.wait_ge(sem("chv"), m_nrm)
                inc("dma_kv", g.dma_start(knin.ap(), kn[0:1, 0:16]), 16)
                wait(g, "dma_kv")
                inc("cc_kv", g.collective_compute(
                    "AllGather", ALU.bypass, replica_groups=grp,
                    ins=[knin.ap().opt()], outs=[knall.ap().opt()]))
                wait(g, "cc_kv")
                inc("dma_kn", g.dma_start(kn4[:], knall.ap().rearrange("r h -> (r h)").unsqueeze(0)), 16)

            @blk.vector
            def _(v):
                v = P.chain(v, "chv")
                wait(v, "dma_kn")
                v.reduce_max(kn[0:1, 32:48],
                             kn4.ap().rearrange("a (r h) -> a h r", r=4), axis=AX.X)
                v.tensor_mul(kn[0:1, 48:64], kn[0:1, 32:48], kn[0:1, 16:32])
            m_prod = tot("chv")

            @blk.scalar
            def _(sc):
                sc = P.chain(sc, "chs")
                sc.wait_ge(sem("chv"), m_prod)
                sc.activation(kn[0:1, 48:64], kn[0:1, 48:64], AF.Sqrt)
            m_sqrtM = tot("chs")

            @blk.tensor
            def _(tn):
                tn.wait_ge(sem("chs"), m_sqrtM)
                inc("mm_mb", tn.matmul(pbank[6][:, 0:16], onesr_f[:], kn[0:1, 48:64],
                                       start=True, stop=True))

            @blk.vector
            def _(v):
                v = P.chain(v, "chv")
                wait(v, "mm_mb")
                v.tensor_scalar_mul(mbias[:], pbank[6][:, 0:16], -1.0 / CEXP)
            m_mbias = tot("chv")

        kvb = bigC.ap().rearrange("p a t -> p (a t)")
        lnv = pst.enter_context(nc.sbuf_tensor("lnv", [1, 512], F32))
        negmh = pst.enter_context(nc.sbuf_tensor("negmh", [1, 512], BF16))
        negmhf = pst.enter_context(nc.sbuf_tensor("negmhf", [1, 512], F32))
        negmb = pst.enter_context(nc.sbuf_tensor("negmb", [128, 512], F32))
        with nc.Block() as blk:
            attB = tot("mm_att") if "mm_att" in P.cnt else 0
            scA = tot("chs")
            veA = tot("chv")

            @blk.sync
            def _(s_):
                s_.wait_ge(sem("chv"), m_mbias)
                for h in range(16):
                    if h >= 2:
                        s_.wait_ge(sem("mm_att"), attB + 258 * (h - 1))
                    par = h % 2
                    kdst = kvb[:, par * 8192: par * 8192 + 4096].rearrange("p (r t) -> p r t", r=4)
                    kallh = kall0 if h < 8 else kall1
                    ksrc = kallh.ap().rearrange("(r row) t -> row r t", r=4)[
                        (h % 8) * 128:(h % 8 + 1) * 128, :, :]
                    inc(f"dma_K{par}", s_.dma_start(kdst, ksrc), 16)
                    vdst = kvb[:, par * 8192 + 4096: par * 8192 + 8192].rearrange(
                        "p (r j o) -> p r j o", r=4, j=8)
                    for r_ in range(4):
                        vsrc = vall.ap().rearrange("(r j p) o -> p r j o", r=4, j=8)[
                            :, r_:r_ + 1, :, h * 128:(h + 1) * 128]
                        inc(f"dma_V{par}", s_.dma_start(vdst[:, r_:r_ + 1, :, :], vsrc), 16)

            @blk.tensor
            def _(tn):
                for h in range(16):
                    par = h % 2
                    tn.wait_ge(sem(f"dma_K{par}"), 16 * (h // 2 + 1))
                    tn.wait_ge(sem(f"dma_V{par}"), 64 * (h // 2 + 1))
                    kslc = kvb[:, par * 8192: par * 8192 + 4096]
                    vslc = kvb[:, par * 8192 + 4096: par * 8192 + 8192]
                    for qh in range(2):
                        hq = h * 2 + qh
                        base = attB + 129 * hq
                        csc = scA + 65 * hq
                        def emit_S(kt):
                            if kt >= 2:
                                tn.wait_ge(sem("chs"), csc + kt - 1)
                            elif hq >= 1:
                                tn.wait_ge(sem("chs"), scA + 65 * (hq - 1) + 64 + kt)
                            ins = tn.matmul(pbank[kt % 2][:],
                                            kslc[:, kt * 128:(kt + 1) * 128],
                                            q_fm[:, h, qh * 512:(qh + 1) * 512],
                                            start=True, stop=True)
                            inc("mm_att", ins)

                        def emit_l16(kt):
                            tn.wait_ge(sem("chs"), csc + kt + 1)
                            if kt == 0 and hq >= 2:
                                tn.wait_ge(sem("chv"), veA + 39 * (hq - 2) + 37)
                            ins = tn.matmul(pbank[4 + qh][0:1, :], onescol_b[:],
                                            P_sb2[:, kt % 4, :],
                                            start=(kt == 0), stop=(kt == 31))
                            inc("mm_att", ins)

                        emit_S(0)
                        for kt in range(31):
                            emit_S(kt + 1)
                            emit_l16(kt)
                        emit_l16(31)

                        tn.wait_ge(sem("chv"), veA + 39 * hq + 3)
                        if hq >= 2:
                            tn.wait_ge(sem("chv"), veA + 39 * (hq - 2) + 38)
                        inc("mm_nb", tn.matmul(pbank[6 + qh][:], onesr_f[:], negmhf[:],
                                               start=True, stop=True))

                        def emit_SR(kt):
                            if kt == 0 and hq >= 2:
                                tn.wait_ge(sem("chv"), veA + 39 * (hq - 2) + 39)
                            if kt >= 2:
                                tn.wait_ge(sem("chs"), csc + 33 + kt - 1)
                            else:
                                tn.wait_ge(sem("chs"), csc + 31 + kt)
                            ins = tn.matmul(pbank[kt % 2][:],
                                            kslc[:, kt * 128:(kt + 1) * 128],
                                            q_fm[:, h, qh * 512:(qh + 1) * 512],
                                            start=True, stop=True)
                            inc("mm_att", ins)

                        def emit_lO(kt):
                            tn.wait_ge(sem("chs"), csc + 33 + kt + 1)
                            tn.matmul(pbank[4 + qh][0:1, :], onescol_b[:],
                                      P_sb2[:, kt % 4, :],
                                      start=(kt == 0), stop=(kt == 31))
                            ins = tn.matmul(pbank[2 + qh][:],
                                            vslc[:, kt * 128:(kt + 1) * 128],
                                            P_sb2[:, kt % 4, :],
                                            start=(kt == 0), stop=(kt == 31))
                            inc("mm_att", ins)

                        emit_SR(0)
                        for kt in range(31):
                            emit_SR(kt + 1)
                            emit_lO(kt)
                        emit_lO(31)
                        tn.wait_ge(sem("chv"), veA + 39 * hq + 37)
                        inc("mm_att", tn.matmul(pbank[6 + qh][:], onesr_f[:], lin_sb[:],
                                                start=True, stop=True))

            @blk.scalar
            def _(sc):
                sc = P.chain(sc, "chs")
                sc.wait_ge(sem("chv"), m_mbias)
                for h in range(16):
                    for qh in range(2):
                        hq = h * 2 + qh
                        base = attB + 129 * hq
                        for kt in range(32):
                            sc.wait_ge(sem("mm_att"), base + (1 if kt == 0 else 2 * kt))
                            if kt >= 4:
                                sc.wait_ge(sem("mm_att"), base + 2 * (kt - 4) + 3)
                            elif hq >= 1:
                                sc.wait_ge(sem("mm_att"),
                                           attB + 129 * (hq - 1) + 64 +
                                           (2 * (28 + kt) + 3 if kt < 3 else 64))
                            sc.activation(P_sb2[:, kt % 4, :], pbank[kt % 2][:], AF.Exp,
                                          bias=mbias[:, h:h + 1], scale=1.0 / CEXP)
                        sc.wait_ge(sem("mm_att"), base + 64)
                        sc.activation(lnv[:], pbank[4 + qh][0:1, :], AF.Ln)
                        for kt in range(32):
                            sc.wait_ge(sem("chv"), veA + 39 * hq + 5 + kt)
                            if kt >= 4:
                                sc.wait_ge(sem("mm_att"), base + 64 + 2 * (kt - 4) + 3)
                            else:
                                sc.wait_ge(sem("mm_att"),
                                           base + (2 * (28 + kt) + 3 if kt < 3 else 64))
                            sc.activation(P_sb2[:, kt % 4, :], pbank[kt % 2][:], AF.Exp,
                                          bias=0.0, scale=1.0)

            @blk.vector
            def _(v):
                v = P.chain(v, "chv")
                for h in range(16):
                    for qh in range(2):
                        hq = h * 2 + qh
                        base = attB + 129 * hq
                        csc = scA + 65 * hq
                        v.wait_ge(sem("chs"), csc + 33)
                        if hq >= 1:
                            v.wait_ge(sem("mm_nb"), hq)
                        v.tensor_scalar_mul(lnv[:], lnv[:], float(CEXP))
                        v.tensor_scalar(lnv[:], lnv[:], kn[0:1, 48 + h:49 + h], None,
                                        op0=ALU.add)
                        v.tensor_scalar_mul(negmhf[:], lnv[:], -1.0)
                        assert tot("chv") == veA + 39 * hq + 3
                        v.wait_ge(sem("mm_nb"), hq + 1)
                        v.tensor_copy(negmb[:], pbank[6 + qh][:])
                        for kt in range(32):
                            v.wait_ge(sem("mm_att"),
                                      base + 64 + (1 if kt == 0 else 2 * kt))
                            v.tensor_add(pbank[kt % 2][:], pbank[kt % 2][:],
                                         negmb[:])
                        assert tot("chv") == veA + 39 * hq + 36
                        v.wait_ge(sem("mm_att"), base + 128)
                        v.reciprocal(lin_sb[:], pbank[4 + qh][0:1, :])
                        v.wait_ge(sem("mm_att"), base + 129)
                        v.tensor_copy(linb_sb[:], pbank[6 + qh][:])
                        v.tensor_mul(attn_fm[:, h, qh * 512:(qh + 1) * 512],
                                     pbank[2 + qh][:], linb_sb[:])
                        assert tot("chv") == veA + 39 * (hq + 1)
            m_attn = tot("chv")

        wo0 = bigD.ap().rearrange("p a t -> p (a t)").rearrange("p (oc c) -> p oc c", oc=16)
        wo1 = bigC.ap().rearrange("p a t -> p (a t)").rearrange("p (oc c) -> p oc c", oc=16)
        nspill = 2 if stage == 3 else 1
        with nc.Block() as blk:
            veO = tot("chv")
            xl = [sbt(f"xl{i}", [128, D], F32, pst) for i in range(2)]
            x2t = [sbt(f"x2t{i}", [128, D], F32, pst) for i in range(2)]
            tmpf = sbt("tmpf", [128, 512], F32, pst)

            @blk.gpsimd
            def _(g):
                wait(g, "dma_kv")
                inc("dma_wo", g.dma_start(
                    wo0[:],
                    woutT_d.ap().rearrange("(oc p) d -> p oc d", p=128)[:, :, 0:1024]), 16)
                g.wait_ge(sem("chv"), m_attn)
                inc("dma_wo", g.dma_start(
                    wo1[:],
                    woutT_d.ap().rearrange("(oc p) d -> p oc d", p=128)[:, :, 1024:2048]), 16)

            @blk.sync
            def _(s):
                for tc in range(8):
                    if tc >= 2:
                        s.wait_ge(sem("chv"), veO + 8 * (tc - 1))
                    inc(f"dma_xr{tc % 2}", s.dma_start(xl[tc % 2][:],
                                                       x_d[tc * 128:(tc + 1) * 128, :]), 16)

            @blk.tensor
            def _(tn):
                tn.wait_ge(sem("chv"), m_attn)
                tn.wait_ge(sem("dma_wo"), 32)
                for tc in range(8):
                    for dq in range(4):
                        bk = (tc * 4 + dq) % 4
                        g_ = tc * 4 + dq
                        if g_ >= 4:
                            gp = g_ - 4
                            tn.wait_ge(sem("chv"), veO + 8 * (gp // 4) + 2 * (gp % 4) + 1)
                        wos = wo0 if dq < 2 else wo1
                        for oc in range(16):
                            ins = tn.matmul(pbank[bk][:],
                                            attn_fm[:, oc, tc * 128:(tc + 1) * 128],
                                            wos[:, oc, (dq % 2) * 512:(dq % 2 + 1) * 512],
                                            start=(oc == 0), stop=(oc == 15))
                        inc("mm_op", ins)

            @blk.vector
            def _(v):
                v = P.chain(v, "chv")
                for tc in range(8):
                    v.wait_ge(sem(f"dma_xr{tc % 2}"), 16 * (tc // 2 + 1))
                    if tc >= 2:
                        v.wait_ge(sem("dma_x2"), 16 * nspill * tc)
                    for dq in range(4):
                        bk = (tc * 4 + dq) % 4
                        v.wait_ge(sem("mm_op"), tc * 4 + dq + 1)
                        v.tensor_mul(tmpf[:], pbank[bk][:], gag[:, dq * 512:(dq + 1) * 512])
                        v.tensor_add(x2t[tc % 2][:, dq * 512:(dq + 1) * 512],
                                     xl[tc % 2][:, dq * 512:(dq + 1) * 512], tmpf[:])
                    assert tot("chv") == veO + 8 * tc + 8
            m_x2 = tot("chv")

            @blk.gpsimd
            def _(g2):
                for tc in range(8):
                    g2.wait_ge(sem("chv"), veO + 8 * (tc + 1))
                    inc("dma_x2", g2.dma_start(x2spill.ap()[tc * 128:(tc + 1) * 128, :],
                                               x2t[tc % 2][:]), 16)
                    if stage == 3:
                        inc("dma_x2", g2.dma_start(dbg["x2"].ap()[tc * 128:(tc + 1) * 128, :],
                                                   x2t[tc % 2][:]), 16)

    if stage == 3:
        with nc.Block() as blk:
            @blk.sync
            def _(s):
                wait(s, "dma_x2")
        st.close()
        return nc, dbg

    h2fm = bigA
    with contextlib.ExitStack() as pst:
        zl = [pst.enter_context(nc.sbuf_tensor(f"zl{i}", [128, D], F32)) for i in range(2)]
        znt = [pst.enter_context(nc.sbuf_tensor(f"znt{i}", [128, D], BF16)) for i in range(2)]
        h2t_t = [pst.enter_context(nc.sbuf_tensor(f"h2t{i}", [128, D], BF16)) for i in range(2)]
        sq2 = pst.enter_context(nc.sbuf_tensor("sq2", [128, D], F32))
        veL2 = tot("chv")
        scL2 = tot("chs")

        def l2_xn(tc):
            return veL2 + 28 * tc + 10

        def l2_tm(tc):
            return veL2 + 28 * tc + 12

        def l2_mod(tc):
            return veL2 + 28 * (tc + 1)

        with nc.Block() as blk:
            d_x2_all = tot("dma_x2")

            @blk.sync
            def _(s_):
                s_.wait_ge(sem("dma_x2"), d_x2_all)
                for tc in range(8):
                    if tc >= 2:
                        s_.wait_ge(sem("chv"), l2_xn(tc - 2))
                    inc(f"dma_z{tc % 2}", s_.dma_start(
                        zl[tc % 2][:], x2spill.ap()[tc * 128:(tc + 1) * 128, :]), 16)

            @blk.vector
            def _(v):
                v = P.chain(v, "chv")
                v.wait_ge(sem("dma_x2"), d_x2_all)
                for tc in range(8):
                    v.wait_ge(sem(f"dma_z{tc % 2}"), 16 * (tc // 2 + 1))
                    x_ = zl[tc % 2]
                    mean, msq, var, rin = (stats[:, 4:5], stats[:, 5:6],
                                           stats[:, 6:7], stats[:, 7:8])
                    v.reduce_sum(mean, x_[:], axis=AX.X)
                    v.tensor_scalar_mul(mean, mean, 1.0 / D)
                    v.tensor_mul(sq2[:], x_[:], x_[:])
                    v.reduce_sum(msq, sq2[:], axis=AX.X)
                    v.tensor_scalar_mul(msq, msq, 1.0 / D)
                    v.tensor_mul(var, mean, mean)
                    v.tensor_sub(var, msq, var)
                    v.tensor_scalar_add(var, var, EPS)
                    v.reciprocal(rin, var)
                    v.wait_ge(sem("chs"), scL2 + tc + 1)
                    v.scalar_tensor_tensor(znt[tc % 2][:], x_[:], mean,
                                           stats[:, 8:9].to_broadcast([128, D]),
                                           op0=ALU.subtract, op1=ALU.mult)
                    assert tot("chv") == l2_xn(tc)
                    if tc >= 2:
                        v.wait_ge(sem(f"dma_h2{tc % 2}"), 16 * (tc // 2))
                    v.tensor_mul(h2t_t[tc % 2][:], znt[tc % 2][:], cbb[:])
                    v.tensor_add(h2t_t[tc % 2][:], h2t_t[tc % 2][:], sbb[:])
                    assert tot("chv") == l2_tm(tc)
                    v.wait_ge(sem("mm_l2"), tc + 1)
                    for dc in range(16):
                        bk = (tc % 2) * 2 + dc // 8
                        v.scalar_tensor_tensor(
                            h2fm[:, dc, tc * 128:(tc + 1) * 128],
                            pbank_bf(bk)[:, (dc % 8) * 128:(dc % 8 + 1) * 128],
                            mparm[:, 32 + dc:33 + dc],
                            mparm[:, 48 + dc:49 + dc].to_broadcast([128, 128]),
                            op0=ALU.mult, op1=ALU.add)
                    assert tot("chv") == l2_mod(tc)

            @blk.scalar
            def _(sc):
                sc = P.chain(sc, "chs")
                sc.wait_ge(sem("chv"), m_x2)
                for tc in range(8):
                    sc.wait_ge(sem("chv"), veL2 + 28 * tc + 9)
                    sc.activation(stats[:, 8:9], stats[:, 7:8], AF.Sqrt)

            @blk.tensor
            def _(tn):
                for tc in range(8):
                    tn.wait_ge(sem("chv"), l2_xn(tc))
                    if tc >= 2:
                        tn.wait_ge(sem("chv"), l2_mod(tc - 2))
                    for dc in range(16):
                        bk = (tc % 2) * 2 + dc // 8
                        ins = tn.transpose(
                            pbank_bf(bk)[:, (dc % 8) * 128:(dc % 8 + 1) * 128],
                            znt[tc % 2][:, dc * 128:(dc + 1) * 128], identb[:])
                    inc("mm_l2", ins)

            @blk.gpsimd
            def _(g):
                for tc in range(8):
                    g.wait_ge(sem("chv"), l2_tm(tc))
                    inc(f"dma_h2{tc % 2}", g.dma_start(h2in.ap()[tc * 128:(tc + 1) * 128, :],
                                                       h2t_t[tc % 2][:]), 16)
                wait(g, "dma_h20")
                wait(g, "dma_h21")
                inc("cc_h2", g.collective_compute(
                    "AllGather", ALU.bypass, replica_groups=[list(range(8))],
                    ins=[h2in.ap().opt()], outs=[h2all.ap().opt()]))
        m_h2 = tot("chv")

    aff7 = sbt("aff7", [E, T], F32)
    afftm = sbt("afftm", [128, 8, E], F32)
    with contextlib.ExitStack() as pst:
        gw_sb = pst.enter_context(nc.sbuf_tensor("gw_sb", [128, 16, E], BF16))
        with nc.Block() as blk:
            scG = tot("chs")
            veGT = tot("chv")

            @blk.gpsimd
            def _(g):
                inc("dma_gw", g.dma_start(
                    gw_sb.ap(), gatew_d.ap().rearrange("(dc p) e -> p dc e", p=128)), 16)

            @blk.tensor
            def _(tn):
                tn.wait_ge(sem("chv"), m_h2)
                wait(tn, "dma_gw")
                for th in range(2):
                    for dc in range(16):
                        ins = tn.matmul(pbank[th][0:E, :], gw_sb[:, dc, :],
                                        h2fm[:, dc, th * 512:(th + 1) * 512],
                                        start=(dc == 0), stop=(dc == 15))
                    inc("mm_g", ins)
                for tc in range(8):
                    tn.wait_ge(sem("chs"), scG + 2)
                    if tc >= 2:
                        tn.wait_ge(sem("chv"), veGT + tc - 1)
                    ins = tn.transpose(pbank[2 + tc % 2][:, 0:E],
                                       aff7[:, tc * 128:(tc + 1) * 128], identf[0:E, 0:E])
                    inc("mm_gt", ins)

            @blk.scalar
            def _(sc):
                sc = P.chain(sc, "chs")
                for th in range(2):
                    sc.wait_ge(sem("mm_g"), th + 1)
                    sc.activation(aff7[:, th * 512:(th + 1) * 512], pbank[th][0:E, :],
                                  AF.Sigmoid, bias=ebias_sb[:])

            @blk.vector
            def _(v):
                v = P.chain(v, "chv")
                for tc in range(8):
                    v.wait_ge(sem("mm_gt"), tc + 1)
                    v.tensor_copy(afftm[:, tc, :], pbank[2 + tc % 2][:, 0:E])

            @blk.gpsimd
            def _(g):
                g.wait_ge(sem("chs"), scG + 2)
                inc("dma_af", g.dma_start(
                    affin.ap().rearrange("(e a) c -> e a c", a=16),
                    aff7.ap().rearrange("e (a c) -> e a c", a=16)), 16)
                wait(g, "dma_af")
                inc("cc_af", g.collective_compute(
                    "AllGather", ALU.bypass, replica_groups=[list(range(8))],
                    ins=[affin.ap().opt()], outs=[affall.ap().opt()]))
        m_afftm = tot("chv")

    CAPK = 1169
    olist = sbt("olist", [16, 128], I16)
    th_sb = sbt("th_sb", [1, 16], F32)
    idx_e = sbt("idx_e", [16, 7 * 80], I16)
    with contextlib.ExitStack() as pst:
        def rt(name):
            return pst.enter_context(nc.sbuf_tensor(name, [16, 512], F32))
        affs = [rt(f"affs{e}") for e in range(E)]
        m1r = rt("m1r"); m2r = rt("m2r"); sumeq = rt("sumeq")
        e1r = rt("e1r"); e2r = rt("e2r")
        acc1 = rt("acc1"); acc2 = rt("acc2"); k1acc = rt("k1acc"); k2acc = rt("k2acc")
        tA = rt("tA"); tB = rt("tB"); tC = rt("tC"); tD = rt("tD")
        keepb = pst.enter_context(nc.sbuf_tensor("keepb", [16, 512], BF16))
        vali = pst.enter_context(nc.sbuf_tensor("vali", [16, 512], I16))
        csrow = pst.enter_context(nc.sbuf_tensor("csrow", [1, 512 * 3], F32))
        sc128 = pst.enter_context(nc.sbuf_tensor("sc128", [128, 64], F32))
        thb16 = pst.enter_context(nc.sbuf_tensor("thb16", [16, E], F32))
        thr3 = pst.enter_context(nc.sbuf_tensor("thr3", [1, 8], F32))
        thr8 = pst.enter_context(nc.sbuf_tensor("thr8", [1, 8], F32))
        lohi = pst.enter_context(nc.sbuf_tensor("lohi", [16, 2], I32))
        midt = pst.enter_context(nc.sbuf_tensor("midt", [16, 1], I32))
        bisf = pst.enter_context(nc.sbuf_tensor("bisf", [16, 2], F32))
        bisp = pst.enter_context(nc.sbuf_tensor("bisp", [16, 2], I16))
        ones16 = pst.enter_context(nc.sbuf_tensor("ones16_sb", [16, 16], F32))
        sck = pst.enter_context(nc.sbuf_tensor("sck", [128, 64], F32))
        tE128 = pst.enter_context(nc.sbuf_tensor("tE128", [128, 64], F32))
        own16 = pst.enter_context(nc.sbuf_tensor("own16", [16, 2 * 64], I16))
        tBi = pst.enter_context(nc.sbuf_tensor("tBi", [16, 512], I16))

        with nc.Block() as blk:
            @blk.gpsimd
            def _(g):
                wait(g, "cc_af")
                for e in range(E):
                    inc("dma_as", g.dma_start(
                        affs[e].ap().rearrange("a (r c) -> a r c", r=8),
                        affall.ap().rearrange("(r e a) c -> e a r c", r=8, e=E)[e]), 16)

            @blk.vector
            def _(v):
                v = P.chain(v, "chv")
                wait(v, "dma_as")
                wait(v, "dma_z0")
                wait(v, "dma_z1")
                wait(v, "dma_gw")
                mins = v.memset(th_sb[:], 0.0)
                P.inc("chv", mins)
                v.wait_ge(sem("chv"), tot("chv"))
                v.tensor_max(m1r[:], affs[0][:], affs[1][:])
                for e in range(2, E):
                    v.tensor_max(m1r[:], m1r[:], affs[e][:])
                for e in range(E):
                    v.tensor_tensor(tA[:], affs[e][:], m1r[:], op=ALU.is_equal)
                    if e == 0:
                        v.tensor_copy(sumeq[:], tA[:])
                        v.tensor_scalar_mul(e1r[:], tA[:], float(E - e))
                    else:
                        v.tensor_add(sumeq[:], sumeq[:], tA[:])
                        v.tensor_scalar_mul(tB[:], tA[:], float(E - e))
                        v.tensor_max(e1r[:], e1r[:], tB[:])
                for e in range(E):
                    v.tensor_tensor(tA[:], affs[e][:], m1r[:], op=ALU.is_equal)
                    v.tensor_scalar_mul(tA[:], tA[:], -1e30)
                    v.tensor_add(tA[:], tA[:], affs[e][:])
                    if e == 0:
                        v.tensor_copy(m2r[:], tA[:])
                    else:
                        v.tensor_max(m2r[:], m2r[:], tA[:])
                v.tensor_single_scalar(tB[:], sumeq[:], 2.0, op=ALU.is_ge)
                v.tensor_copy(tBi[:], tB[:])
                v.copy_predicated(m2r[:], tBi[:], m1r[:])
                for e in range(E):
                    v.tensor_tensor(tA[:], affs[e][:], m1r[:], op=ALU.is_equal)
                    v.tensor_scalar_mul(tA[:], tA[:], float(E - e))
                    v.tensor_tensor(tC[:], tA[:], e1r[:], op=ALU.is_equal)
                    v.tensor_scalar(tC[:], tC[:], -1.0, 1.0, op0=ALU.mult, op1=ALU.add)
                    v.tensor_mul(tA[:], tA[:], tC[:])
                    if e == 0:
                        v.tensor_copy(tD[:], tA[:])
                    else:
                        v.tensor_max(tD[:], tD[:], tA[:])
                for e in range(E):
                    v.tensor_tensor(tA[:], affs[e][:], m2r[:], op=ALU.is_equal)
                    v.tensor_scalar_mul(tA[:], tA[:], float(E - e))
                    if e == 0:
                        v.tensor_copy(tC[:], tA[:])
                    else:
                        v.tensor_max(tC[:], tC[:], tA[:])
                v.copy_predicated(tC[:], tBi[:], tD[:])
                v.tensor_scalar(e2r[:], tC[:], -1.0, float(E), op0=ALU.mult, op1=ALU.add)
                v.tensor_scalar(e1r[:], e1r[:], -1.0, float(E), op0=ALU.mult, op1=ALU.add)
            m_top2 = tot("chv")

            @blk.vector
            def _(v):
                v = P.chain(v, "chv")
                for e in range(E):
                    v.tensor_tensor(tA[:], affs[e][:], m2r[:], op=ALU.is_ge)
                    v.tensor_scalar_add(tB[:], affs[e][:], 1.0)
                    v.tensor_mul(tB[:], tA[:], tB[:])
                    v.tensor_scalar(tB[:], tB[:], eoh[:, e:e + 1], None, op0=ALU.mult)
                    if e == 0:
                        v.tensor_copy(tD[:], tB[:])
                    else:
                        v.tensor_add(tD[:], tD[:], tB[:])
                v.tensor_scalar_add(tD[:], tD[:], -1.0)
            m_sco = tot("chv")

            @blk.gpsimd
            def _(g):
                inc("dma_th", g.dma_start(lohi[:], bisinit_d.ap()), 16)
                inc("dma_th", g.dma_start(ones16[:], ones16_d.ap()), 16)
                g.wait_ge(sem("chv"), m_sco + 10 * 30)
                inc("dma_th", g.dma_start(thin.ap(), lohi[0:1, 0:1].bitcast(F32)), 16)
                g.wait_ge(sem("dma_th"), 48)
                inc("cc_th", g.collective_compute(
                    "AllGather", ALU.bypass, replica_groups=[list(range(8))],
                    ins=[thin.ap().opt()], outs=[thall.ap().opt()]))
                wait(g, "cc_th")
                inc("dma_th", g.dma_start(thr8[:], bass.AP(thall, 0, [[8, 1], [1, 8]])), 16)

            @blk.tensor
            def _(tn):
                tn.wait_ge(sem("dma_th"), 32)
                for it in range(30):
                    tn.wait_ge(sem("chv"), m_sco + 10 * it + 4)
                    ins = tn.matmul(pbank[5][0:16, 0:1], ones16[:],
                                    bisf[:, 0:1], start=True, stop=True)
                    inc("mm_bis", ins)

            @blk.vector
            def _(v):
                v = P.chain(v, "chv")
                v.wait_ge(sem("dma_th"), 32)
                for it in range(30):
                    v.tensor_tensor(midt[:], lohi[:, 0:1], lohi[:, 1:2], op=ALU.add)
                    v.tensor_single_scalar(midt[:], midt[:], 1,
                                           op=ALU.logical_shift_right)
                    v.tensor_scalar(tA[:], tD[:], midt[:, 0:1].bitcast(F32), None,
                                    op0=ALU.is_ge)
                    v.reduce_sum(bisf[:, 0:1], tA[:], axis=AX.X)
                    v.wait_ge(sem("mm_bis"), it + 1)
                    v.tensor_single_scalar(bisf[:, 0:1], pbank[5][0:16, 0:1], 1170.0,
                                           op=ALU.is_ge)
                    v.tensor_copy(bisp[:, 0:1], bisf[:, 0:1])
                    v.tensor_single_scalar(bisf[:, 1:2], bisf[:, 0:1], 0.0,
                                           op=ALU.is_equal)
                    v.tensor_copy(bisp[:, 1:2], bisf[:, 1:2])
                    v.copy_predicated(lohi[:, 0:1], bisp[:, 0:1], midt[:])
                    v.copy_predicated(lohi[:, 1:2], bisp[:, 1:2], midt[:])
                    assert tot("chv") == m_sco + 10 * (it + 1)
                v.wait_ge(sem("dma_th"), 64)
                v.tensor_copy(bass.AP(th_sb, 0, [[16, 1], [2, E]]), thr8[0:1, 0:E])
            m_kth = tot("chv")

            @blk.tensor
            def _(tn):
                tn.wait_ge(sem("chv"), m_kth)
                for e in range(E):
                    ins = tn.matmul(pbank[0][0:16, e:e + 1], onesr_f[:, 0:16],
                                    th_sb[0:1, 2 * e:2 * e + 1], start=True, stop=True)
                inc("mm_thb", ins)
                ins = tn.matmul(pbank[1][:, 0:E], onesr_f[:],
                                th_sb[0:1, 0:2 * E:2], start=True, stop=True)
                inc("mm_thb", ins)

            @blk.vector
            def _(v):
                v = P.chain(v, "chv")
                wait(v, "mm_thb")
                v.tensor_copy(thb16[:], pbank[0][0:16, 0:E])
                v.tensor_copy(thtm[:], pbank[1][:, 0:E])
            m_thb = tot("chv")

            @blk.vector
            def _(v):
                v = P.chain(v, "chv")
                for e in range(E):
                    v.tensor_tensor(tA[:], affs[e][:], m2r[:], op=ALU.is_ge)
                    v.tensor_scalar(tB[:], affs[e][:], thb16[:, e:e + 1], None, op0=ALU.is_ge)
                    v.tensor_mul(tA[:], tA[:], tB[:])
                    if e >= 1:
                        wait(v, "gp_sp", e)
                    v.tensor_mul(tB[:], tA[:], gmapp1[:])
                    v.tensor_scalar_add(tB[:], tB[:], -1.0)
                    v.tensor_copy(vali[:], tB[:])
                    m_val = tot("chv")
                    v.tensor_copy(keepb[:], tA[:])
                    m_keepb = tot("chv")
                    v.wait_ge(sem("mm_pos"), 2 * e + 1)
                    v.tensor_copy(csrow[0:1, 0:512], pbank[3][0:1, :])
                    v.tensor_tensor_scan(csrow[0:1, 512:1024], csrow[0:1, 0:512],
                                         csrow[0:1, 0:512], 0.0,
                                         op0=ALU.add, op1=ALU.bypass)
                    v.tensor_sub(csrow[0:1, 1024:1536], csrow[0:1, 512:1024],
                                 csrow[0:1, 0:512])
                    m_excl = tot("chv")
                    v.wait_ge(sem("mm_pos"), 2 * e + 2)
                    v.tensor_copy(tB[:], pbank[2][0:16, :])
                    v.wait_ge(sem("mm_pos2"), e + 1)
                    v.tensor_copy(tC[:], pbank[4][0:16, :])
                    v.tensor_add(tB[:], tB[:], tC[:])
                    v.tensor_single_scalar(tC[:], e1r[:], float(e), op=ALU.is_equal)
                    v.tensor_single_scalar(tD[:], e2r[:], float(e), op=ALU.is_equal)
                    if e == 0:
                        v.tensor_mul(acc1[:], tC[:], tB[:])
                        v.tensor_mul(acc2[:], tD[:], tB[:])
                        v.tensor_mul(k1acc[:], tC[:], tA[:])
                        v.tensor_mul(k2acc[:], tD[:], tA[:])
                    else:
                        v.tensor_mul(m1r[:], tC[:], tB[:])
                        v.tensor_add(acc1[:], acc1[:], m1r[:])
                        v.tensor_mul(m1r[:], tD[:], tB[:])
                        v.tensor_add(acc2[:], acc2[:], m1r[:])
                        v.tensor_mul(m1r[:], tC[:], tA[:])
                        v.tensor_add(k1acc[:], k1acc[:], m1r[:])
                        v.tensor_mul(m1r[:], tD[:], tA[:])
                        v.tensor_add(k2acc[:], k2acc[:], m1r[:])
                    P.marks[f"val{e}"] = m_val
                    P.marks[f"keepb{e}"] = m_keepb
                    P.marks[f"excl{e}"] = m_excl
                v.wait_ge(sem("gp_sp"), E)
                v.tensor_scalar_mul(tA[:], e1r[:], float(CAPP))
                v.tensor_add(tA[:], tA[:], acc1[:])
                v.tensor_mul(tA[:], tA[:], k1acc[:])
                v.tensor_scalar_min(tA[:], tA[:], float(NSLOT - 1))
                v.tensor_scalar_mul(tB[:], e2r[:], float(CAPP))
                v.tensor_add(tB[:], tB[:], acc2[:])
                v.tensor_mul(tB[:], tB[:], k2acc[:])
                v.tensor_scalar_min(tB[:], tB[:], float(NSLOT - 1))
                v.tensor_copy(vali[:], tA[:])
                v.tensor_copy(keepb.ap().bitcast(I16), tB[:])
            m_sidx = tot("chv")

            @blk.tensor
            def _(tn):
                for e in range(E):
                    tn.wait_ge(sem("chv"), P.marks[f"keepb{e}"])
                    if e >= 1:
                        tn.wait_ge(sem("chv"), P.marks[f"excl{e - 1}"] + 2)
                    ins = tn.matmul(pbank[3][0:1, :], onescol_b[0:16, 0:1], keepb[:],
                                    start=True, stop=True)
                    inc("mm_pos", ins)
                    ins = tn.matmul(pbank[2][0:16, :], u16s[:], keepb[:],
                                    start=True, stop=True)
                    inc("mm_pos", ins)
                    tn.wait_ge(sem("chv"), P.marks[f"excl{e}"])
                    ins = tn.matmul(pbank[4][0:16, :], onesr_f[:, 0:16],
                                    csrow[0:1, 1024:1536], start=True, stop=True)
                    inc("mm_pos2", ins)

            @blk.gpsimd
            def _(g):
                g.load_library(LIB_SG)
                for e in range(E):
                    g.wait_ge(sem("chv"), P.marks[f"val{e}"])
                    if e >= 1:
                        wait(g, "gp_sp", e)
                    inc("gp_sp", g.sparse_gather(idx_e[:, e * 80:(e + 1) * 80], vali[:],
                                                 num_found=nf_sb[:]))
                g.wait_ge(sem("chv"), m_sidx)
                pid = nc.partition_id()
                src1 = bass.AP(vali, pid * 64, [[512, 16], [1, 64]])
                src2 = bass.AP(keepb, pid * 64, [[512, 16], [1, 64]]).bitcast(I16)
                inc("dma_ow", g.dma_start(own16[:, 0:64], src1), 16)
                inc("dma_ow", g.dma_start(own16[:, 64:128], src2), 16)
                wait(g, "dma_ow")
                inc("dma_ol", g.dma_start(olscr.ap(), own16.ap()), 16)
                wait(g, "dma_ol")
                for k_ in range(2):
                    for a_ in range(16):
                        inc("dma_ol", g.dma_start(
                            bass.AP(olist, 64 * k_ + 4 * a_, [[128, 16], [1, 4]]),
                            bass.AP(olscr, 64 * k_ + 128 * a_, [[1, 16], [16, 4]]),
                            allow_slow_non_contiguous=True), 16)
        m_rt = tot("chv")

    with nc.Block() as blk:
        @blk.vector
        def _(v):
            v = P.chain(v, "chv")
            v.wait_ge(sem("chv"), m_thb)
            for tc in range(8):
                a_ = afftm[:, tc, :]
                m1t = stats[:, 0:1]
                m2t = stats[:, 3:4]
                v.reduce_max(m1t, a_, axis=AX.X)
                v.tensor_tensor(sq_rt1[:], a_, m1t.to_broadcast([128, E]), op=ALU.is_equal)
                v.reduce_sum(stats[:, 11:12], sq_rt1[:], axis=AX.X)
                v.tensor_single_scalar(stats[:, 12:13], stats[:, 11:12], 2.0, op=ALU.is_ge)
                v.tensor_scalar_mul(sq_rt2[:], sq_rt1[:], -1e30)
                v.tensor_add(sq_rt2[:], sq_rt2[:], a_)
                v.reduce_max(stats[:, 4:5], sq_rt2[:], axis=AX.X)
                v.tensor_copy(m2t, stats[:, 4:5])
                v.tensor_copy(manyi[:], stats[:, 12:13])
                v.copy_predicated(m2t, manyi[:], m1t)
                v.tensor_mul(sq_rt2[:], sq_rt1[:], irev[:])
                v.reduce_max(stats[:, 10:11], sq_rt2[:], axis=AX.X)
                v.tensor_tensor(sq_rt1[:], sq_rt2[:], stats[:, 10:11].to_broadcast([128, E]),
                                op=ALU.is_equal)
                v.tensor_scalar(sq_rt3[:], sq_rt1[:], -1.0, 1.0, op0=ALU.mult, op1=ALU.add)
                v.tensor_mul(sq_rt2[:], sq_rt2[:], sq_rt3[:])
                v.reduce_max(stats[:, 13:14], sq_rt2[:], axis=AX.X)
                v.tensor_tensor(sq_rt3[:], a_, m2t.to_broadcast([128, E]), op=ALU.is_equal)
                v.tensor_mul(sq_rt3[:], sq_rt3[:], irev[:])
                v.reduce_max(stats[:, 14:15], sq_rt3[:], axis=AX.X)
                v.tensor_tensor(sq_rt2[:], sq_rt2[:], stats[:, 13:14].to_broadcast([128, E]),
                                op=ALU.is_equal)
                v.tensor_tensor(sq_rt3[:], sq_rt3[:], stats[:, 14:15].to_broadcast([128, E]),
                                op=ALU.is_equal)
                v.copy_predicated(sq_rt3[:], manyi[:].to_broadcast([128, E]), sq_rt2[:])
                v.tensor_mul(sq_rt1[:], sq_rt1[:], thtm[:])
                v.reduce_sum(stats[:, 5:6], sq_rt1[:], axis=AX.X)
                v.tensor_mul(sq_rt3[:], sq_rt3[:], thtm[:])
                v.reduce_sum(stats[:, 6:7], sq_rt3[:], axis=AX.X)
                v.tensor_tensor(stats[:, 7:8], m1t, stats[:, 5:6], op=ALU.is_ge)
                v.tensor_tensor(stats[:, 8:9], m2t, stats[:, 6:7], op=ALU.is_ge)
                v.tensor_mul(wownb[:, tc:tc + 1], stats[:, 7:8], m1t)
                v.tensor_mul(wownb[:, 8 + tc:9 + tc], stats[:, 8:9], m2t)
    m_wown = tot("chv")

    if stage == 4:
        with nc.Block() as blk:
            @blk.sync
            def _(s_):
                s_.wait_ge(sem("chv"), m_wown)
                wait(s_, "dma_ol")
                wait(s_, "dma_z0")
                wait(s_, "dma_z1")
                inc("dma_o", s_.dma_start(dbg["aff"].ap(), aff7.ap()), 16)
                inc("dma_o", s_.dma_start(dbg["theta"].ap(), th_sb.ap()), 16)
                inc("dma_o", s_.dma_start(dbg["sidx"].ap(), olist.ap()), 16)
                inc("dma_o", s_.dma_start(dbg["wown"].ap(), wownb.ap()), 16)
                inc("dma_o", s_.dma_start(dbg["h2t"].ap(), bigA.ap()), 16)
                wait(s_, "dma_o")
        st.close()
        return nc, dbg

    pst5 = contextlib.ExitStack()
    silb = sbt("silb", [128, 512], BF16, pst5)
    routsb = sbt("routsb", [128, 4, D], BF16, pst5)
    wstr5 = sbt("wstr5", [128, 2, 16, 512], BF16, pst5)
    x1g = bigB.ap().rearrange("p a t -> p (a t)").rearrange("p (fc t) -> p fc t", fc=16)
    shared_tm = bigC.ap().rearrange("p a t -> p (a t)").rearrange("p (tc d) -> p tc d", tc=8)
    with nc.Block() as blk:
        scS = tot("chs")
        veS = tot("chv")
        mmS = tot("mm_ff") if "mm_ff" in P.cnt else 0

        @blk.gpsimd
        def _(g):
            g.wait_ge(sem("chv"), veS)
            g.wait_ge(sem("chs"), scS)
            for nm in ("dma_ol", "dma_z0", "dma_z1", "gp_sp", "dma_o"):
                if nm in P.cnt:
                    wait(g, nm)
            for fq in range(4):
                if fq >= 2:
                    wait(g, "mm_ff", 8 * (fq - 1))
                inc(f"dma_w1{fq % 2}", g.dma_start(
                    wstr5[:, fq % 2, :, :],
                    w1s_d.ap().rearrange("(dc p) f -> p dc f", p=128)
                    [:, :, fq * 512:(fq + 1) * 512]), 16)

        @blk.tensor
        def _(tn):
            tn.wait_ge(sem("chv"), m_h2)
            for fq in range(4):
                tn.wait_ge(sem(f"dma_w1{fq % 2}"), 16 * (fq // 2 + 1))
                for k in range(8):
                    fc, th = divmod(k, 2)
                    if fq * 8 + k >= 4:
                        tn.wait_ge(sem("chs"), scS + (fq * 8 + k) - 3)
                    for dc in range(16):
                        ins = tn.matmul(pbank[k % 4][:],
                                        wstr5[:, fq % 2, dc, fc * 128:(fc + 1) * 128],
                                        h2fm[:, dc, th * 512:(th + 1) * 512],
                                        start=(dc == 0), stop=(dc == 15))
                    inc("mm_ff", ins)

        @blk.scalar
        def _(sc):
            sc = P.chain(sc, "chs")
            for fq in range(4):
                for k in range(8):
                    fc, th = divmod(k, 2)
                    sc.wait_ge(sem("mm_ff"), 8 * fq + k + 1)
                    f_g = fq * 512 + fc * 128
                    if fq < 2:
                        sc.copy(x1g[:, f_g // 128, th * 512:(th + 1) * 512],
                                pbank[k % 4][:])
                    else:
                        idx = (fq - 2) * 8 + k
                        if idx >= 1:
                            sc.wait_ge(sem("chv"), veS + idx)
                        sc.activation(silb[:], pbank[k % 4][:], AF.Silu)

        @blk.vector
        def _(v):
            v = P.chain(v, "chv")
            for fq in range(2, 4):
                for k in range(8):
                    fc, th = divmod(k, 2)
                    idx = (fq - 2) * 8 + k
                    v.wait_ge(sem("chs"), scS + 16 + idx + 1)
                    fg = fq * 512 + fc * 128 - 1024
                    v.tensor_mul(x1g[:, 8 + fg // 128, th * 512:(th + 1) * 512],
                                 x1g[:, fg // 128, th * 512:(th + 1) * 512], silb[:])
    m_w1s = tot("chv")
    scW1 = tot("chs")

    with nc.Block() as blk:
        scS2 = tot("chs")

        @blk.gpsimd
        def _(g):
            wait(g, "mm_ff", 32)
            for dq in range(4):
                if dq >= 2:
                    wait(g, "mm_f2", 8 * (dq - 1))
                inc(f"dma_w2{dq % 2}", g.dma_start(
                    wstr5[:, dq % 2, 0:8, :],
                    w2s_d.ap().rearrange("(fc p) d -> p fc d", p=128)
                    [:, :, dq * 512:(dq + 1) * 512]), 16)

        @blk.tensor
        def _(tn):
            tn.wait_ge(sem("chv"), m_w1s)
            for dq in range(4):
                tn.wait_ge(sem(f"dma_w2{dq % 2}"), 16 * (dq // 2 + 1))
                for tc in range(8):
                    g_ = dq * 8 + tc
                    if g_ >= 4:
                        tn.wait_ge(sem("chs"), scS2 + g_ - 3)
                    for fc in range(8):
                        ins = tn.matmul(pbank[g_ % 4][:],
                                        x1g[:, 8 + fc, tc * 128:(tc + 1) * 128],
                                        wstr5[:, dq % 2, fc, :],
                                        start=(fc == 0), stop=(fc == 7))
                    inc("mm_f2", ins)

        @blk.scalar
        def _(sc):
            sc = P.chain(sc, "chs")
            for dq in range(4):
                for tc in range(8):
                    sc.wait_ge(sem("mm_f2"), 8 * dq + tc + 1)
                    sc.copy(shared_tm[:, tc, dq * 512:(dq + 1) * 512],
                            pbank[(dq * 8 + tc) % 4][:])
    m_sh = tot("chs")

    with contextlib.ExitStack() as pst:
        elrep = pst.enter_context(nc.sbuf_tensor("elrep", [128, 80], I16))
        elist = pst.enter_context(nc.sbuf_tensor("elist", [16, 80], I16))
        elf = pst.enter_context(nc.sbuf_tensor("elf", [16, 80], F32))
        elacc = pst.enter_context(nc.sbuf_tensor("elacc", [16, 80], F32))

        with nc.Block() as blk:
            @blk.vector
            def _(v):
                v = P.chain(v, "chv")
                v.wait_ge(sem("gp_sp"), E)
                for e in range(E):
                    v.tensor_copy(elf[:], idx_e[:, e * 80:(e + 1) * 80])
                    if e == 0:
                        v.tensor_scalar(elacc[:], elf[:], eoh[:, 0:1], None, op0=ALU.mult)
                    else:
                        v.tensor_scalar(elf[:], elf[:], eoh[:, e:e + 1], None, op0=ALU.mult)
                        v.tensor_add(elacc[:], elacc[:], elf[:])
                v.tensor_scalar_max(elacc[:], elacc[:], 0.0)
                v.tensor_scalar_min(elacc[:], elacc[:], 8191.0)
                v.tensor_copy(elist[:], elacc[:])
            m_el = tot("chv")

            @blk.gpsimd
            def _(g):
                g.load_library(LIB_MLP)
                g.wait_ge(sem("chv"), m_el)
                for rep in range(8):
                    inc("dma_el", g.dma_start(elrep[rep * 16:(rep + 1) * 16, :], elist[:]), 16)
                wait(g, "dma_el")
                wait(g, "cc_h2")

        CH = (512, 512, 256)
        for c in range(3):
            nsl = CH[c]
            gexp = (bigD.ap().rearrange("p a t -> p (a t)")[:, 0:16 * nsl]
                    .rearrange("p (dc x) -> p dc x", dc=16))
            with nc.Block() as blk:
                scR = tot("chs")
                veR2 = tot("chv")
                mmR = tot("mm_r1") if "mm_r1" in P.cnt else 0
                mmR2 = tot("mm_r2") if "mm_r2" in P.cnt else 0

                @blk.gpsimd
                def _(g):
                    if c > 0:
                        wait(g, "mm_r1", mmR)
                        wait(g, "mm_r2", mmR2)
                    inc("gd_x", g.dma_gather(
                        gexp[:, :, :], h2all.ap(), elrep[:, 32 * c:32 * c + nsl // 16],
                        nsl, nsl, D, transpose=True), 16)
                    for fq in range(4):
                        if fq >= 2:
                            wait(g, "mm_r1", mmR + 4 * (fq - 1))
                        inc(f"dma_r1{fq % 2}", g.dma_start(
                            wstr5[:, fq % 2, :, :],
                            w1r_d.ap().rearrange("(dc p) f -> p dc f", p=128)
                            [:, :, fq * 512:(fq + 1) * 512]), 16)

                @blk.tensor
                def _(tn):
                    wait(tn, "gd_x")
                    if c == 0:
                        tn.wait_ge(sem("chs"), m_sh)
                    for fq in range(4):
                        tn.wait_ge(sem(f"dma_r1{fq % 2}"), 16 * (2 * c + fq // 2 + 1))
                        for fc in range(4):
                            g_ = fq * 4 + fc
                            if g_ >= 4:
                                tn.wait_ge(sem("chs"), scR + g_ - 3)
                            for dc in range(16):
                                ins = tn.matmul(pbank[g_ % 4][0:128, 0:nsl],
                                                wstr5[:, fq % 2, dc, fc * 128:(fc + 1) * 128],
                                                gexp[:, dc, 0:nsl],
                                                start=(dc == 0), stop=(dc == 15))
                            inc("mm_r1", ins)

                @blk.scalar
                def _(sc):
                    sc = P.chain(sc, "chs")
                    for fq in range(4):
                        for fc in range(4):
                            g_ = fq * 4 + fc
                            sc.wait_ge(sem("mm_r1"), mmR + g_ + 1)
                            f_g = fq * 512 + fc * 128
                            if fq < 2:
                                sc.copy(x1g[:, f_g // 128, 0:nsl], pbank[g_ % 4][0:128, 0:nsl])
                            else:
                                idx = (fq - 2) * 4 + fc
                                if idx >= 1:
                                    sc.wait_ge(sem("chv"), veR2 + idx)
                                elif c > 0:
                                    sc.wait_ge(sem("chv"), veR2)
                                sc.activation(silb[:, 0:nsl], pbank[g_ % 4][0:128, 0:nsl], AF.Silu)

                @blk.vector
                def _(v):
                    v = P.chain(v, "chv")
                    for fq in range(2, 4):
                        for fc in range(4):
                            idx = (fq - 2) * 4 + fc
                            v.wait_ge(sem("chs"), scR + 8 + idx + 1)
                            fg = fq * 512 + fc * 128 - 1024
                            v.tensor_mul(x1g[:, 8 + fg // 128, 0:nsl],
                                         x1g[:, fg // 128, 0:nsl], silb[:, 0:nsl])
                m_gr = tot("chv")

                @blk.gpsimd
                def _(g):
                    wait(g, "mm_r1", mmR + 16)
                    for dq in range(4):
                        if dq >= 2:
                            wait(g, "mm_r2", mmR2 + (nsl // 128) * (dq - 1))
                        inc(f"dma_r2{dq % 2}", g.dma_start(
                            wstr5[:, dq % 2, 0:8, :],
                            w2r_d.ap().rearrange("(fc p) d -> p fc d", p=128)
                            [:, :, dq * 512:(dq + 1) * 512]), 16)

                @blk.tensor
                def _(tn):
                    tn.wait_ge(sem("chv"), m_gr)
                    scR2 = scR + 16
                    for dq in range(4):
                        tn.wait_ge(sem(f"dma_r2{dq % 2}"), 16 * (2 * c + dq // 2 + 1))
                        for sc_ in range(nsl // 128):
                            g_ = dq * (nsl // 128) + sc_
                            if g_ >= 2:
                                tn.wait_ge(sem("chs"), scR2 + g_ - 1)
                            for fc in range(8):
                                ins = tn.matmul(pbank[4 + g_ % 2][:],
                                                x1g[:, 8 + fc, sc_ * 128:(sc_ + 1) * 128],
                                                wstr5[:, dq % 2, fc, :],
                                                start=(fc == 0), stop=(fc == 7))
                            inc("mm_r2", ins)

                @blk.scalar
                def _(sc):
                    sc = P.chain(sc, "chs")
                    for dq in range(4):
                        for sc_ in range(nsl // 128):
                            g_ = dq * (nsl // 128) + sc_
                            if g_ == 0 and c > 0:
                                sc.wait_ge(sem("dma_ro"), 16 * c)
                            sc.wait_ge(sem("mm_r2"), mmR2 + g_ + 1)
                            sc.copy(routsb[:, sc_, dq * 512:(dq + 1) * 512],
                                    pbank[4 + g_ % 2][:])
                m_rout = tot("chs")

                @blk.gpsimd
                def _(g):
                    g.wait_ge(sem("chs"), m_rout)
                    inc("dma_ro", g.dma_start(
                        routin.ap()[c * 512:c * 512 + nsl, :]
                        .rearrange("(sc p) d -> p sc d", p=128),
                        routsb[:, 0:nsl // 128, :]), 16)

        with nc.Block() as blk:
            @blk.gpsimd
            def _(g):
                wait(g, "dma_ro")
                inc("cc_ro", g.collective_compute(
                    "AllGather", ALU.bypass, replica_groups=[list(range(8))],
                    ins=[routin.ap().opt()], outs=[routall.ap().opt()]))

    pst5.close()
    with contextlib.ExitStack() as pst:
        olrep = pst.enter_context(nc.sbuf_tensor("olrep", [128, 128], I16))
        xf = [pst.enter_context(nc.sbuf_tensor(f"xf{i}", [128, D], F32)) for i in range(2)]
        of = [pst.enter_context(nc.sbuf_tensor(f"of{i}", [128, D], F32)) for i in range(2)]
        t1c = pst.enter_context(nc.sbuf_tensor("t1c", [128, D], F32))
        gown = bigD.ap().rearrange("p a t -> p (a t)").rearrange("p (b d) -> p b d", b=8)

        with nc.Block() as blk:
            veC = tot("chv")
            d_ro = tot("dma_ro")

            @blk.gpsimd
            def _(g):
                wait(g, "cc_ro")
                for rep in range(8):
                    inc("dma_or", g.dma_start(olrep[rep * 16:(rep + 1) * 16, :], olist[:]), 16)
                wait(g, "dma_or")
                for j2 in range(2):
                    inc("gd_o", g.dma_gather(
                        gown[:, 4 * j2:4 * j2 + 4, :], routall.ap(),
                        olrep[:, 32 * (2 * j2):32 * (2 * j2) + 32],
                        512, 512, D), 16)
                for tc in range(4):
                    if tc >= 2:
                        g.wait_ge(sem("chv"), veC + 5 * (tc - 1))
                    inc(f"dma_xf{tc % 2}", g.dma_start(
                        xf[tc % 2][:], x2spill.ap()[tc * 128:(tc + 1) * 128, :]), 16)
                g.wait_ge(sem("chv"), veC + 5 * 4)
                for j2 in range(2):
                    inc("gd_o", g.dma_gather(
                        gown[:, 4 * j2:4 * j2 + 4, :], routall.ap(),
                        olrep[:, 32 * (2 * j2 + 1):32 * (2 * j2 + 1) + 32],
                        512, 512, D), 16)
                for tc in range(4, 8):
                    g.wait_ge(sem("chv"), veC + 5 * (tc - 1))
                    inc(f"dma_xf{tc % 2}", g.dma_start(
                        xf[tc % 2][:], x2spill.ap()[tc * 128:(tc + 1) * 128, :]), 16)

            @blk.vector
            def _(v):
                v = P.chain(v, "chv")
                for tc in range(8):
                    half, off = divmod(tc, 4)
                    v.wait_ge(sem("gd_o"), 32 * (half + 1))
                    v.wait_ge(sem(f"dma_xf{tc % 2}"), 16 * (tc // 2 + 1))
                    if tc >= 2:
                        v.wait_ge(sem(f"dma_out{tc % 2}"), 16 * (tc // 2))
                    g1 = gown[:, off, :]
                    g2 = gown[:, 4 + off, :]
                    v.tensor_scalar(t1c[:], g1, wownb[:, tc:tc + 1], None, op0=ALU.mult)
                    v.scalar_tensor_tensor(t1c[:], g2, wownb[:, 8 + tc:9 + tc], t1c[:],
                                           op0=ALU.mult, op1=ALU.add)
                    v.tensor_add(t1c[:], t1c[:], shared_tm[:, tc, :])
                    v.tensor_mul(t1c[:], t1c[:], gag[:, D:2 * D])
                    v.tensor_add(of[tc % 2][:], xf[tc % 2][:], t1c[:])
                    assert tot("chv") == veC + 5 * (tc + 1)

            @blk.sync
            def _(s2):
                for tc in range(8):
                    s2.wait_ge(sem("chv"), veC + 5 * (tc + 1))
                    inc(f"dma_out{tc % 2}", s2.dma_start(out_d.ap()[tc * 128:(tc + 1) * 128, :],
                                                         of[tc % 2][:]), 16)
                wait(s2, "dma_out0")
                wait(s2, "dma_out1")

    st.close()
    return nc, dbg


def rope_tables(qi):
    pos = np.arange(qi * 1024, (qi + 1) * 1024, dtype=np.float64)
    invf = (10000.0 ** (-np.arange(0, 64, dtype=np.float64) / 64.0))
    ang = pos[None, :] * np.concatenate([invf, invf])[:, None]
    return np.cos(ang), np.sin(ang)


def make_inputs(ins, r):
    b, qi = r // 4, r % 4
    bf = lambda a: np.ascontiguousarray(a).astype(ml_dtypes.bfloat16)
    f32 = lambda a: np.ascontiguousarray(a, dtype=np.float32)
    x_full = np.asarray(ins["L0_attn_x"], np.float32).reshape(8192, 2048)
    cos, sin = rope_tables(qi)
    sc = 1.0 / np.sqrt(128.0)
    R = np.zeros((128, 128), np.float32)
    for i in range(64):
        R[i, i + 64] = -1.0
        R[i + 64, i] = 1.0
    gmap = np.zeros((16, 512), np.float32)
    for a in range(16):
        for col in range(512):
            gmap[a, col] = 1024 * (col // 64) + 64 * a + (col % 64) + 1.0
    m = {
        "x": f32(x_full[r * 1024:(r + 1) * 1024]),
        "wqkvT": f32(np.asarray(ins["L0_attn_w_qkv"]).T),
        "woutT": f32(np.asarray(ins["L0_attn_w_out"]).T),
        "gatew": f32(np.asarray(ins["L0_ffn_gate_w"])),
        "ebias": f32(np.asarray(ins["L0_ffn_expert_bias"]).reshape(E, 1)),
        "w1s": f32(np.asarray(ins["L0_ffn_shared_w1"])[0]),
        "w2s": f32(np.asarray(ins["L0_ffn_shared_w2"])[0]),
        "w1r": f32(np.asarray(ins["L0_ffn_routed_w1"])[min(r, E - 1)]),
        "w2r": f32(np.asarray(ins["L0_ffn_routed_w2"])[min(r, E - 1)]),
        "cc": f32(np.asarray(ins["L0_c"])),
        "wmod": f32(np.asarray(ins["L0_wmod"])[:, r * 1536:(r + 1) * 1536]),
        "identb": bf(np.eye(128)),
        "identf": f32(np.eye(128)),
        "rotT": bf(R.T),
        "cosq": bf(cos * sc), "sinq": bf(sin * sc),
        "cosk": bf(cos), "sink": bf(sin),
        "u16s": bf(np.triu(np.ones((16, 16)), 1)),
        "gmapp1": gmap,
        "irev": f32(np.tile(np.arange(E - 1, -1, -1, dtype=np.float32)[None, :] + 1.0, (128, 1))),
        "bsel": f32(np.tile([[1.0, 0.0]] if b == 0 else [[0.0, 1.0]], (6, 1))),
        "rowm": f32(np.array([0, 0, 1, 0, 0, 1], np.float32).reshape(6, 1)),
        "eoh": f32(np.tile(np.eye(E)[min(r, E - 1)][None, :], (16, 1))),
        "onesr_b": bf(np.ones((1, 128))),
        "onesr_f": f32(np.ones((1, 128))),
        "onescol_b": bf(np.ones((128, 1))),
        "ones2_b": bf(np.ones((128, 2))),
        "bisinit": np.tile(np.array([[0, 0x40000000]], np.int32), (16, 1)),
        "ones16": f32(np.ones((16, 16))),
    }
    return m


def kernel(**inputs):
    nc, _ = build(5)
    nc.compile()
    in_maps = [make_inputs(inputs, r) for r in range(8)]
    res = run_bass_kernel_spmd(nc, in_maps, core_ids=list(range(8)))
    out = np.concatenate([np.asarray(res.results[r]["out"], np.float32)
                          for r in range(8)], axis=0)
    return out.reshape(2, 4096, 2048)


# version 09
# speedup vs baseline: 2.7071x; 1.0034x over previous
import contextlib
import numpy as np
import ml_dtypes
import concourse.bacc as bacc
import concourse.bass as bass
import concourse.mybir as mybir
from concourse.bass_utils import run_bass_kernel_spmd
from concourse.library_config import attn as LIB_ATTN, mlp as LIB_MLP, sparse_gather as LIB_SG

F32 = mybir.dt.float32
BF16 = mybir.dt.bfloat16
I16 = mybir.dt.int16
I32 = mybir.dt.int32
U32 = mybir.dt.uint32
AF = mybir.ActivationFunctionType
ALU = mybir.AluOpType
AX = mybir.AxisListType
DRM = mybir.MatmulPerfMode.DoubleRow

D, T, NH, HD, E = 2048, 1024, 16, 128, 7
CAPP = 1280
NSLOT = 8 * CAPP
FF, F2 = 1024, 2048
EPS = 1e-5
CEXP = 10.0



CHAIN_SKIP = {"wait_ge", "then_inc", "dma_start", "dma_start_transpose", "dma_gather",
              "dma_scatter_add", "collective_compute", "indirect_dma_start",
              "load_library", "sem_inc", "nop", "engine_nop", "alloc_register",
              "reg_load", "free_register", "drain", "memset", "memzero"}


class _Chained:

    def __init__(self, P, eng, name):
        self._P, self._eng, self._name = P, eng, name

    def __getattr__(self, k):
        v = getattr(self._eng, k)
        if k in CHAIN_SKIP or not callable(v):
            return v
        P, eng, name = self._P, self._eng, self._name

        def wrap(*a, **kw):
            r = v(*a, **kw)
            n = P.inc(name, r)
            eng.wait_ge(P.sem(name), n)
            return r
        return wrap


class Prog:
    def __init__(self, nc, st):
        self.nc, self.st, self.sems, self.cnt = nc, st, {}, {}
        self.marks = {}

    def sem(self, name):
        if name not in self.sems:
            self.sems[name] = self.st.enter_context(self.nc.semaphore(name))
            self.cnt[name] = 0
        return self.sems[name]

    def inc(self, name, ins, k=1):
        ins.then_inc(self.sem(name), k)
        self.cnt[name] += k
        return self.cnt[name]

    def tot(self, name):
        self.sem(name)
        return self.cnt[name]

    def wait(self, eng, name, val=None):
        eng.wait_ge(self.sem(name), self.tot(name) if val is None else val)

    def chain(self, eng, name):
        return _Chained(self, eng, name)


def build(stage=5):
    nc = bacc.Bacc("TRN2")
    st = contextlib.ExitStack()
    P = Prog(nc, st)
    sem, inc, tot, wait = P.sem, P.inc, P.tot, P.wait

    def din(name, shape, dt=F32):
        return nc.dram_tensor(name, shape, dt, kind="ExternalInput")

    x_d = din("x", [T, D])
    wqkvT_d = din("wqkvT", [D, 3 * D])
    woutT_d = din("woutT", [D, D])
    gatew_d = din("gatew", [D, E])
    ebias_d = din("ebias", [E, 1])
    w1s_d = din("w1s", [D, F2])
    w2s_d = din("w2s", [FF, D])
    w1r_d = din("w1r", [D, F2])
    w2r_d = din("w2r", [FF, D])
    c_d = din("cc", [2, D])
    wmod_d = din("wmod", [D, 1536])
    identb_d = din("identb", [128, 128], BF16)
    identf_d = din("identf", [128, 128], F32)
    rotT_d = din("rotT", [128, 128], BF16)
    cosq_d = din("cosq", [128, T], BF16)
    sinq_d = din("sinq", [128, T], BF16)
    cosk_d = din("cosk", [128, T], BF16)
    sink_d = din("sink", [128, T], BF16)
    u16s_d = din("u16s", [16, 16], BF16)
    gmapp1_d = din("gmapp1", [16, 512], F32)
    irev_d = din("irev", [128, E], F32)
    bsel_d = din("bsel", [6, 2], F32)
    rowm_d = din("rowm", [6, 1], F32)
    eoh_d = din("eoh", [16, E], F32)
    onesr_b_d = din("onesr_b", [1, 128], BF16)
    onesr_f_d = din("onesr_f", [1, 128], F32)
    onescol_b_d = din("onescol_b", [128, 1], BF16)
    ones2_b_d = din("ones2_b", [128, 2], BF16)
    bisinit_d = din("bisinit", [16, 2], I32)
    ones16_d = din("ones16", [16, 16], F32)

    out_d = nc.dram_tensor("out", [T, D], F32, kind="ExternalOutput")

    modin = nc.dram_tensor("modin", [2, 1536], F32)
    modall = nc.dram_tensor("modall", [16, 1536], F32)
    kin0 = nc.dram_tensor("kin0", [D // 2, T], BF16)
    kin1 = nc.dram_tensor("kin1", [D // 2, T], BF16)
    vin = nc.dram_tensor("vin", [T, D], BF16)
    knin = nc.dram_tensor("knin", [1, NH], F32)
    kall0 = nc.dram_tensor("kall0", [2 * D, T], BF16)
    kall1 = nc.dram_tensor("kall1", [2 * D, T], BF16)
    vall = nc.dram_tensor("vall", [4 * T, D], BF16)
    knall = nc.dram_tensor("knall", [4, NH], F32)
    h2in = nc.dram_tensor("h2in", [T, D], BF16)
    affin = nc.dram_tensor("affin", [112, 64], F32)
    h2all = nc.dram_tensor("h2all", [8 * T, D], BF16)
    affall = nc.dram_tensor("affall", [8 * 112, 64], F32)
    routin = nc.dram_tensor("routin", [CAPP, D], BF16)
    routall = nc.dram_tensor("routall", [NSLOT, D], BF16)
    x2spill = nc.dram_tensor("x2spill", [T, D], F32)
    scr128 = nc.dram_tensor("scr128", [128, 64], F32)
    olscr = nc.dram_tensor("olscr", [16, 128], I16)
    thin = nc.dram_tensor("thin", [1, 1], F32)
    thall = nc.dram_tensor("thall", [8, 1], F32)

    dbg = {}

    def dbgadd(name, shape, dt):
        dbg[name] = nc.dram_tensor("dbg_" + name, shape, dt, kind="ExternalOutput")

    if stage == 1:
        dbgadd("h1t", [128, 16, T], BF16)
        dbgadd("vbo", [6, D], F32)
        dbgadd("mparm", [128, 64], F32)
    if stage == 2:
        dbgadd("qf", [128, NH, T], BF16)
        dbgadd("kf", [128, NH, T], BF16)
        dbgadd("vt", [128, 8, D], BF16)
    if stage == 3:
        dbgadd("x2", [T, D], F32)
    if stage == 4:
        dbgadd("aff", [E, T], F32)
        dbgadd("theta", [1, 16], F32)
        dbgadd("sidx", [16, 128], I16)
        dbgadd("wown", [128, 16], F32)
        dbgadd("h2t", [128, 16, T], BF16)

    def sbt(name, shape, dt, stk=None):
        return (stk or st).enter_context(nc.sbuf_tensor(name, shape, dt))

    identb = sbt("identb_s", [128, 128], BF16)
    identf = sbt("identf_s", [128, 128], F32)
    rotT = sbt("rotT_s", [128, 128], BF16)
    u16s = sbt("u16s_s", [16, 16], BF16)
    gmapp1 = sbt("gmapp1_s", [16, 512], F32)
    irev = sbt("irev_s", [128, E], F32)
    bsel = sbt("bsel_s", [6, 2], F32)
    rowm = sbt("rowm_s", [6, 1], F32)
    eoh = sbt("eoh_s", [16, E], F32)
    onesr_b = sbt("onesr_b_s", [1, 128], BF16)
    onesr_f = sbt("onesr_f_s", [1, 128], F32)
    onescol_b = sbt("onescol_b_s", [128, 1], BF16)
    ones2_b = sbt("ones2_b_s", [128, 2], BF16)
    ebias_sb = sbt("ebias_sb", [E, 1], F32)
    mparm = sbt("mparm", [128, 64], F32)
    gag = sbt("gag", [128, 2 * D], BF16)
    cbb = sbt("cbb", [128, D], BF16)
    sbb = sbt("sbb", [128, D], BF16)
    mbias = sbt("mbias", [128, NH], F32)
    thtm = sbt("thtm", [128, E], F32)
    wownb = sbt("wownb", [128, 16], F32)
    stats = sbt("stats", [128, 16], F32)
    nf_sb = sbt("nf_sb", [1, 1], U32)
    sq_rt1 = sbt("sq_rt1", [128, E], F32)
    sq_rt2 = sbt("sq_rt2", [128, E], F32)
    sq_rt3 = sbt("sq_rt3", [128, E], F32)
    manyi = sbt("manyi", [128, 1], I16)
    kn = sbt("kn", [1, 64], F32)

    pbank = [st.enter_context(nc.psum_tensor(f"pb{i}", [128, 512], F32)) for i in range(8)]

    def pbank_bf(i):
        return pbank[i].ap().bitcast(BF16)

    with nc.Block() as blk:
        @blk.sync
        def _(s):
            for t_, d_ in ((identb, identb_d), (identf, identf_d), (rotT, rotT_d),
                           (u16s, u16s_d), (gmapp1, gmapp1_d), (irev, irev_d),
                           (bsel, bsel_d), (rowm, rowm_d), (eoh, eoh_d), (onesr_b, onesr_b_d),
                           (onesr_f, onesr_f_d), (onescol_b, onescol_b_d),
                           (ones2_b, ones2_b_d), (ebias_sb, ebias_d)):
                inc("dma_s", s.dma_start(t_[:], d_[:]), 16)

    with contextlib.ExitStack() as pst:
        vb = sbt("vb", [6, D], F32, pst)
        csb = sbt("csb", [2, D], F32, pst)
        csil = sbt("csil", [2, D], F32, pst)
        ct = sbt("ct", [128, 32], F32, pst)
        wmv = sbt("wmv", [128, 16, 1536], F32, pst)
        modv = sbt("modv", [2, 1536], F32, pst)
        vrow = sbt("vrow", [1, D], F32, pst)
        garow = sbt("garow", [1, D], BF16, pst)
        gbrow = sbt("gbrow", [1, D], BF16, pst)
        cbrow = sbt("cbrow", [1, D], BF16, pst)
        sbrow = sbt("sbrow", [1, D], BF16, pst)
        mv = sbt("mv", [6, 2 * D], F32, pst)

        with nc.Block() as blk:
            @blk.sync
            def _(s):
                inc("dma_c", s.dma_start(csb[:], c_d[:]), 16)

            @blk.gpsimd
            def _(g):
                inc("dma_w", g.dma_start(
                    wmv.ap(),
                    wmod_d.ap().rearrange("(dc p) j -> p dc j", p=128)), 16)

            @blk.scalar
            def _(sc):
                sc = P.chain(sc, "chs")
                wait(sc, "dma_c")
                sc.activation(csil[:], csb[:], AF.Silu)
            m_sil = tot("chs")

            @blk.tensor
            def _(tn):
                wait(tn, "dma_s")
                tn.wait_ge(sem("chs"), m_sil)
                for dc in range(16):
                    ins = tn.transpose(pbank[5][:, 2 * dc:2 * dc + 2],
                                       csil[:, dc * 128:(dc + 1) * 128], identf[0:2, 0:2])
                inc("mm_s", ins)

            @blk.vector
            def _(v):
                v = P.chain(v, "chv")
                wait(v, "mm_s")
                v.tensor_copy(ct[:], pbank[5][:, 0:32])
            m_ct = tot("chv")
            mm0 = tot("mm_s")

            @blk.tensor
            def _(tn):
                tn.wait_ge(sem("chv"), m_ct)
                wait(tn, "dma_w")
                for jg in range(3):
                    if jg >= 2:
                        tn.wait_ge(sem("chv"), m_ct + jg - 1)
                    for dc in range(16):
                        ins = tn.matmul(pbank[jg % 2][0:2, :], ct[:, 2 * dc:2 * dc + 2],
                                        wmv[:, dc, jg * 512:(jg + 1) * 512],
                                        start=(dc == 0), stop=(dc == 15))
                    inc("mm_s", ins)

            @blk.vector
            def _(v):
                v = P.chain(v, "chv")
                for jg in range(3):
                    wait(v, "mm_s", mm0 + jg + 1)
                    v.tensor_copy(modv[:, jg * 512:(jg + 1) * 512], pbank[jg % 2][0:2, :])
            m_modv = tot("chv")

            @blk.gpsimd
            def _(g):
                g.wait_ge(sem("chv"), m_modv)
                inc("dma_g", g.dma_start(modin[:], modv[:]), 16)
                wait(g, "dma_g")
                inc("cc_s", g.collective_compute(
                    "AllGather", ALU.bypass, replica_groups=[list(range(8))],
                    ins=[modin.ap().opt()], outs=[modall.ap().opt()]))

        with nc.Block() as blk:
            @blk.gpsimd
            def _(g):
                wait(g, "cc_s")
                for v_ in range(6):
                    for b_ in range(2):
                        J0, done = 2048 * v_, 0
                        while done < 2048:
                            rk, c0 = J0 // 1536, J0 % 1536
                            n = min(1536 - c0, 2048 - done)
                            inc("dma_g", g.dma_start(
                                mv[v_:v_ + 1, b_ * D + done:b_ * D + done + n],
                                modall[2 * rk + b_:2 * rk + b_ + 1, c0:c0 + n]), 16)
                            J0 += n
                            done += n

            @blk.vector
            def _(v):
                v = P.chain(v, "chv")
                wait(v, "dma_g")
                wait(v, "dma_s")
                v.tensor_scalar(vb[:], mv[:, 0:D], bsel[:, 0:1], None, op0=ALU.mult)
                v.scalar_tensor_tensor(vb[:], mv[:, D:2 * D], bsel[:, 1:2], vb[:],
                                       op0=ALU.mult, op1=ALU.add)
                sgn = mv[:, 0:D]
                ab = mv[:, D:2 * D]
                v.tensor_single_scalar(sgn, vb[:], 0.0, op=ALU.is_ge)
                v.tensor_scalar(sgn, sgn, 2.0, -1.0, op0=ALU.mult, op1=ALU.add)
                v.tensor_scalar(ab, vb[:], -1.0, None, op0=ALU.mult)
                v.tensor_max(ab, ab, vb[:])
                v.tensor_scalar_max(ab, ab, 0.25)
                v.tensor_mul(sgn, sgn, ab)
                v.tensor_sub(sgn, sgn, vb[:])
                v.scalar_tensor_tensor(vb[:], sgn, rowm[:, 0:1], vb[:],
                                       op0=ALU.mult, op1=ALU.add)
            m_vb = tot("chv")

            @blk.gpsimd
            def _(g):
                g.wait_ge(sem("chv"), m_vb)
                inc("dma_gr", g.dma_start(garow[:], vb[2:3, :]), 16)
                inc("dma_gr", g.dma_start(gbrow[:], vb[5:6, :]), 16)
                inc("dma_gr", g.dma_start(cbrow[:], vb[4:5, :]), 16)
                inc("dma_gr", g.dma_start(sbrow[:], vb[3:4, :]), 16)
                for i, row in enumerate((1, 0, 4, 3)):
                    if i > 0:
                        wait(g, "mm_vr", i)
                    inc("dma_vr", g.dma_start(vrow[:], vb[row:row + 1, :]), 16)

            @blk.tensor
            def _(tn):
                for i in range(4):
                    tn.wait_ge(sem("dma_vr"), 16 * (i + 1))
                    for dc in range(16):
                        ins = tn.transpose(pbank[2][:, i * 16 + dc:i * 16 + dc + 1],
                                           vrow[:, dc * 128:(dc + 1) * 128],
                                           identf[0:1, 0:1])
                    inc("mm_vr", ins)
                wait(tn, "dma_gr")
                for kgrp in range(16):
                    i, jg = divmod(kgrp, 4)
                    if kgrp >= 2:
                        tn.wait_ge(sem("chv"), m_vb + 3 + kgrp - 1)
                    row = (garow, gbrow, cbrow, sbrow)[i]
                    inc("mm_gag", tn.matmul(pbank[3 + kgrp % 2][:],
                                            onesr_b[:],
                                            row[:, jg * 512:(jg + 1) * 512],
                                            start=True, stop=True))

            @blk.vector
            def _(v):
                v = P.chain(v, "chv")
                wait(v, "mm_vr")
                v.tensor_copy(mparm[:], pbank[2][:, 0:64])
                v.tensor_scalar_add(mparm[:, 0:16], mparm[:, 0:16], 1.0)
                v.tensor_scalar_add(mparm[:, 32:48], mparm[:, 32:48], 1.0)
                assert tot("chv") == m_vb + 3
                for kgrp in range(16):
                    i, jg = divmod(kgrp, 4)
                    wait(v, "mm_gag", kgrp + 1)
                    dst = (gag[:, jg * 512:(jg + 1) * 512],
                           gag[:, D + jg * 512:D + (jg + 1) * 512],
                           cbb[:, jg * 512:(jg + 1) * 512],
                           sbb[:, jg * 512:(jg + 1) * 512])[i]
                    v.tensor_copy(dst, pbank[3 + kgrp % 2][:])
                v.tensor_scalar_add(cbb[:], cbb[:], 1.0)

        if stage == 1:
            with nc.Block() as blk:
                @blk.sync
                def _(s):
                    s.wait_ge(sem("chv"), tot("chv"))
                    inc("dma_o", s.dma_start(dbg["vbo"].ap(), vb.ap()), 16)
                    wait(s, "dma_o")

    bigA = sbt("bigA", [128, NH, T], BF16)
    bigB = sbt("bigB", [128, NH, T], BF16)
    bigC = sbt("bigC", [128, NH, T], BF16)
    bigD = sbt("bigD", [128, NH, T], BF16)

    h1t, q_fm, k_fm = bigA, bigB, bigC
    v_tm = bigD.ap().rearrange("p a t -> p (a t)").rearrange("p (tc o) -> p tc o", tc=8)

    with contextlib.ExitStack() as pst:
        xt = [sbt(f"xt{i}", [128, D], F32, pst) for i in range(2)]
        xnt = [sbt(f"xnt{i}", [128, D], BF16, pst) for i in range(2)]
        sqt = sbt("sqt", [128, D], F32, pst)
        veL0 = tot("chv")
        scL0 = tot("chs")

        def ln_xn(tc):
            return veL0 + 26 * tc + 10

        def ln_mod(tc):
            return veL0 + 26 * (tc + 1)

        with nc.Block() as blk:
            @blk.sync
            def _(s):
                for tc in range(8):
                    if tc >= 2:
                        s.wait_ge(sem("chv"), ln_xn(tc - 2))
                    inc(f"dma_x{tc % 2}", s.dma_start(
                        xt[tc % 2][:], x_d[tc * 128:(tc + 1) * 128, :]), 16)

            @blk.vector
            def _(v):
                v = P.chain(v, "chv")
                for tc in range(8):
                    v.wait_ge(sem(f"dma_x{tc % 2}"), 16 * (tc // 2 + 1))
                    x_ = xt[tc % 2]
                    mean, msq, var, rin = (stats[:, 4:5], stats[:, 5:6],
                                           stats[:, 6:7], stats[:, 7:8])
                    v.reduce_sum(mean, x_[:], axis=AX.X)
                    v.tensor_scalar_mul(mean, mean, 1.0 / D)
                    v.tensor_mul(sqt[:], x_[:], x_[:])
                    v.reduce_sum(msq, sqt[:], axis=AX.X)
                    v.tensor_scalar_mul(msq, msq, 1.0 / D)
                    v.tensor_mul(var, mean, mean)
                    v.tensor_sub(var, msq, var)
                    v.tensor_scalar_add(var, var, EPS)
                    v.reciprocal(rin, var)
                    v.wait_ge(sem("chs"), scL0 + tc + 1)
                    v.scalar_tensor_tensor(xnt[tc % 2][:], x_[:], mean,
                                           stats[:, 8:9].to_broadcast([128, D]),
                                           op0=ALU.subtract, op1=ALU.mult)
                    assert tot("chv") == veL0 + 26 * tc + 10
                    v.wait_ge(sem("mm_ln"), tc + 1)
                    for dc in range(16):
                        bk = (tc % 2) * 2 + dc // 8
                        v.scalar_tensor_tensor(
                            h1t[:, dc, tc * 128:(tc + 1) * 128],
                            pbank_bf(bk)[:, (dc % 8) * 128:(dc % 8 + 1) * 128],
                            mparm[:, dc:dc + 1],
                            mparm[:, 16 + dc:17 + dc].to_broadcast([128, 128]),
                            op0=ALU.mult, op1=ALU.add)
                    assert tot("chv") == ln_mod(tc)

            @blk.scalar
            def _(sc):
                sc = P.chain(sc, "chs")
                for tc in range(8):
                    sc.wait_ge(sem("chv"), veL0 + 26 * tc + 9)
                    sc.activation(stats[:, 8:9], stats[:, 7:8], AF.Sqrt)

            @blk.tensor
            def _(tn):
                for tc in range(8):
                    tn.wait_ge(sem("chv"), ln_xn(tc))
                    if tc >= 2:
                        tn.wait_ge(sem("chv"), ln_mod(tc - 2))
                    for dc in range(16):
                        bk = (tc % 2) * 2 + dc // 8
                        ins = tn.transpose(
                            pbank_bf(bk)[:, (dc % 8) * 128:(dc % 8 + 1) * 128],
                            xnt[tc % 2][:, dc * 128:(dc + 1) * 128], identb[:])
                    inc("mm_ln", ins)

    m_h1 = tot("chv")

    if stage == 1:
        with nc.Block() as blk:
            @blk.sync
            def _(s):
                s.wait_ge(sem("chv"), m_h1)
                inc("dma_o", s.dma_start(dbg["h1t"].ap(), h1t.ap()), 16)
                inc("dma_o", s.dma_start(dbg["mparm"].ap(), mparm.ap()), 16)
                wait(s, "dma_o")
        st.close()
        return nc, dbg

    with contextlib.ExitStack() as pst:
        wstr = sbt("wstr", [128, 2, 16, 512], BF16, pst)
        cosq = sbt("cosq_s", [128, T], BF16, pst)
        sinq = sbt("sinq_s", [128, T], BF16, pst)
        cosk = sbt("cosk_s", [128, T], BF16, pst)
        sink = sbt("sink_s", [128, T], BF16, pst)
        rot_sb = sbt("rot_sb", [128, T], BF16, pst)
        tmp1 = sbt("tmp1", [128, T], BF16, pst)
        tmp2 = sbt("tmp2", [128, T], BF16, pst)

        scB = tot("chs")
        h1_ready = m_h1

        with nc.Block() as blk:
            @blk.sync
            def _(s):
                for t_, d_ in ((cosq, cosq_d), (sinq, sinq_d), (cosk, cosk_d), (sink, sink_d)):
                    inc("dma_rt", s.dma_start(t_[:], d_[:]), 16)

            @blk.gpsimd
            def _(g):
                for og in range(12):
                    if og >= 2:
                        wait(g, "mm_qkv", 8 * (og - 1))
                    inc(f"dma_q{og % 2}", g.dma_start(
                        wstr[:, og % 2, :, :],
                        wqkvT_d.ap().rearrange("(dc p) o -> p dc o", p=128)
                        [:, :, og * 512:(og + 1) * 512]), 16)

            @blk.tensor
            def _(tn):
                tn.wait_ge(sem("chv"), h1_ready)
                for og in range(12):
                    tn.wait_ge(sem(f"dma_q{og % 2}"), 16 * (og // 2 + 1))
                    for k in range(8):
                        if og >= 1:
                            tn.wait_ge(sem("chs"), scB + 8 * (og - 1) + k + 1)
                        if og < 8:
                            oc, th = divmod(k, 2)
                            for dc in range(16):
                                ins = tn.matmul(
                                    pbank[k][:],
                                    wstr[:, og % 2, dc, oc * 128:(oc + 1) * 128],
                                    h1t[:, dc, th * 512:(th + 1) * 512],
                                    start=(dc == 0), stop=(dc == 15))
                        else:
                            tc = k
                            for dc in range(16):
                                ins = tn.matmul(
                                    pbank[k][:],
                                    h1t[:, dc, tc * 128:(tc + 1) * 128],
                                    wstr[:, og % 2, dc, :],
                                    start=(dc == 0), stop=(dc == 15))
                        inc("mm_qkv", ins)

            @blk.scalar
            def _(sc):
                sc = P.chain(sc, "chs")
                for og in range(12):
                    for k in range(8):
                        sc.wait_ge(sem("mm_qkv"), 8 * og + k + 1)
                        if og < 8:
                            oc, th = divmod(k, 2)
                            head = (og % 4) * 4 + oc
                            dest = (q_fm if og < 4 else k_fm)[:, head, th * 512:(th + 1) * 512]
                        else:
                            dest = v_tm[:, k, (og - 8) * 512:(og - 7) * 512]
                        sc.copy(dest, pbank[k][:])
            assert tot("chs") == scB + 96

        with nc.Block() as blk:
            veR = tot("chv")
            scR = tot("chs")

            @blk.tensor
            def _(tn):
                wait(tn, "dma_rt")
                for hh in range(32):
                    tgt = q_fm if hh < 16 else k_fm
                    h = hh % 16
                    for th in range(2):
                        bk = (hh % 2) * 2 + th
                        if hh >= 2:
                            tn.wait_ge(sem("chs"), scR + 2 * (hh - 1))
                        ins = tn.matmul(pbank[bk][:], rotT[:],
                                        tgt[:, h, th * 512:(th + 1) * 512],
                                        start=True, stop=True)
                    inc("mm_rope", ins)

            @blk.scalar
            def _(sc):
                sc = P.chain(sc, "chs")
                for hh in range(32):
                    sc.wait_ge(sem("mm_rope"), hh + 1)
                    if hh >= 1:
                        sc.wait_ge(sem("chv"), veR + 3 * hh)
                    for th in range(2):
                        bk = (hh % 2) * 2 + th
                        sc.copy(rot_sb[:, th * 512:(th + 1) * 512], pbank[bk][:])

            @blk.vector
            def _(v):
                v = P.chain(v, "chv")
                for hh in range(32):
                    tgt = q_fm if hh < 16 else k_fm
                    co = cosq if hh < 16 else cosk
                    si = sinq if hh < 16 else sink
                    h = hh % 16
                    v.wait_ge(sem("chs"), scR + 2 * (hh + 1))
                    v.tensor_mul(tmp1[:], tgt[:, h, :], co[:])
                    v.tensor_mul(tmp2[:], rot_sb[:], si[:])
                    v.tensor_add(tgt[:, h, :], tmp1[:], tmp2[:])
                    assert tot("chv") == veR + 3 * (hh + 1)

        m_rope_v = tot("chv")
        m_qkv_sc = tot("chs")
        veR3 = m_rope_v - 96

    if stage == 2:
        with nc.Block() as blk:
            @blk.sync
            def _(s):
                s.wait_ge(sem("chv"), m_rope_v)
                s.wait_ge(sem("chs"), m_qkv_sc)
                inc("dma_o", s.dma_start(dbg["qf"].ap(), q_fm.ap()), 16)
                inc("dma_o", s.dma_start(dbg["kf"].ap(), k_fm.ap()), 16)
                inc("dma_o", s.dma_start(dbg["vt"].ap(), bigD.ap()), 16)
                wait(s, "dma_o")
        st.close()
        return nc, dbg

    with contextlib.ExitStack() as pst:
        sq_at = sbt("sq_at", [128, T], BF16, pst)
        P_sb2 = sbt("P_sb2", [128, 4, 512], BF16, pst)
        lin_sb = sbt("lin_sb", [1, 512], F32, pst)
        linb_sb = sbt("linb_sb", [128, 512], F32, pst)
        kn4 = sbt("kn4", [1, 64], F32, pst)

        attn_fm = bigA

        with nc.Block() as blk:
            scN = tot("chs")
            veN = tot("chv")

            @blk.scalar
            def _(sc):
                sc = P.chain(sc, "chs")
                sc.wait_ge(sem("chv"), m_rope_v)
                for hh in range(32):
                    if hh >= 1:
                        sc.wait_ge(sem("mm_nrm"), 2 * hh)
                    src = (k_fm if hh < 16 else q_fm)[:, hh % 16, :]
                    sc.activation(sq_at[:], src, AF.Square)

            @blk.tensor
            def _(tn):
                for hh in range(32):
                    tn.wait_ge(sem("chs"), scN + hh + 1)
                    if hh >= 1:
                        tn.wait_ge(sem("chv"), veN + hh)
                    for th in range(2):
                        ins = tn.matmul(pbank[6 + hh % 2][0:1, :], onescol_b[:],
                                        sq_at[:, th * 512:(th + 1) * 512],
                                        start=(th == 0), stop=(th == 1))
                    inc("mm_nrm", ins, 2)

            @blk.vector
            def _(v):
                v = P.chain(v, "chv")
                for hh in range(32):
                    v.wait_ge(sem("mm_nrm"), 2 * (hh + 1))
                    col = hh if hh < 16 else 16 + (hh - 16)
                    v.reduce_max(kn[0:1, col:col + 1], pbank[6 + hh % 2][0:1, :], axis=AX.X)
            m_nrm = tot("chv")

            @blk.gpsimd
            def _(g):
                grp = [[0, 1, 2, 3], [4, 5, 6, 7]]
                g.wait_ge(sem("chs"), m_qkv_sc)
                inc("dma_kv", g.dma_start(
                    vin.ap().rearrange("(tc p) o -> p tc o", p=128), v_tm), 16)
                g.wait_ge(sem("dma_kv"), 16)
                inc("cc_kv", g.collective_compute(
                    "AllGather", ALU.bypass, replica_groups=grp,
                    ins=[vin.ap().opt()], outs=[vall.ap().opt()]))
                g.wait_ge(sem("chv"), veR3 + 3 * 24)
                inc("dma_kv", g.dma_start(
                    kin0.ap().rearrange("(dc p) t -> p dc t", p=128),
                    k_fm.ap()[:, 0:8, :]), 16)
                g.wait_ge(sem("dma_kv"), 32)
                inc("cc_kv", g.collective_compute(
                    "AllGather", ALU.bypass, replica_groups=grp,
                    ins=[kin0.ap().opt()], outs=[kall0.ap().opt()]))
                g.wait_ge(sem("chv"), veR3 + 3 * 32)
                inc("dma_kv", g.dma_start(
                    kin1.ap().rearrange("(dc p) t -> p dc t", p=128),
                    k_fm.ap()[:, 8:16, :]), 16)
                g.wait_ge(sem("dma_kv"), 48)
                inc("cc_kv", g.collective_compute(
                    "AllGather", ALU.bypass, replica_groups=grp,
                    ins=[kin1.ap().opt()], outs=[kall1.ap().opt()]))
                g.wait_ge(sem("chv"), m_nrm)
                inc("dma_kv", g.dma_start(knin.ap(), kn[0:1, 0:16]), 16)
                wait(g, "dma_kv")
                inc("cc_kv", g.collective_compute(
                    "AllGather", ALU.bypass, replica_groups=grp,
                    ins=[knin.ap().opt()], outs=[knall.ap().opt()]))
                wait(g, "cc_kv")
                inc("dma_kn", g.dma_start(kn4[:], knall.ap().rearrange("r h -> (r h)").unsqueeze(0)), 16)

            @blk.vector
            def _(v):
                v = P.chain(v, "chv")
                wait(v, "dma_kn")
                v.reduce_max(kn[0:1, 32:48],
                             kn4.ap().rearrange("a (r h) -> a h r", r=4), axis=AX.X)
                v.tensor_mul(kn[0:1, 48:64], kn[0:1, 32:48], kn[0:1, 16:32])
            m_prod = tot("chv")

            @blk.scalar
            def _(sc):
                sc = P.chain(sc, "chs")
                sc.wait_ge(sem("chv"), m_prod)
                sc.activation(kn[0:1, 48:64], kn[0:1, 48:64], AF.Sqrt)
            m_sqrtM = tot("chs")

            @blk.tensor
            def _(tn):
                tn.wait_ge(sem("chs"), m_sqrtM)
                inc("mm_mb", tn.matmul(pbank[6][:, 0:16], onesr_f[:], kn[0:1, 48:64],
                                       start=True, stop=True))

            @blk.vector
            def _(v):
                v = P.chain(v, "chv")
                wait(v, "mm_mb")
                v.tensor_scalar_mul(mbias[:], pbank[6][:, 0:16], -1.0 / CEXP)
            m_mbias = tot("chv")

        kvb = bigC.ap().rearrange("p a t -> p (a t)")
        lnv = pst.enter_context(nc.sbuf_tensor("lnv", [1, 512], F32))
        negmh = pst.enter_context(nc.sbuf_tensor("negmh", [1, 512], BF16))
        negmhf = pst.enter_context(nc.sbuf_tensor("negmhf", [1, 512], F32))
        negmb = pst.enter_context(nc.sbuf_tensor("negmb", [128, 512], F32))
        with nc.Block() as blk:
            attB = tot("mm_att") if "mm_att" in P.cnt else 0
            scA = tot("chs")
            veA = tot("chv")

            @blk.sync
            def _(s_):
                s_.wait_ge(sem("chv"), m_mbias)
                for h in range(16):
                    if h >= 2:
                        s_.wait_ge(sem("mm_att"), attB + 258 * (h - 1))
                    par = h % 2
                    kdst = kvb[:, par * 8192: par * 8192 + 4096].rearrange("p (r t) -> p r t", r=4)
                    kallh = kall0 if h < 8 else kall1
                    ksrc = kallh.ap().rearrange("(r row) t -> row r t", r=4)[
                        (h % 8) * 128:(h % 8 + 1) * 128, :, :]
                    inc(f"dma_K{par}", s_.dma_start(kdst, ksrc), 16)
                    vdst = kvb[:, par * 8192 + 4096: par * 8192 + 8192].rearrange(
                        "p (r j o) -> p r j o", r=4, j=8)
                    for r_ in range(4):
                        vsrc = vall.ap().rearrange("(r j p) o -> p r j o", r=4, j=8)[
                            :, r_:r_ + 1, :, h * 128:(h + 1) * 128]
                        inc(f"dma_V{par}", s_.dma_start(vdst[:, r_:r_ + 1, :, :], vsrc), 16)

            @blk.tensor
            def _(tn):
                for h in range(16):
                    par = h % 2
                    tn.wait_ge(sem(f"dma_K{par}"), 16 * (h // 2 + 1))
                    tn.wait_ge(sem(f"dma_V{par}"), 64 * (h // 2 + 1))
                    kslc = kvb[:, par * 8192: par * 8192 + 4096]
                    vslc = kvb[:, par * 8192 + 4096: par * 8192 + 8192]
                    for qh in range(2):
                        hq = h * 2 + qh
                        base = attB + 129 * hq
                        csc = scA + 65 * hq
                        def emit_S(kt):
                            if kt >= 2:
                                tn.wait_ge(sem("chs"), csc + kt - 1)
                            elif hq >= 1:
                                tn.wait_ge(sem("chs"), scA + 65 * (hq - 1) + 64 + kt)
                            ins = tn.matmul(pbank[kt % 2][:],
                                            kslc[:, kt * 128:(kt + 1) * 128],
                                            q_fm[:, h, qh * 512:(qh + 1) * 512],
                                            start=True, stop=True)
                            inc("mm_att", ins)

                        def emit_l16(kt):
                            tn.wait_ge(sem("chs"), csc + kt + 1)
                            if kt == 0 and hq >= 2:
                                tn.wait_ge(sem("chv"), veA + 39 * (hq - 2) + 37)
                            ins = tn.matmul(pbank[4 + qh][0:1, :], onescol_b[:],
                                            P_sb2[:, kt % 4, :],
                                            start=(kt == 0), stop=(kt == 31))
                            inc("mm_att", ins)

                        emit_S(0)
                        for kt in range(31):
                            emit_S(kt + 1)
                            emit_l16(kt)
                        emit_l16(31)

                        tn.wait_ge(sem("chv"), veA + 39 * hq + 3)
                        if hq >= 2:
                            tn.wait_ge(sem("chv"), veA + 39 * (hq - 2) + 38)
                        inc("mm_nb", tn.matmul(pbank[6 + qh][:], onesr_f[:], negmhf[:],
                                               start=True, stop=True))

                        def emit_SR(kt):
                            if kt == 0 and hq >= 2:
                                tn.wait_ge(sem("chv"), veA + 39 * (hq - 2) + 39)
                            if kt >= 2:
                                tn.wait_ge(sem("chs"), csc + 33 + kt - 1)
                            else:
                                tn.wait_ge(sem("chs"), csc + 31 + kt)
                            ins = tn.matmul(pbank[kt % 2][:],
                                            kslc[:, kt * 128:(kt + 1) * 128],
                                            q_fm[:, h, qh * 512:(qh + 1) * 512],
                                            start=True, stop=True)
                            inc("mm_att", ins)

                        def emit_lO(kt):
                            tn.wait_ge(sem("chs"), csc + 33 + kt + 1)
                            tn.matmul(pbank[4 + qh][0:1, :], onescol_b[:],
                                      P_sb2[:, kt % 4, :],
                                      start=(kt == 0), stop=(kt == 31))
                            ins = tn.matmul(pbank[2 + qh][:],
                                            vslc[:, kt * 128:(kt + 1) * 128],
                                            P_sb2[:, kt % 4, :],
                                            start=(kt == 0), stop=(kt == 31))
                            inc("mm_att", ins)

                        emit_SR(0)
                        for kt in range(31):
                            emit_SR(kt + 1)
                            emit_lO(kt)
                        emit_lO(31)
                        tn.wait_ge(sem("chv"), veA + 39 * hq + 37)
                        inc("mm_att", tn.matmul(pbank[6 + qh][:], onesr_f[:], lin_sb[:],
                                                start=True, stop=True))

            @blk.scalar
            def _(sc):
                sc = P.chain(sc, "chs")
                sc.wait_ge(sem("chv"), m_mbias)
                for h in range(16):
                    for qh in range(2):
                        hq = h * 2 + qh
                        base = attB + 129 * hq
                        for kt in range(32):
                            sc.wait_ge(sem("mm_att"), base + (1 if kt == 0 else 2 * kt))
                            if kt >= 4:
                                sc.wait_ge(sem("mm_att"), base + 2 * (kt - 4) + 3)
                            elif hq >= 1:
                                sc.wait_ge(sem("mm_att"),
                                           attB + 129 * (hq - 1) + 64 +
                                           (2 * (28 + kt) + 3 if kt < 3 else 64))
                            sc.activation(P_sb2[:, kt % 4, :], pbank[kt % 2][:], AF.Exp,
                                          bias=mbias[:, h:h + 1], scale=1.0 / CEXP)
                        sc.wait_ge(sem("mm_att"), base + 64)
                        sc.activation(lnv[:], pbank[4 + qh][0:1, :], AF.Ln)
                        for kt in range(32):
                            sc.wait_ge(sem("chv"), veA + 39 * hq + 5 + kt)
                            if kt >= 4:
                                sc.wait_ge(sem("mm_att"), base + 64 + 2 * (kt - 4) + 3)
                            else:
                                sc.wait_ge(sem("mm_att"),
                                           base + (2 * (28 + kt) + 3 if kt < 3 else 64))
                            sc.activation(P_sb2[:, kt % 4, :], pbank[kt % 2][:], AF.Exp,
                                          bias=0.0, scale=1.0)

            @blk.vector
            def _(v):
                v = P.chain(v, "chv")
                for h in range(16):
                    for qh in range(2):
                        hq = h * 2 + qh
                        base = attB + 129 * hq
                        csc = scA + 65 * hq
                        v.wait_ge(sem("chs"), csc + 33)
                        if hq >= 1:
                            v.wait_ge(sem("mm_nb"), hq)
                        v.tensor_scalar_mul(lnv[:], lnv[:], float(CEXP))
                        v.tensor_scalar(lnv[:], lnv[:], kn[0:1, 48 + h:49 + h], None,
                                        op0=ALU.add)
                        v.tensor_scalar_mul(negmhf[:], lnv[:], -1.0)
                        assert tot("chv") == veA + 39 * hq + 3
                        v.wait_ge(sem("mm_nb"), hq + 1)
                        v.tensor_copy(negmb[:], pbank[6 + qh][:])
                        for kt in range(32):
                            v.wait_ge(sem("mm_att"),
                                      base + 64 + (1 if kt == 0 else 2 * kt))
                            v.tensor_add(pbank[kt % 2][:], pbank[kt % 2][:],
                                         negmb[:])
                        assert tot("chv") == veA + 39 * hq + 36
                        v.wait_ge(sem("mm_att"), base + 128)
                        v.reciprocal(lin_sb[:], pbank[4 + qh][0:1, :])
                        v.wait_ge(sem("mm_att"), base + 129)
                        v.tensor_copy(linb_sb[:], pbank[6 + qh][:])
                        v.tensor_mul(attn_fm[:, h, qh * 512:(qh + 1) * 512],
                                     pbank[2 + qh][:], linb_sb[:])
                        assert tot("chv") == veA + 39 * (hq + 1)
            m_attn = tot("chv")

        wo0 = bigD.ap().rearrange("p a t -> p (a t)").rearrange("p (oc c) -> p oc c", oc=16)
        wo1 = bigC.ap().rearrange("p a t -> p (a t)").rearrange("p (oc c) -> p oc c", oc=16)
        nspill = 2 if stage == 3 else 1
        with nc.Block() as blk:
            veO = tot("chv")
            xl = [sbt(f"xl{i}", [128, D], F32, pst) for i in range(2)]
            x2t = [sbt(f"x2t{i}", [128, D], F32, pst) for i in range(2)]
            tmpf = sbt("tmpf", [128, 512], F32, pst)

            @blk.gpsimd
            def _(g):
                wait(g, "dma_kv")
                inc("dma_wo", g.dma_start(
                    wo0[:],
                    woutT_d.ap().rearrange("(oc p) d -> p oc d", p=128)[:, :, 0:1024]), 16)
                g.wait_ge(sem("chv"), m_attn)
                inc("dma_wo", g.dma_start(
                    wo1[:],
                    woutT_d.ap().rearrange("(oc p) d -> p oc d", p=128)[:, :, 1024:2048]), 16)

            @blk.sync
            def _(s):
                for tc in range(8):
                    if tc >= 2:
                        s.wait_ge(sem("chv"), veO + 8 * (tc - 1))
                    inc(f"dma_xr{tc % 2}", s.dma_start(xl[tc % 2][:],
                                                       x_d[tc * 128:(tc + 1) * 128, :]), 16)

            @blk.tensor
            def _(tn):
                tn.wait_ge(sem("chv"), m_attn)
                tn.wait_ge(sem("dma_wo"), 32)
                for tc in range(8):
                    for dq in range(4):
                        bk = (tc * 4 + dq) % 4
                        g_ = tc * 4 + dq
                        if g_ >= 4:
                            gp = g_ - 4
                            tn.wait_ge(sem("chv"), veO + 8 * (gp // 4) + 2 * (gp % 4) + 1)
                        wos = wo0 if dq < 2 else wo1
                        for oc in range(16):
                            ins = tn.matmul(pbank[bk][:],
                                            attn_fm[:, oc, tc * 128:(tc + 1) * 128],
                                            wos[:, oc, (dq % 2) * 512:(dq % 2 + 1) * 512],
                                            start=(oc == 0), stop=(oc == 15))
                        inc("mm_op", ins)

            @blk.vector
            def _(v):
                v = P.chain(v, "chv")
                for tc in range(8):
                    v.wait_ge(sem(f"dma_xr{tc % 2}"), 16 * (tc // 2 + 1))
                    if tc >= 2:
                        v.wait_ge(sem("dma_x2"), 16 * nspill * tc)
                    for dq in range(4):
                        bk = (tc * 4 + dq) % 4
                        v.wait_ge(sem("mm_op"), tc * 4 + dq + 1)
                        v.tensor_mul(tmpf[:], pbank[bk][:], gag[:, dq * 512:(dq + 1) * 512])
                        v.tensor_add(x2t[tc % 2][:, dq * 512:(dq + 1) * 512],
                                     xl[tc % 2][:, dq * 512:(dq + 1) * 512], tmpf[:])
                    assert tot("chv") == veO + 8 * tc + 8
            m_x2 = tot("chv")

            @blk.gpsimd
            def _(g2):
                for tc in range(8):
                    g2.wait_ge(sem("chv"), veO + 8 * (tc + 1))
                    inc("dma_x2", g2.dma_start(x2spill.ap()[tc * 128:(tc + 1) * 128, :],
                                               x2t[tc % 2][:]), 16)
                    if stage == 3:
                        inc("dma_x2", g2.dma_start(dbg["x2"].ap()[tc * 128:(tc + 1) * 128, :],
                                                   x2t[tc % 2][:]), 16)

    if stage == 3:
        with nc.Block() as blk:
            @blk.sync
            def _(s):
                wait(s, "dma_x2")
        st.close()
        return nc, dbg

    h2fm = bigA
    with contextlib.ExitStack() as pst:
        zl = [pst.enter_context(nc.sbuf_tensor(f"zl{i}", [128, D], F32)) for i in range(2)]
        znt = [pst.enter_context(nc.sbuf_tensor(f"znt{i}", [128, D], BF16)) for i in range(2)]
        h2t_t = [pst.enter_context(nc.sbuf_tensor(f"h2t{i}", [128, D], BF16)) for i in range(2)]
        sq2 = pst.enter_context(nc.sbuf_tensor("sq2", [128, D], F32))
        veL2 = tot("chv")
        scL2 = tot("chs")

        def l2_xn(tc):
            return veL2 + 28 * tc + 10

        def l2_tm(tc):
            return veL2 + 28 * tc + 12

        def l2_mod(tc):
            return veL2 + 28 * (tc + 1)

        with nc.Block() as blk:
            d_x2_all = tot("dma_x2")

            @blk.sync
            def _(s_):
                s_.wait_ge(sem("dma_x2"), d_x2_all)
                for tc in range(8):
                    if tc >= 2:
                        s_.wait_ge(sem("chv"), l2_xn(tc - 2))
                    inc(f"dma_z{tc % 2}", s_.dma_start(
                        zl[tc % 2][:], x2spill.ap()[tc * 128:(tc + 1) * 128, :]), 16)

            @blk.vector
            def _(v):
                v = P.chain(v, "chv")
                v.wait_ge(sem("dma_x2"), d_x2_all)
                for tc in range(8):
                    v.wait_ge(sem(f"dma_z{tc % 2}"), 16 * (tc // 2 + 1))
                    x_ = zl[tc % 2]
                    mean, msq, var, rin = (stats[:, 4:5], stats[:, 5:6],
                                           stats[:, 6:7], stats[:, 7:8])
                    v.reduce_sum(mean, x_[:], axis=AX.X)
                    v.tensor_scalar_mul(mean, mean, 1.0 / D)
                    v.tensor_mul(sq2[:], x_[:], x_[:])
                    v.reduce_sum(msq, sq2[:], axis=AX.X)
                    v.tensor_scalar_mul(msq, msq, 1.0 / D)
                    v.tensor_mul(var, mean, mean)
                    v.tensor_sub(var, msq, var)
                    v.tensor_scalar_add(var, var, EPS)
                    v.reciprocal(rin, var)
                    v.wait_ge(sem("chs"), scL2 + tc + 1)
                    v.scalar_tensor_tensor(znt[tc % 2][:], x_[:], mean,
                                           stats[:, 8:9].to_broadcast([128, D]),
                                           op0=ALU.subtract, op1=ALU.mult)
                    assert tot("chv") == l2_xn(tc)
                    if tc >= 2:
                        v.wait_ge(sem(f"dma_h2{tc % 2}"), 16 * (tc // 2))
                    v.tensor_mul(h2t_t[tc % 2][:], znt[tc % 2][:], cbb[:])
                    v.tensor_add(h2t_t[tc % 2][:], h2t_t[tc % 2][:], sbb[:])
                    assert tot("chv") == l2_tm(tc)
                    v.wait_ge(sem("mm_l2"), tc + 1)
                    for dc in range(16):
                        bk = (tc % 2) * 2 + dc // 8
                        v.scalar_tensor_tensor(
                            h2fm[:, dc, tc * 128:(tc + 1) * 128],
                            pbank_bf(bk)[:, (dc % 8) * 128:(dc % 8 + 1) * 128],
                            mparm[:, 32 + dc:33 + dc],
                            mparm[:, 48 + dc:49 + dc].to_broadcast([128, 128]),
                            op0=ALU.mult, op1=ALU.add)
                    assert tot("chv") == l2_mod(tc)

            @blk.scalar
            def _(sc):
                sc = P.chain(sc, "chs")
                sc.wait_ge(sem("chv"), m_x2)
                for tc in range(8):
                    sc.wait_ge(sem("chv"), veL2 + 28 * tc + 9)
                    sc.activation(stats[:, 8:9], stats[:, 7:8], AF.Sqrt)

            @blk.tensor
            def _(tn):
                for tc in range(8):
                    tn.wait_ge(sem("chv"), l2_xn(tc))
                    if tc >= 2:
                        tn.wait_ge(sem("chv"), l2_mod(tc - 2))
                    for dc in range(16):
                        bk = (tc % 2) * 2 + dc // 8
                        ins = tn.transpose(
                            pbank_bf(bk)[:, (dc % 8) * 128:(dc % 8 + 1) * 128],
                            znt[tc % 2][:, dc * 128:(dc + 1) * 128], identb[:])
                    inc("mm_l2", ins)

            @blk.gpsimd
            def _(g):
                for tc in range(8):
                    g.wait_ge(sem("chv"), l2_tm(tc))
                    inc(f"dma_h2{tc % 2}", g.dma_start(h2in.ap()[tc * 128:(tc + 1) * 128, :],
                                                       h2t_t[tc % 2][:]), 16)
                wait(g, "dma_h20")
                wait(g, "dma_h21")
                inc("cc_h2", g.collective_compute(
                    "AllGather", ALU.bypass, replica_groups=[list(range(8))],
                    ins=[h2in.ap().opt()], outs=[h2all.ap().opt()]))
        m_h2 = tot("chv")

    aff7 = sbt("aff7", [E, T], F32)
    afftm = sbt("afftm", [128, 8, E], F32)
    with contextlib.ExitStack() as pst:
        gw_sb = pst.enter_context(nc.sbuf_tensor("gw_sb", [128, 16, E], BF16))
        with nc.Block() as blk:
            scG = tot("chs")
            veGT = tot("chv")

            @blk.gpsimd
            def _(g):
                inc("dma_gw", g.dma_start(
                    gw_sb.ap(), gatew_d.ap().rearrange("(dc p) e -> p dc e", p=128)), 16)

            @blk.tensor
            def _(tn):
                tn.wait_ge(sem("chv"), m_h2)
                wait(tn, "dma_gw")
                for th in range(2):
                    for dc in range(16):
                        ins = tn.matmul(pbank[th][0:E, :], gw_sb[:, dc, :],
                                        h2fm[:, dc, th * 512:(th + 1) * 512],
                                        start=(dc == 0), stop=(dc == 15))
                    inc("mm_g", ins)
                for tc in range(8):
                    tn.wait_ge(sem("chs"), scG + 2)
                    if tc >= 2:
                        tn.wait_ge(sem("chv"), veGT + tc - 1)
                    ins = tn.transpose(pbank[2 + tc % 2][:, 0:E],
                                       aff7[:, tc * 128:(tc + 1) * 128], identf[0:E, 0:E])
                    inc("mm_gt", ins)

            @blk.scalar
            def _(sc):
                sc = P.chain(sc, "chs")
                for th in range(2):
                    sc.wait_ge(sem("mm_g"), th + 1)
                    sc.activation(aff7[:, th * 512:(th + 1) * 512], pbank[th][0:E, :],
                                  AF.Sigmoid, bias=ebias_sb[:])

            @blk.vector
            def _(v):
                v = P.chain(v, "chv")
                for tc in range(8):
                    v.wait_ge(sem("mm_gt"), tc + 1)
                    v.tensor_copy(afftm[:, tc, :], pbank[2 + tc % 2][:, 0:E])

            @blk.gpsimd
            def _(g):
                g.wait_ge(sem("chs"), scG + 2)
                inc("dma_af", g.dma_start(
                    affin.ap().rearrange("(e a) c -> e a c", a=16),
                    aff7.ap().rearrange("e (a c) -> e a c", a=16)), 16)
                wait(g, "dma_af")
                inc("cc_af", g.collective_compute(
                    "AllGather", ALU.bypass, replica_groups=[list(range(8))],
                    ins=[affin.ap().opt()], outs=[affall.ap().opt()]))
        m_afftm = tot("chv")

    CAPK = 1169
    olist = sbt("olist", [16, 128], I16)
    th_sb = sbt("th_sb", [1, 16], F32)
    idx_e = sbt("idx_e", [16, 7 * 80], I16)
    with contextlib.ExitStack() as pst:
        def rt(name):
            return pst.enter_context(nc.sbuf_tensor(name, [16, 512], F32))
        affs = [rt(f"affs{e}") for e in range(E)]
        m1r = rt("m1r"); m2r = rt("m2r"); sumeq = rt("sumeq")
        e1r = rt("e1r"); e2r = rt("e2r")
        acc1 = rt("acc1"); acc2 = rt("acc2"); k1acc = rt("k1acc"); k2acc = rt("k2acc")
        tA = rt("tA"); tB = rt("tB"); tC = rt("tC"); tD = rt("tD")
        keepb = pst.enter_context(nc.sbuf_tensor("keepb", [16, 512], BF16))
        vali = pst.enter_context(nc.sbuf_tensor("vali", [16, 512], I16))
        csrow = pst.enter_context(nc.sbuf_tensor("csrow", [1, 512 * 3], F32))
        sc128 = pst.enter_context(nc.sbuf_tensor("sc128", [128, 64], F32))
        thb16 = pst.enter_context(nc.sbuf_tensor("thb16", [16, E], F32))
        thr3 = pst.enter_context(nc.sbuf_tensor("thr3", [1, 8], F32))
        thr8 = pst.enter_context(nc.sbuf_tensor("thr8", [1, 8], F32))
        lohi = pst.enter_context(nc.sbuf_tensor("lohi", [16, 2], I32))
        midt = pst.enter_context(nc.sbuf_tensor("midt", [16, 1], I32))
        bisf = pst.enter_context(nc.sbuf_tensor("bisf", [16, 2], F32))
        bisp = pst.enter_context(nc.sbuf_tensor("bisp", [16, 2], I16))
        ones16 = pst.enter_context(nc.sbuf_tensor("ones16_sb", [16, 16], F32))
        sck = pst.enter_context(nc.sbuf_tensor("sck", [128, 64], F32))
        tE128 = pst.enter_context(nc.sbuf_tensor("tE128", [128, 64], F32))
        own16 = pst.enter_context(nc.sbuf_tensor("own16", [16, 2 * 64], I16))
        tBi = pst.enter_context(nc.sbuf_tensor("tBi", [16, 512], I16))

        with nc.Block() as blk:
            @blk.gpsimd
            def _(g):
                wait(g, "cc_af")
                for e in range(E):
                    inc("dma_as", g.dma_start(
                        affs[e].ap().rearrange("a (r c) -> a r c", r=8),
                        affall.ap().rearrange("(r e a) c -> e a r c", r=8, e=E)[e]), 16)

            @blk.vector
            def _(v):
                v = P.chain(v, "chv")
                wait(v, "dma_as")
                wait(v, "dma_z0")
                wait(v, "dma_z1")
                wait(v, "dma_gw")
                mins = v.memset(th_sb[:], 0.0)
                P.inc("chv", mins)
                v.wait_ge(sem("chv"), tot("chv"))
                v.tensor_max(m1r[:], affs[0][:], affs[1][:])
                for e in range(2, E):
                    v.tensor_max(m1r[:], m1r[:], affs[e][:])
                for e in range(E):
                    v.tensor_tensor(tA[:], affs[e][:], m1r[:], op=ALU.is_equal)
                    if e == 0:
                        v.tensor_copy(sumeq[:], tA[:])
                        v.tensor_scalar_mul(e1r[:], tA[:], float(E - e))
                    else:
                        v.tensor_add(sumeq[:], sumeq[:], tA[:])
                        v.tensor_scalar_mul(tB[:], tA[:], float(E - e))
                        v.tensor_max(e1r[:], e1r[:], tB[:])
                for e in range(E):
                    v.tensor_tensor(tA[:], affs[e][:], m1r[:], op=ALU.is_equal)
                    v.tensor_scalar_mul(tA[:], tA[:], -1e30)
                    v.tensor_add(tA[:], tA[:], affs[e][:])
                    if e == 0:
                        v.tensor_copy(m2r[:], tA[:])
                    else:
                        v.tensor_max(m2r[:], m2r[:], tA[:])
                v.tensor_single_scalar(tB[:], sumeq[:], 2.0, op=ALU.is_ge)
                v.tensor_copy(tBi[:], tB[:])
                v.copy_predicated(m2r[:], tBi[:], m1r[:])
                for e in range(E):
                    v.tensor_tensor(tA[:], affs[e][:], m1r[:], op=ALU.is_equal)
                    v.tensor_scalar_mul(tA[:], tA[:], float(E - e))
                    v.tensor_tensor(tC[:], tA[:], e1r[:], op=ALU.is_equal)
                    v.tensor_scalar(tC[:], tC[:], -1.0, 1.0, op0=ALU.mult, op1=ALU.add)
                    v.tensor_mul(tA[:], tA[:], tC[:])
                    if e == 0:
                        v.tensor_copy(tD[:], tA[:])
                    else:
                        v.tensor_max(tD[:], tD[:], tA[:])
                for e in range(E):
                    v.tensor_tensor(tA[:], affs[e][:], m2r[:], op=ALU.is_equal)
                    v.tensor_scalar_mul(tA[:], tA[:], float(E - e))
                    if e == 0:
                        v.tensor_copy(tC[:], tA[:])
                    else:
                        v.tensor_max(tC[:], tC[:], tA[:])
                v.copy_predicated(tC[:], tBi[:], tD[:])
                v.tensor_scalar(e2r[:], tC[:], -1.0, float(E), op0=ALU.mult, op1=ALU.add)
                v.tensor_scalar(e1r[:], e1r[:], -1.0, float(E), op0=ALU.mult, op1=ALU.add)
            m_top2 = tot("chv")

            @blk.vector
            def _(v):
                v = P.chain(v, "chv")
                for e in range(E):
                    v.tensor_tensor(tA[:], affs[e][:], m2r[:], op=ALU.is_ge)
                    v.tensor_scalar_add(tB[:], affs[e][:], 1.0)
                    v.tensor_mul(tB[:], tA[:], tB[:])
                    v.tensor_scalar(tB[:], tB[:], eoh[:, e:e + 1], None, op0=ALU.mult)
                    if e == 0:
                        v.tensor_copy(tD[:], tB[:])
                    else:
                        v.tensor_add(tD[:], tD[:], tB[:])
                v.tensor_scalar_add(tD[:], tD[:], -1.0)
            m_sco = tot("chv")

            @blk.gpsimd
            def _(g):
                inc("dma_th", g.dma_start(lohi[:], bisinit_d.ap()), 16)
                inc("dma_th", g.dma_start(ones16[:], ones16_d.ap()), 16)
                g.wait_ge(sem("chv"), m_sco + 10 * 30)
                inc("dma_th", g.dma_start(thin.ap(), lohi[0:1, 0:1].bitcast(F32)), 16)
                g.wait_ge(sem("dma_th"), 48)
                inc("cc_th", g.collective_compute(
                    "AllGather", ALU.bypass, replica_groups=[list(range(8))],
                    ins=[thin.ap().opt()], outs=[thall.ap().opt()]))
                wait(g, "cc_th")
                inc("dma_th", g.dma_start(thr8[:], bass.AP(thall, 0, [[8, 1], [1, 8]])), 16)

            @blk.tensor
            def _(tn):
                tn.wait_ge(sem("dma_th"), 32)
                for it in range(30):
                    tn.wait_ge(sem("chv"), m_sco + 10 * it + 4)
                    ins = tn.matmul(pbank[5][0:16, 0:1], ones16[:],
                                    bisf[:, 0:1], start=True, stop=True)
                    inc("mm_bis", ins)

            @blk.vector
            def _(v):
                v = P.chain(v, "chv")
                v.wait_ge(sem("dma_th"), 32)
                for it in range(30):
                    v.tensor_tensor(midt[:], lohi[:, 0:1], lohi[:, 1:2], op=ALU.add)
                    v.tensor_single_scalar(midt[:], midt[:], 1,
                                           op=ALU.logical_shift_right)
                    v.tensor_scalar(tA[:], tD[:], midt[:, 0:1].bitcast(F32), None,
                                    op0=ALU.is_ge)
                    v.reduce_sum(bisf[:, 0:1], tA[:], axis=AX.X)
                    v.wait_ge(sem("mm_bis"), it + 1)
                    v.tensor_single_scalar(bisf[:, 0:1], pbank[5][0:16, 0:1], 1170.0,
                                           op=ALU.is_ge)
                    v.tensor_copy(bisp[:, 0:1], bisf[:, 0:1])
                    v.tensor_single_scalar(bisf[:, 1:2], bisf[:, 0:1], 0.0,
                                           op=ALU.is_equal)
                    v.tensor_copy(bisp[:, 1:2], bisf[:, 1:2])
                    v.copy_predicated(lohi[:, 0:1], bisp[:, 0:1], midt[:])
                    v.copy_predicated(lohi[:, 1:2], bisp[:, 1:2], midt[:])
                    assert tot("chv") == m_sco + 10 * (it + 1)
                v.wait_ge(sem("dma_th"), 64)
                v.tensor_copy(bass.AP(th_sb, 0, [[16, 1], [2, E]]), thr8[0:1, 0:E])
            m_kth = tot("chv")

            @blk.tensor
            def _(tn):
                tn.wait_ge(sem("chv"), m_kth)
                for e in range(E):
                    ins = tn.matmul(pbank[0][0:16, e:e + 1], onesr_f[:, 0:16],
                                    th_sb[0:1, 2 * e:2 * e + 1], start=True, stop=True)
                inc("mm_thb", ins)
                ins = tn.matmul(pbank[1][:, 0:E], onesr_f[:],
                                th_sb[0:1, 0:2 * E:2], start=True, stop=True)
                inc("mm_thb", ins)

            @blk.vector
            def _(v):
                v = P.chain(v, "chv")
                wait(v, "mm_thb")
                v.tensor_copy(thb16[:], pbank[0][0:16, 0:E])
                v.tensor_copy(thtm[:], pbank[1][:, 0:E])
            m_thb = tot("chv")

            @blk.vector
            def _(v):
                v = P.chain(v, "chv")
                for e in range(E):
                    v.tensor_tensor(tA[:], affs[e][:], m2r[:], op=ALU.is_ge)
                    v.tensor_scalar(tB[:], affs[e][:], thb16[:, e:e + 1], None, op0=ALU.is_ge)
                    v.tensor_mul(tA[:], tA[:], tB[:])
                    if e >= 1:
                        wait(v, "gp_sp", e)
                    v.tensor_mul(tB[:], tA[:], gmapp1[:])
                    v.tensor_scalar_add(tB[:], tB[:], -1.0)
                    v.tensor_copy(vali[:], tB[:])
                    m_val = tot("chv")
                    v.tensor_copy(keepb[:], tA[:])
                    m_keepb = tot("chv")
                    v.wait_ge(sem("mm_pos"), 2 * e + 1)
                    v.tensor_copy(csrow[0:1, 0:512], pbank[3][0:1, :])
                    v.tensor_tensor_scan(csrow[0:1, 512:1024], csrow[0:1, 0:512],
                                         csrow[0:1, 0:512], 0.0,
                                         op0=ALU.add, op1=ALU.bypass)
                    v.tensor_sub(csrow[0:1, 1024:1536], csrow[0:1, 512:1024],
                                 csrow[0:1, 0:512])
                    m_excl = tot("chv")
                    v.wait_ge(sem("mm_pos"), 2 * e + 2)
                    v.tensor_copy(tB[:], pbank[2][0:16, :])
                    v.wait_ge(sem("mm_pos2"), e + 1)
                    v.tensor_copy(tC[:], pbank[4][0:16, :])
                    v.tensor_add(tB[:], tB[:], tC[:])
                    v.tensor_single_scalar(tC[:], e1r[:], float(e), op=ALU.is_equal)
                    v.tensor_single_scalar(tD[:], e2r[:], float(e), op=ALU.is_equal)
                    if e == 0:
                        v.tensor_mul(acc1[:], tC[:], tB[:])
                        v.tensor_mul(acc2[:], tD[:], tB[:])
                        v.tensor_mul(k1acc[:], tC[:], tA[:])
                        v.tensor_mul(k2acc[:], tD[:], tA[:])
                    else:
                        v.tensor_mul(m1r[:], tC[:], tB[:])
                        v.tensor_add(acc1[:], acc1[:], m1r[:])
                        v.tensor_mul(m1r[:], tD[:], tB[:])
                        v.tensor_add(acc2[:], acc2[:], m1r[:])
                        v.tensor_mul(m1r[:], tC[:], tA[:])
                        v.tensor_add(k1acc[:], k1acc[:], m1r[:])
                        v.tensor_mul(m1r[:], tD[:], tA[:])
                        v.tensor_add(k2acc[:], k2acc[:], m1r[:])
                    P.marks[f"val{e}"] = m_val
                    P.marks[f"keepb{e}"] = m_keepb
                    P.marks[f"excl{e}"] = m_excl
                v.wait_ge(sem("gp_sp"), E)
                v.tensor_scalar_mul(tA[:], e1r[:], float(CAPP))
                v.tensor_add(tA[:], tA[:], acc1[:])
                v.tensor_mul(tA[:], tA[:], k1acc[:])
                v.tensor_scalar_min(tA[:], tA[:], float(NSLOT - 1))
                v.tensor_scalar_mul(tB[:], e2r[:], float(CAPP))
                v.tensor_add(tB[:], tB[:], acc2[:])
                v.tensor_mul(tB[:], tB[:], k2acc[:])
                v.tensor_scalar_min(tB[:], tB[:], float(NSLOT - 1))
                v.tensor_copy(vali[:], tA[:])
                v.tensor_copy(keepb.ap().bitcast(I16), tB[:])
            m_sidx = tot("chv")

            @blk.tensor
            def _(tn):
                for e in range(E):
                    tn.wait_ge(sem("chv"), P.marks[f"keepb{e}"])
                    if e >= 1:
                        tn.wait_ge(sem("chv"), P.marks[f"excl{e - 1}"] + 2)
                    ins = tn.matmul(pbank[3][0:1, :], onescol_b[0:16, 0:1], keepb[:],
                                    start=True, stop=True)
                    inc("mm_pos", ins)
                    ins = tn.matmul(pbank[2][0:16, :], u16s[:], keepb[:],
                                    start=True, stop=True)
                    inc("mm_pos", ins)
                    tn.wait_ge(sem("chv"), P.marks[f"excl{e}"])
                    ins = tn.matmul(pbank[4][0:16, :], onesr_f[:, 0:16],
                                    csrow[0:1, 1024:1536], start=True, stop=True)
                    inc("mm_pos2", ins)

            @blk.gpsimd
            def _(g):
                g.load_library(LIB_SG)
                for e in range(E):
                    g.wait_ge(sem("chv"), P.marks[f"val{e}"])
                    if e >= 1:
                        wait(g, "gp_sp", e)
                    inc("gp_sp", g.sparse_gather(idx_e[:, e * 80:(e + 1) * 80], vali[:],
                                                 num_found=nf_sb[:]))
                g.wait_ge(sem("chv"), m_sidx)
                pid = nc.partition_id()
                src1 = bass.AP(vali, pid * 64, [[512, 16], [1, 64]])
                src2 = bass.AP(keepb, pid * 64, [[512, 16], [1, 64]]).bitcast(I16)
                inc("dma_ow", g.dma_start(own16[:, 0:64], src1), 16)
                inc("dma_ow", g.dma_start(own16[:, 64:128], src2), 16)
                wait(g, "dma_ow")
                inc("dma_ol", g.dma_start(olscr.ap(), own16.ap()), 16)
                wait(g, "dma_ol")
                for k_ in range(2):
                    for a_ in range(16):
                        inc("dma_ol", g.dma_start(
                            bass.AP(olist, 64 * k_ + 4 * a_, [[128, 16], [1, 4]]),
                            bass.AP(olscr, 64 * k_ + 128 * a_, [[1, 16], [16, 4]]),
                            allow_slow_non_contiguous=True), 16)
        m_rt = tot("chv")

    with nc.Block() as blk:
        @blk.vector
        def _(v):
            v = P.chain(v, "chv")
            v.wait_ge(sem("chv"), m_thb)
            for tc in range(8):
                a_ = afftm[:, tc, :]
                m1t = stats[:, 0:1]
                m2t = stats[:, 3:4]
                v.reduce_max(m1t, a_, axis=AX.X)
                v.tensor_tensor(sq_rt1[:], a_, m1t.to_broadcast([128, E]), op=ALU.is_equal)
                v.reduce_sum(stats[:, 11:12], sq_rt1[:], axis=AX.X)
                v.tensor_single_scalar(stats[:, 12:13], stats[:, 11:12], 2.0, op=ALU.is_ge)
                v.tensor_scalar_mul(sq_rt2[:], sq_rt1[:], -1e30)
                v.tensor_add(sq_rt2[:], sq_rt2[:], a_)
                v.reduce_max(stats[:, 4:5], sq_rt2[:], axis=AX.X)
                v.tensor_copy(m2t, stats[:, 4:5])
                v.tensor_copy(manyi[:], stats[:, 12:13])
                v.copy_predicated(m2t, manyi[:], m1t)
                v.tensor_mul(sq_rt2[:], sq_rt1[:], irev[:])
                v.reduce_max(stats[:, 10:11], sq_rt2[:], axis=AX.X)
                v.tensor_tensor(sq_rt1[:], sq_rt2[:], stats[:, 10:11].to_broadcast([128, E]),
                                op=ALU.is_equal)
                v.tensor_scalar(sq_rt3[:], sq_rt1[:], -1.0, 1.0, op0=ALU.mult, op1=ALU.add)
                v.tensor_mul(sq_rt2[:], sq_rt2[:], sq_rt3[:])
                v.reduce_max(stats[:, 13:14], sq_rt2[:], axis=AX.X)
                v.tensor_tensor(sq_rt3[:], a_, m2t.to_broadcast([128, E]), op=ALU.is_equal)
                v.tensor_mul(sq_rt3[:], sq_rt3[:], irev[:])
                v.reduce_max(stats[:, 14:15], sq_rt3[:], axis=AX.X)
                v.tensor_tensor(sq_rt2[:], sq_rt2[:], stats[:, 13:14].to_broadcast([128, E]),
                                op=ALU.is_equal)
                v.tensor_tensor(sq_rt3[:], sq_rt3[:], stats[:, 14:15].to_broadcast([128, E]),
                                op=ALU.is_equal)
                v.copy_predicated(sq_rt3[:], manyi[:].to_broadcast([128, E]), sq_rt2[:])
                v.tensor_mul(sq_rt1[:], sq_rt1[:], thtm[:])
                v.reduce_sum(stats[:, 5:6], sq_rt1[:], axis=AX.X)
                v.tensor_mul(sq_rt3[:], sq_rt3[:], thtm[:])
                v.reduce_sum(stats[:, 6:7], sq_rt3[:], axis=AX.X)
                v.tensor_tensor(stats[:, 7:8], m1t, stats[:, 5:6], op=ALU.is_ge)
                v.tensor_tensor(stats[:, 8:9], m2t, stats[:, 6:7], op=ALU.is_ge)
                v.tensor_mul(wownb[:, tc:tc + 1], stats[:, 7:8], m1t)
                v.tensor_mul(wownb[:, 8 + tc:9 + tc], stats[:, 8:9], m2t)
    m_wown = tot("chv")

    if stage == 4:
        with nc.Block() as blk:
            @blk.sync
            def _(s_):
                s_.wait_ge(sem("chv"), m_wown)
                wait(s_, "dma_ol")
                wait(s_, "dma_z0")
                wait(s_, "dma_z1")
                inc("dma_o", s_.dma_start(dbg["aff"].ap(), aff7.ap()), 16)
                inc("dma_o", s_.dma_start(dbg["theta"].ap(), th_sb.ap()), 16)
                inc("dma_o", s_.dma_start(dbg["sidx"].ap(), olist.ap()), 16)
                inc("dma_o", s_.dma_start(dbg["wown"].ap(), wownb.ap()), 16)
                inc("dma_o", s_.dma_start(dbg["h2t"].ap(), bigA.ap()), 16)
                wait(s_, "dma_o")
        st.close()
        return nc, dbg

    pst5 = contextlib.ExitStack()
    silb = sbt("silb", [128, 512], BF16, pst5)
    routsb = sbt("routsb", [128, 4, D], BF16, pst5)
    wstr5 = sbt("wstr5", [128, 2, 16, 512], BF16, pst5)
    x1g = bigB.ap().rearrange("p a t -> p (a t)").rearrange("p (fc t) -> p fc t", fc=16)
    shared_tm = bigC.ap().rearrange("p a t -> p (a t)").rearrange("p (tc d) -> p tc d", tc=8)
    with nc.Block() as blk:
        scS = tot("chs")
        veS = tot("chv")
        mmS = tot("mm_ff") if "mm_ff" in P.cnt else 0

        @blk.gpsimd
        def _(g):
            g.wait_ge(sem("chv"), veS)
            g.wait_ge(sem("chs"), scS)
            for nm in ("dma_ol", "dma_z0", "dma_z1", "gp_sp", "dma_o"):
                if nm in P.cnt:
                    wait(g, nm)
            for fq in range(4):
                if fq >= 2:
                    wait(g, "mm_ff", 8 * (fq - 1))
                inc(f"dma_w1{fq % 2}", g.dma_start(
                    wstr5[:, fq % 2, :, :],
                    w1s_d.ap().rearrange("(dc p) f -> p dc f", p=128)
                    [:, :, fq * 512:(fq + 1) * 512]), 16)

        @blk.tensor
        def _(tn):
            tn.wait_ge(sem("chv"), m_h2)
            for fq in range(4):
                tn.wait_ge(sem(f"dma_w1{fq % 2}"), 16 * (fq // 2 + 1))
                for k in range(8):
                    fc, th = divmod(k, 2)
                    if fq * 8 + k >= 4:
                        tn.wait_ge(sem("chs"), scS + (fq * 8 + k) - 3)
                    for dc in range(16):
                        ins = tn.matmul(pbank[k % 4][:],
                                        wstr5[:, fq % 2, dc, fc * 128:(fc + 1) * 128],
                                        h2fm[:, dc, th * 512:(th + 1) * 512],
                                        start=(dc == 0), stop=(dc == 15))
                    inc("mm_ff", ins)

        @blk.scalar
        def _(sc):
            sc = P.chain(sc, "chs")
            for fq in range(4):
                for k in range(8):
                    fc, th = divmod(k, 2)
                    sc.wait_ge(sem("mm_ff"), 8 * fq + k + 1)
                    f_g = fq * 512 + fc * 128
                    if fq < 2:
                        sc.copy(x1g[:, f_g // 128, th * 512:(th + 1) * 512],
                                pbank[k % 4][:])
                    else:
                        idx = (fq - 2) * 8 + k
                        if idx >= 1:
                            sc.wait_ge(sem("chv"), veS + idx)
                        sc.activation(silb[:], pbank[k % 4][:], AF.Silu)

        @blk.vector
        def _(v):
            v = P.chain(v, "chv")
            for fq in range(2, 4):
                for k in range(8):
                    fc, th = divmod(k, 2)
                    idx = (fq - 2) * 8 + k
                    v.wait_ge(sem("chs"), scS + 16 + idx + 1)
                    fg = fq * 512 + fc * 128 - 1024
                    v.tensor_mul(x1g[:, 8 + fg // 128, th * 512:(th + 1) * 512],
                                 x1g[:, fg // 128, th * 512:(th + 1) * 512], silb[:])
    m_w1s = tot("chv")
    scW1 = tot("chs")

    with nc.Block() as blk:
        scS2 = tot("chs")

        @blk.gpsimd
        def _(g):
            for dq in range(4):
                if dq < 2:
                    wait(g, "mm_ff", 24 + 8 * dq)
                else:
                    wait(g, "mm_f2", 8 * (dq - 1))
                inc(f"dma_w2{dq % 2}", g.dma_start(
                    wstr5[:, dq % 2, 0:8, :],
                    w2s_d.ap().rearrange("(fc p) d -> p fc d", p=128)
                    [:, :, dq * 512:(dq + 1) * 512]), 16)

        @blk.tensor
        def _(tn):
            tn.wait_ge(sem("chv"), m_w1s)
            for dq in range(4):
                tn.wait_ge(sem(f"dma_w2{dq % 2}"), 16 * (dq // 2 + 1))
                for tc in range(8):
                    g_ = dq * 8 + tc
                    if g_ >= 4:
                        tn.wait_ge(sem("chs"), scS2 + g_ - 3)
                    for fc in range(8):
                        ins = tn.matmul(pbank[g_ % 4][:],
                                        x1g[:, 8 + fc, tc * 128:(tc + 1) * 128],
                                        wstr5[:, dq % 2, fc, :],
                                        start=(fc == 0), stop=(fc == 7))
                    inc("mm_f2", ins)

        @blk.scalar
        def _(sc):
            sc = P.chain(sc, "chs")
            for dq in range(4):
                for tc in range(8):
                    sc.wait_ge(sem("mm_f2"), 8 * dq + tc + 1)
                    sc.copy(shared_tm[:, tc, dq * 512:(dq + 1) * 512],
                            pbank[(dq * 8 + tc) % 4][:])
    m_sh = tot("chs")

    with contextlib.ExitStack() as pst:
        elrep = pst.enter_context(nc.sbuf_tensor("elrep", [128, 80], I16))
        elist = pst.enter_context(nc.sbuf_tensor("elist", [16, 80], I16))
        elf = pst.enter_context(nc.sbuf_tensor("elf", [16, 80], F32))
        elacc = pst.enter_context(nc.sbuf_tensor("elacc", [16, 80], F32))

        with nc.Block() as blk:
            @blk.vector
            def _(v):
                v = P.chain(v, "chv")
                v.wait_ge(sem("gp_sp"), E)
                for e in range(E):
                    v.tensor_copy(elf[:], idx_e[:, e * 80:(e + 1) * 80])
                    if e == 0:
                        v.tensor_scalar(elacc[:], elf[:], eoh[:, 0:1], None, op0=ALU.mult)
                    else:
                        v.tensor_scalar(elf[:], elf[:], eoh[:, e:e + 1], None, op0=ALU.mult)
                        v.tensor_add(elacc[:], elacc[:], elf[:])
                v.tensor_scalar_max(elacc[:], elacc[:], 0.0)
                v.tensor_scalar_min(elacc[:], elacc[:], 8191.0)
                v.tensor_copy(elist[:], elacc[:])
            m_el = tot("chv")

            @blk.gpsimd
            def _(g):
                g.load_library(LIB_MLP)
                g.wait_ge(sem("chv"), m_el)
                for rep in range(8):
                    inc("dma_el", g.dma_start(elrep[rep * 16:(rep + 1) * 16, :], elist[:]), 16)
                wait(g, "dma_el")
                wait(g, "cc_h2")

        CH = (512, 512, 256)
        for c in range(3):
            nsl = CH[c]
            gexp = (bigD.ap().rearrange("p a t -> p (a t)")[:, 0:16 * nsl]
                    .rearrange("p (dc x) -> p dc x", dc=16))
            with nc.Block() as blk:
                scR = tot("chs")
                veR2 = tot("chv")
                mmR = tot("mm_r1") if "mm_r1" in P.cnt else 0
                mmR2 = tot("mm_r2") if "mm_r2" in P.cnt else 0

                @blk.gpsimd
                def _(g):
                    if c > 0:
                        wait(g, "mm_r1", mmR)
                        wait(g, "mm_r2", mmR2)
                    inc("gd_x", g.dma_gather(
                        gexp[:, :, :], h2all.ap(), elrep[:, 32 * c:32 * c + nsl // 16],
                        nsl, nsl, D, transpose=True), 16)
                    for fq in range(4):
                        if fq >= 2:
                            wait(g, "mm_r1", mmR + 4 * (fq - 1))
                        inc(f"dma_r1{fq % 2}", g.dma_start(
                            wstr5[:, fq % 2, :, :],
                            w1r_d.ap().rearrange("(dc p) f -> p dc f", p=128)
                            [:, :, fq * 512:(fq + 1) * 512]), 16)

                @blk.tensor
                def _(tn):
                    wait(tn, "gd_x")
                    if c == 0:
                        tn.wait_ge(sem("chs"), m_sh)
                    for fq in range(4):
                        tn.wait_ge(sem(f"dma_r1{fq % 2}"), 16 * (2 * c + fq // 2 + 1))
                        for fc in range(4):
                            g_ = fq * 4 + fc
                            if g_ >= 4:
                                tn.wait_ge(sem("chs"), scR + g_ - 3)
                            for dc in range(16):
                                ins = tn.matmul(pbank[g_ % 4][0:128, 0:nsl],
                                                wstr5[:, fq % 2, dc, fc * 128:(fc + 1) * 128],
                                                gexp[:, dc, 0:nsl],
                                                start=(dc == 0), stop=(dc == 15))
                            inc("mm_r1", ins)

                @blk.scalar
                def _(sc):
                    sc = P.chain(sc, "chs")
                    for fq in range(4):
                        for fc in range(4):
                            g_ = fq * 4 + fc
                            sc.wait_ge(sem("mm_r1"), mmR + g_ + 1)
                            f_g = fq * 512 + fc * 128
                            if fq < 2:
                                sc.copy(x1g[:, f_g // 128, 0:nsl], pbank[g_ % 4][0:128, 0:nsl])
                            else:
                                idx = (fq - 2) * 4 + fc
                                if idx >= 1:
                                    sc.wait_ge(sem("chv"), veR2 + idx)
                                elif c > 0:
                                    sc.wait_ge(sem("chv"), veR2)
                                sc.activation(silb[:, 0:nsl], pbank[g_ % 4][0:128, 0:nsl], AF.Silu)

                @blk.vector
                def _(v):
                    v = P.chain(v, "chv")
                    for fq in range(2, 4):
                        for fc in range(4):
                            idx = (fq - 2) * 4 + fc
                            v.wait_ge(sem("chs"), scR + 8 + idx + 1)
                            fg = fq * 512 + fc * 128 - 1024
                            v.tensor_mul(x1g[:, 8 + fg // 128, 0:nsl],
                                         x1g[:, fg // 128, 0:nsl], silb[:, 0:nsl])
                m_gr = tot("chv")

                @blk.gpsimd
                def _(g):
                    for dq in range(4):
                        if dq < 2:
                            wait(g, "mm_r1", mmR + 12 + 4 * dq)
                        else:
                            wait(g, "mm_r2", mmR2 + (nsl // 128) * (dq - 1))
                        inc(f"dma_r2{dq % 2}", g.dma_start(
                            wstr5[:, dq % 2, 0:8, :],
                            w2r_d.ap().rearrange("(fc p) d -> p fc d", p=128)
                            [:, :, dq * 512:(dq + 1) * 512]), 16)

                @blk.tensor
                def _(tn):
                    tn.wait_ge(sem("chv"), m_gr)
                    scR2 = scR + 16
                    for dq in range(4):
                        tn.wait_ge(sem(f"dma_r2{dq % 2}"), 16 * (2 * c + dq // 2 + 1))
                        for sc_ in range(nsl // 128):
                            g_ = dq * (nsl // 128) + sc_
                            if g_ >= 2:
                                tn.wait_ge(sem("chs"), scR2 + g_ - 1)
                            for fc in range(8):
                                ins = tn.matmul(pbank[4 + g_ % 2][:],
                                                x1g[:, 8 + fc, sc_ * 128:(sc_ + 1) * 128],
                                                wstr5[:, dq % 2, fc, :],
                                                start=(fc == 0), stop=(fc == 7))
                            inc("mm_r2", ins)

                @blk.scalar
                def _(sc):
                    sc = P.chain(sc, "chs")
                    for dq in range(4):
                        for sc_ in range(nsl // 128):
                            g_ = dq * (nsl // 128) + sc_
                            if g_ == 0 and c > 0:
                                sc.wait_ge(sem("dma_ro"), 16 * c)
                            sc.wait_ge(sem("mm_r2"), mmR2 + g_ + 1)
                            sc.copy(routsb[:, sc_, dq * 512:(dq + 1) * 512],
                                    pbank[4 + g_ % 2][:])
                m_rout = tot("chs")

                @blk.gpsimd
                def _(g):
                    g.wait_ge(sem("chs"), m_rout)
                    inc("dma_ro", g.dma_start(
                        routin.ap()[c * 512:c * 512 + nsl, :]
                        .rearrange("(sc p) d -> p sc d", p=128),
                        routsb[:, 0:nsl // 128, :]), 16)

        with nc.Block() as blk:
            @blk.gpsimd
            def _(g):
                wait(g, "dma_ro")
                inc("cc_ro", g.collective_compute(
                    "AllGather", ALU.bypass, replica_groups=[list(range(8))],
                    ins=[routin.ap().opt()], outs=[routall.ap().opt()]))

    pst5.close()
    with contextlib.ExitStack() as pst:
        olrep = pst.enter_context(nc.sbuf_tensor("olrep", [128, 128], I16))
        xf = [pst.enter_context(nc.sbuf_tensor(f"xf{i}", [128, D], F32)) for i in range(2)]
        of = [pst.enter_context(nc.sbuf_tensor(f"of{i}", [128, D], F32)) for i in range(2)]
        t1c = pst.enter_context(nc.sbuf_tensor("t1c", [128, D], F32))
        gown = bigD.ap().rearrange("p a t -> p (a t)").rearrange("p (b d) -> p b d", b=8)

        with nc.Block() as blk:
            veC = tot("chv")
            d_ro = tot("dma_ro")

            @blk.gpsimd
            def _(g):
                wait(g, "cc_ro")
                for rep in range(8):
                    inc("dma_or", g.dma_start(olrep[rep * 16:(rep + 1) * 16, :], olist[:]), 16)
                wait(g, "dma_or")
                for j2 in range(2):
                    inc("gd_o", g.dma_gather(
                        gown[:, 4 * j2:4 * j2 + 4, :], routall.ap(),
                        olrep[:, 32 * (2 * j2):32 * (2 * j2) + 32],
                        512, 512, D), 16)
                for tc in range(4):
                    if tc >= 2:
                        g.wait_ge(sem("chv"), veC + 5 * (tc - 1))
                    inc(f"dma_xf{tc % 2}", g.dma_start(
                        xf[tc % 2][:], x2spill.ap()[tc * 128:(tc + 1) * 128, :]), 16)
                g.wait_ge(sem("chv"), veC + 5 * 4)
                for j2 in range(2):
                    inc("gd_o", g.dma_gather(
                        gown[:, 4 * j2:4 * j2 + 4, :], routall.ap(),
                        olrep[:, 32 * (2 * j2 + 1):32 * (2 * j2 + 1) + 32],
                        512, 512, D), 16)
                for tc in range(4, 8):
                    g.wait_ge(sem("chv"), veC + 5 * (tc - 1))
                    inc(f"dma_xf{tc % 2}", g.dma_start(
                        xf[tc % 2][:], x2spill.ap()[tc * 128:(tc + 1) * 128, :]), 16)

            @blk.vector
            def _(v):
                v = P.chain(v, "chv")
                for tc in range(8):
                    half, off = divmod(tc, 4)
                    v.wait_ge(sem("gd_o"), 32 * (half + 1))
                    v.wait_ge(sem(f"dma_xf{tc % 2}"), 16 * (tc // 2 + 1))
                    if tc >= 2:
                        v.wait_ge(sem(f"dma_out{tc % 2}"), 16 * (tc // 2))
                    g1 = gown[:, off, :]
                    g2 = gown[:, 4 + off, :]
                    v.tensor_scalar(t1c[:], g1, wownb[:, tc:tc + 1], None, op0=ALU.mult)
                    v.scalar_tensor_tensor(t1c[:], g2, wownb[:, 8 + tc:9 + tc], t1c[:],
                                           op0=ALU.mult, op1=ALU.add)
                    v.tensor_add(t1c[:], t1c[:], shared_tm[:, tc, :])
                    v.tensor_mul(t1c[:], t1c[:], gag[:, D:2 * D])
                    v.tensor_add(of[tc % 2][:], xf[tc % 2][:], t1c[:])
                    assert tot("chv") == veC + 5 * (tc + 1)

            @blk.sync
            def _(s2):
                for tc in range(8):
                    s2.wait_ge(sem("chv"), veC + 5 * (tc + 1))
                    inc(f"dma_out{tc % 2}", s2.dma_start(out_d.ap()[tc * 128:(tc + 1) * 128, :],
                                                         of[tc % 2][:]), 16)
                wait(s2, "dma_out0")
                wait(s2, "dma_out1")

    st.close()
    return nc, dbg


def rope_tables(qi):
    pos = np.arange(qi * 1024, (qi + 1) * 1024, dtype=np.float64)
    invf = (10000.0 ** (-np.arange(0, 64, dtype=np.float64) / 64.0))
    ang = pos[None, :] * np.concatenate([invf, invf])[:, None]
    return np.cos(ang), np.sin(ang)


def make_inputs(ins, r):
    b, qi = r // 4, r % 4
    bf = lambda a: np.ascontiguousarray(a).astype(ml_dtypes.bfloat16)
    f32 = lambda a: np.ascontiguousarray(a, dtype=np.float32)
    x_full = np.asarray(ins["L0_attn_x"], np.float32).reshape(8192, 2048)
    cos, sin = rope_tables(qi)
    sc = 1.0 / np.sqrt(128.0)
    R = np.zeros((128, 128), np.float32)
    for i in range(64):
        R[i, i + 64] = -1.0
        R[i + 64, i] = 1.0
    gmap = np.zeros((16, 512), np.float32)
    for a in range(16):
        for col in range(512):
            gmap[a, col] = 1024 * (col // 64) + 64 * a + (col % 64) + 1.0
    m = {
        "x": f32(x_full[r * 1024:(r + 1) * 1024]),
        "wqkvT": f32(np.asarray(ins["L0_attn_w_qkv"]).T),
        "woutT": f32(np.asarray(ins["L0_attn_w_out"]).T),
        "gatew": f32(np.asarray(ins["L0_ffn_gate_w"])),
        "ebias": f32(np.asarray(ins["L0_ffn_expert_bias"]).reshape(E, 1)),
        "w1s": f32(np.asarray(ins["L0_ffn_shared_w1"])[0]),
        "w2s": f32(np.asarray(ins["L0_ffn_shared_w2"])[0]),
        "w1r": f32(np.asarray(ins["L0_ffn_routed_w1"])[min(r, E - 1)]),
        "w2r": f32(np.asarray(ins["L0_ffn_routed_w2"])[min(r, E - 1)]),
        "cc": f32(np.asarray(ins["L0_c"])),
        "wmod": f32(np.asarray(ins["L0_wmod"])[:, r * 1536:(r + 1) * 1536]),
        "identb": bf(np.eye(128)),
        "identf": f32(np.eye(128)),
        "rotT": bf(R.T),
        "cosq": bf(cos * sc), "sinq": bf(sin * sc),
        "cosk": bf(cos), "sink": bf(sin),
        "u16s": bf(np.triu(np.ones((16, 16)), 1)),
        "gmapp1": gmap,
        "irev": f32(np.tile(np.arange(E - 1, -1, -1, dtype=np.float32)[None, :] + 1.0, (128, 1))),
        "bsel": f32(np.tile([[1.0, 0.0]] if b == 0 else [[0.0, 1.0]], (6, 1))),
        "rowm": f32(np.array([0, 0, 1, 0, 0, 1], np.float32).reshape(6, 1)),
        "eoh": f32(np.tile(np.eye(E)[min(r, E - 1)][None, :], (16, 1))),
        "onesr_b": bf(np.ones((1, 128))),
        "onesr_f": f32(np.ones((1, 128))),
        "onescol_b": bf(np.ones((128, 1))),
        "ones2_b": bf(np.ones((128, 2))),
        "bisinit": np.tile(np.array([[0, 0x40000000]], np.int32), (16, 1)),
        "ones16": f32(np.ones((16, 16))),
    }
    return m


def kernel(**inputs):
    nc, _ = build(5)
    nc.compile()
    in_maps = [make_inputs(inputs, r) for r in range(8)]
    res = run_bass_kernel_spmd(nc, in_maps, core_ids=list(range(8)))
    out = np.concatenate([np.asarray(res.results[r]["out"], np.float32)
                          for r in range(8)], axis=0)
    return out.reshape(2, 4096, 2048)
